# version 03
# baseline (speedup 1.0000x reference)
import math
import numpy as np
import concourse.bass as bass
import concourse.mybir as mybir
from concourse.bass_utils import run_bass_kernel_spmd

F32 = mybir.dt.float32
BF16 = mybir.dt.bfloat16
AF = mybir.ActivationFunctionType
ALU = mybir.AluOpType
AX = mybir.AxisListType

NCORES = 8
SEQ = 2048
DM = 1024
NT = SEQ // 128
ND = DM // 128
NHEAD = 16
DH = 64
LN_EPS = 1e-5
MAGIC = 12582912.0

ENG_SEM = {"tensor": "pe", "vector": "dve", "scalar": "act", "gpsimd": "gp"}
SEM_NAMES = ("wdma", "gdma", "adma", "pe", "dve", "act", "gp")


def ts_(i, n):
    return slice(i * n, (i + 1) * n)


def ots(base, i, n):
    return slice(base + i * n, base + (i + 1) * n)


class NS(dict):
    def __getattr__(self, k):
        return self[k]

    def __setattr__(self, k, v):
        self[k] = v


class _DummyIns:
    def then_inc(self, *a, **k):
        return self


class _DummyEng:
    def __getattr__(self, name):
        def f(*a, **k):
            return _DummyIns()
        return f


class Sched:

    def __init__(self, nc, active, eng, sems):
        self.nc = nc
        self.active = active
        self._eng = eng
        self.sems = sems
        self.cnt = {k: 0 for k in sems}
        d = _DummyEng()
        self.sync = eng if active == "sync" else d
        self.tensor = eng if active == "tensor" else d
        self.vector = eng if active == "vector" else d
        self.scalar = eng if active == "scalar" else d
        self.gpsimd = eng if active == "gpsimd" else d

    def on(self, engname):
        return self.active == engname

    def eng(self, engname):
        return getattr(self, engname)

    def tinc(self, ins, sem, k=1):
        ins.then_inc(self.sems[sem], k)
        self.cnt[sem] += k
        return self.cnt[sem]

    def wait(self, engname, sem, val=None):
        if val is None:
            val = self.cnt[sem]
        if val <= 0:
            return
        if self.active == engname:
            self._eng.wait_ge(self.sems[sem], val)

    def sf(self, engname, ins):
        sem = ENG_SEM[engname]
        m = self.tinc(ins, sem, 1)
        self.wait(engname, sem, m)
        return m



class WStream:

    def __init__(self, B, tag, cast_eng="vector", single_stage=False):
        self.B = B
        self.tag = tag
        self.ce = cast_eng
        self.i = 0
        self.single_stage = single_stage
        self.stage_free = [0, 0]
        self.stage_guard = None
        self.wbuf_free = [0, 0]
        self.wbuf_extra = [None, None]
        self.ready = [0, 0]
        self.last_dma = 0

    def load(self, S, dram_ap, cols, cast_fn=None, stage_hook=None):
        B = self.B
        sl = self.i % 2
        self.i += 1
        S.wait(self.ce, "pe", self.wbuf_free[sl])
        if self.wbuf_extra[sl] is not None:
            S.wait(self.ce, *self.wbuf_extra[sl])
            self.wbuf_extra[sl] = None
        stage = B[f"{self.tag}_stage0"]
        scols = stage.shape[1] if hasattr(stage, "shape") else 2048
        pieces = []
        off = 0
        while off < cols:
            w = min(scols, cols - off)
            pieces.append((off, w))
            off += w
        mr = 0
        for off, w in pieces:
            S.wait("sync", ENG_SEM[self.ce], self.stage_free[0])
            if self.stage_guard is not None:
                S.wait("sync", *self.stage_guard)
                self.stage_guard = None
            ins = S.sync.dma_start(stage[:, 0:w], dram_ap[:, off:off + w])
            mdma = S.tinc(ins, "wdma", 16)
            self.last_dma = mdma
            S.wait(self.ce, "wdma", mdma)
            if stage_hook is not None:
                mh = stage_hook(S, stage, off, w, mdma)
                self.stage_guard = ("pe", mh)
            if cast_fn is not None:
                ins = cast_fn(S, B[f"{self.tag}_wbuf{sl}"], stage, off, w)
            else:
                ins = S.eng(self.ce).tensor_copy(
                    B[f"{self.tag}_wbuf{sl}"][:, off:off + w], stage[:, 0:w])
            mr = S.tinc(ins, ENG_SEM[self.ce], 1)
            self.stage_free[0] = mr
        self.ready[sl] = mr
        return sl, mr

    def load_bf16(self, S, dram_ap, cols):
        sl = self.i % 2
        self.i += 1
        S.wait("sync", "pe", self.wbuf_free[sl])
        if self.wbuf_extra[sl] is not None:
            S.wait("sync", *self.wbuf_extra[sl])
            self.wbuf_extra[sl] = None
        ins = S.sync.dma_start(self.B[f"{self.tag}_wbuf{sl}"][:, 0:cols], dram_ap)
        m = S.tinc(ins, "wdma", 16)
        self.last_dma = m
        return sl, m

    def wbuf(self, sl):
        return self.B[f"{self.tag}_wbuf{sl}"]

    def release(self, S, sl, pe_milestone):
        self.wbuf_free[sl] = pe_milestone



def emit_load_x(S, B, x_hbm):
    for t in range(NT):
        ins = S.sync.dma_start(B.X[:, ts_(t, DM)], x_hbm[ts_(t, 128), :])
        S.tinc(ins, "wdma", 16)
    return ("wdma", S.cnt["wdma"])


def emit_store_x(S, B, out_hbm, x_ready):
    sem, m = x_ready
    S.wait("gpsimd", sem, m)
    base = S.cnt["gdma"]
    for t in range(NT):
        ins = S.gpsimd.dma_start(out_hbm[ts_(t, 128), :], B.X[:, ts_(t, DM)])
        S.tinc(ins, "gdma", 16)
    S.wait("gpsimd", "gdma", base + 16 * NT)


class PsumBank:

    def __init__(self, handles):
        self.h = handles
        self.free = [("dve", 0)] * len(handles)
        self.i = 0

    def next(self, S, waiter="tensor"):
        k = self.i % len(self.h)
        self.i += 1
        sem, m = self.free[k]
        S.wait(waiter, sem, m)
        return k, self.h[k]

    def done(self, k, sem, m):
        self.free[k] = (sem, m)


def emit_ln(S, B, x_ready, full_y=False, yf=None, src=None, make_yt=True,
            fp32_add_to_x=False):
    if src is None:
        src = B.X
    sem, m0 = x_ready
    S.wait("vector", sem, m0)
    S.wait("scalar", sem, m0)
    last_v = last_s = None
    for t in range(NT):
        last_v = S.vector.reduce_sum(B.S1[:, t:t + 1], src[:, ts_(t, DM)], axis=AX.X)
        last_s = S.scalar.activation(B.LNDISC[:, 0:DM], src[:, ts_(t, DM)], AF.Square,
                                     accum_out=B.S2[:, t:t + 1])
    S.sf("vector", last_v)
    ms = S.tinc(last_s, "act", 1)
    S.wait("vector", "act", ms)
    i1 = S.vector.tensor_scalar(B.MU[:, 0:NT], B.S1[:, 0:NT], 1.0 / DM, None, op0=ALU.mult)
    i2 = S.vector.tensor_scalar(B.VAR[:, 0:NT], B.S2[:, 0:NT], 1.0 / DM, None, op0=ALU.mult)
    S.sf("vector", i2)
    i3 = S.vector.tensor_tensor(B.MUSQ[:, 0:NT], B.MU[:, 0:NT], B.MU[:, 0:NT], op=ALU.mult)
    S.sf("vector", i3)
    i4 = S.vector.scalar_tensor_tensor(B.VAR[:, 0:NT], B.VAR[:, 0:NT], LN_EPS,
                                       B.MUSQ[:, 0:NT], op0=ALU.add, op1=ALU.subtract)
    m4 = S.tinc(i4, "dve", 1)
    S.wait("scalar", "dve", m4)
    i5 = S.scalar.activation(B.SD[:, 0:NT], B.VAR[:, 0:NT], AF.Sqrt)
    m5 = S.tinc(i5, "act", 1)
    S.wait("vector", "act", m5)
    i6 = S.vector.reciprocal(B.RSTD[:, 0:NT], B.SD[:, 0:NT])
    S.sf("vector", i6)

    yroll_pe = [0, 0]
    for t in range(NT):
        if full_y:
            ysrc = yf[:, ts_(t, DM)]
        else:
            S.wait("vector", "pe", yroll_pe[t % 2])
            ysrc = B.YROLL[:, ts_(t % 2, DM)]
        if fp32_add_to_x:
            iz = S.vector.tensor_scalar(B.LNDISC[:, 0:DM], src[:, ts_(t, DM)],
                                        B.MU[:, t:t + 1], B.RSTD[:, t:t + 1],
                                        op0=ALU.subtract, op1=ALU.mult)
            S.sf("vector", iz)
            ic = S.vector.tensor_copy(ysrc, B.LNDISC[:, 0:DM])
            S.tinc(ic, "dve", 1)
            ia = S.vector.tensor_tensor(B.X[:, ts_(t, DM)], B.X[:, ts_(t, DM)],
                                        B.LNDISC[:, 0:DM], op=ALU.add)
            S.sf("vector", ia)
            my = S.cnt["dve"]
            continue
        iy = S.vector.tensor_scalar(ysrc, src[:, ts_(t, DM)],
                                    B.MU[:, t:t + 1], B.RSTD[:, t:t + 1],
                                    op0=ALU.subtract, op1=ALU.mult)
        my = S.tinc(iy, "dve", 1)
        if make_yt:
            S.wait("tensor", "dve", my)
            mlast = emit_transpose_tile_to_feat(S, B, ysrc, B.YT, t)
            if not full_y:
                yroll_pe[t % 2] = mlast
    return NS(yt=("dve", S.cnt["dve"]))


def emit_transpose_tile_to_feat(S, B, src_tile_ap, yt_handle, t):
    yt3 = yt_handle[:, :].rearrange("p (d q) -> p d q", q=SEQ)
    mlast = 0
    for pack in range(2):
        kb, ps = B.psb_pool.next(S, waiter="tensor")
        lastmm = None
        for j in range(4):
            d = pack * 4 + j
            lastmm = S.tensor.transpose(
                ps[:, ts_(j, 128)],
                src_tile_ap[:, ts_(d, 128)],
                B.IDENT[:, 0:128])
        mp = S.tinc(lastmm, "pe", 1)
        mlast = mp
        S.wait("vector", "pe", mp)
        src3 = ps[:, 0:512].rearrange("p (b f) -> p b f", f=128)
        dst3 = yt3[:, pack * 4:(pack + 1) * 4, t * 128:(t + 1) * 128]
        ev = S.vector.tensor_copy(dst3, src3)
        me = S.tinc(ev, "dve", 1)
        B.psb_pool.done(kb, "dve", me)
    return mlast


def emit_out_transpose_add(S, B, outt_handle, c, scale_note=None):
    o3 = outt_handle[:, 0:8 * 512].rearrange("p (d q) -> p d q", q=512)
    mlast = 0
    for tsub in range(4):
        t = 4 * c + tsub
        for pack in range(2):
            kb, ps = B.psb_pool.next(S, waiter="tensor")
            lastmm = None
            for j in range(4):
                d = pack * 4 + j
                lastmm = S.tensor.transpose(
                    ps[:, ts_(j, 128)],
                    o3[:, d, ts_(tsub, 128)],
                    B.IDENT[:, 0:128])
            mp = S.tinc(lastmm, "pe", 1)
            S.wait("vector", "pe", mp)
            dst = B.X[:, t * DM + pack * 512: t * DM + (pack + 1) * 512]
            ev = S.vector.tensor_tensor(dst, dst, ps[:, 0:512], op=ALU.add)
            me = S.tinc(ev, "dve", 1)
            B.psb_pool.done(kb, "dve", me)
            mlast = me
    return ("dve", mlast)



def emit_l1ffn(S, B, W, x_ready):
    r = emit_ln(S, B, x_ready)
    yt_sem, yt_m = r.yt
    ws = WStream(B, "w", cast_eng="vector", single_stage=True)
    S.wait("tensor", yt_sem, yt_m)

    x_m = 0
    relu_free = [0, 0]
    y0_free = [0, 0]
    outt_pe = 0
    for c in range(4):
        for j in range(8):
            sl, mr = ws.load(S, W.wfc_s[j, :, :], 8 * 512)
            S.wait("tensor", ENG_SEM[ws.ce], mr)
            wb = ws.wbuf(sl)
            lastmm_for_slot = None
            for sub in range(4):
                fblk = 4 * j + sub
                kb, ps = B.psf_pool.next(S, waiter="tensor")
                lastmm = None
                for kt in range(ND):
                    lhsT = wb[:, kt * 512 + sub * 128: kt * 512 + (sub + 1) * 128]
                    rhs = B.YT[:, kt * SEQ + c * 512: kt * SEQ + (c + 1) * 512]
                    lastmm = S.tensor.matmul(ps[:, 0:512], lhsT, rhs,
                                             start=(kt == 0), stop=(kt == ND - 1))
                mp = S.tinc(lastmm, "pe", 1)
                lastmm_for_slot = mp
                S.wait("scalar", "pe", mp)
                S.wait("scalar", "dve", relu_free[sub % 2])
                ir = S.scalar.activation(B.RELU[:, ts_(sub % 2, 512)], ps[:, 0:512], AF.Relu)
                ma = S.tinc(ir, "act", 1)
                B.psf_pool.done(kb, "act", ma)
                S.wait("vector", "act", ma)
                iq = S.vector.tensor_mul(B.H[:, ts_(fblk, 512)],
                                         B.RELU[:, ts_(sub % 2, 512)],
                                         B.RELU[:, ts_(sub % 2, 512)])
                relu_free[sub % 2] = S.tinc(iq, "dve", 1)
            ws.release(S, sl, lastmm_for_slot)
        h_done = S.cnt["dve"]

        S.wait("tensor", "dve", h_done)
        for j in range(8):
            sl, mr = ws.load(S, W.wpr_s[j, :, :], 32 * 128)
            S.wait("tensor", ENG_SEM[ws.ce], mr)
            wb = ws.wbuf(sl)
            kb, ps = B.psf_pool.next(S, waiter="tensor")
            lastmm = None
            for kt in range(32):
                lhsT = wb[:, ts_(kt, 128)]
                rhs = B.H[:, ts_(kt, 512)]
                lastmm = S.tensor.matmul(ps[:, 0:512], lhsT, rhs,
                                         start=(kt == 0), stop=(kt == 31))
            mp = S.tinc(lastmm, "pe", 1)
            ws.release(S, sl, mp)
            S.wait("scalar", "pe", mp)
            sslot = 0
            S.wait("scalar", "dve", y0_free[sslot])
            if j == 0:
                S.wait("vector", "pe", outt_pe)
            ie = S.scalar.activation(B.Y0[:, ts_(sslot, 512)], ps[:, 0:512], AF.Copy)
            ma = S.tinc(ie, "act", 1)
            B.psf_pool.done(kb, "act", ma)
            S.wait("vector", "act", ma)
            y0 = B.Y0[:, ts_(sslot, 512)]
            mem = B.MEM[:, ts_(sslot, 512)]
            spk = B.SPK[:, ts_(sslot, 512)]
            acc = B.OUTT[:, ts_(j, 512)]
            i = S.vector.tensor_scalar(acc, y0, 1.0, None, op0=ALU.is_gt)
            S.sf("vector", i)
            prev_spk_ap = acc
            for st in range(2, 6):
                if st == 2:
                    i = S.vector.tensor_scalar(mem, y0, 1.9, None, op0=ALU.mult)
                else:
                    i = S.vector.scalar_tensor_tensor(mem, mem, 0.9, y0,
                                                      op0=ALU.mult, op1=ALU.add)
                S.sf("vector", i)
                i = S.vector.tensor_sub(mem, mem, prev_spk_ap)
                S.sf("vector", i)
                i = S.vector.tensor_scalar(spk, mem, 1.0, None, op0=ALU.is_gt)
                S.sf("vector", i)
                i = S.vector.tensor_add(acc, acc, spk)
                S.sf("vector", i)
                prev_spk_ap = spk
            i = S.vector.tensor_scalar(acc, acc, 0.1, None, op0=ALU.mult)
            S.sf("vector", i)
            y0_free[sslot] = S.cnt["dve"]
        S.wait("tensor", "dve", S.cnt["dve"])
        _, x_m = emit_out_transpose_add(S, B, B.OUTT, c)
    return ("dve", x_m)



def emit_attn(S, B, W, x_ready, cfg):
    full_y = bool(cfg.get("inner_residual"))
    if full_y:
        r1 = emit_ln(S, B, x_ready, full_y=True, yf=B.YF, make_yt=False,
                     fp32_add_to_x=True)
        r = emit_ln(S, B, r1.yt, src=B.YF)
    else:
        r = emit_ln(S, B, x_ready)
    yt_sem, yt_m = r.yt

    ws = WStream(B, "aw", cast_eng="vector", single_stage=True)
    wso = ws
    S.wait("tensor", yt_sem, yt_m)

    v4 = B.VAUG[:, 0:NT * 130].rearrange("p (t hh c) -> p t hh c", hh=2, c=65)
    im = S.vector.memset(v4[:, :, :, 64:65], 1.0)
    S.tinc(im, "dve", 1)

    sc_free = [("act", 0), ("act", 0)]
    av_free = ("dve", 0)
    u_free = ("dve", 0)
    pt_free = [0, 0]
    bb_free = [("act", 0), ("act", 0)]

    for p in range(8):
        if cfg.get("dev_qkv_bias"):
            ufree_now = u_free

            def bias_hook(S2, stage, off, w, mdma):
                S2.wait("tensor", "wdma", mdma)
                S2.wait("tensor", *ufree_now)
                lastm = 0
                for kk in range(w // 128):
                    kg = (off // 128) + kk
                    which = kg // 8
                    kt = kg % 8
                    ins = S2.tensor.matmul(
                        B.PSF3[:, which:which + 1], stage[:, ts_(kk, 128)],
                        B.LNBC[:, kt:kt + 1], start=(kg == 0), stop=(kg == 23))
                    lastm = S2.tinc(ins, "pe", 1)
                return lastm

            sl, mr = ws.load(S, W.qkv_s[p, :, :], 3072,
                             cast_fn=cfg.get("cast_fn"), stage_hook=bias_hook)
        else:
            sl, mr = ws.load(S, W.qkv_s[p, :, :], 3072,
                             cast_fn=cfg.get("cast_fn"))
        S.wait("tensor", ENG_SEM[ws.ce], mr)
        wb = ws.wbuf(sl)
        last_w_use = 0

        if cfg.get("dev_qkv_bias"):
            mp_b = ws.stage_guard[1]
            S.wait("vector", "pe", mp_b)
            for which, dstcol in ((0, B.BQE), (1, B.BKE), (2, B.VBC)):
                iv = S.vector.tensor_tensor(
                    dstcol[:, 0:1], B.PSF3[:, which:which + 1],
                    W.bqkv_col[:, 8 * which + p:8 * which + p + 1], op=ALU.add)
                mv = S.sf("vector", iv)
            u_free = ("dve", mv)
            kb, psb = B.psb_pool.next(S, waiter="tensor")
            S.wait("tensor", "dve", u_free[1])
            itr = S.tensor.transpose(psb[0:1, 0:128], B.VBC[:, 0:1], B.IDENT[:, 0:128])
            mp = S.tinc(itr, "pe", 1)
            S.wait("vector", "pe", mp)
            ib = S.vector.tensor_copy(B.BVRD[0:1, 0:128], psb[0:1, 0:128])
            mv = S.sf("vector", ib)
            B.psb_pool.done(kb, "dve", mv)

        if cfg.get("dev_qkv_bias"):
            qk_biases = (B.BQE[:, 0:1], B.BKE[:, 0:1])
        elif cfg.get("qk_bias"):
            qk_biases = (W.bq_col[:, p:p + 1], W.bk_col[:, p:p + 1])
        else:
            qk_biases = (None, None)
        for which, dst, bias_ap in (
            (0, B.QTP, qk_biases[0]),
            (1, B.KTP, qk_biases[1]),
        ):
            scale = 0.125 if which == 0 else None
            for c in range(4):
                k, ps = (c % 2), B[f"PSF{c % 2}"]
                S.wait("tensor", *sc_free[k])
                lastmm = None
                for kt in range(ND):
                    lhsT = wb[:, ots(which * 1024, kt, 128)]
                    rhs = B.YT[:, ots(kt * SEQ, c, 512)]
                    lastmm = S.tensor.matmul(ps[:, 0:512], lhsT, rhs,
                                             start=(kt == 0), stop=(kt == ND - 1))
                mp = S.tinc(lastmm, "pe", 1)
                last_w_use = mp
                S.wait("vector", "pe", mp)
                if bias_ap is not None and scale is not None:
                    ie = S.vector.tensor_scalar(dst[:, ts_(c, 512)], ps[:, 0:512],
                                                bias_ap, 0.125,
                                                op0=ALU.add, op1=ALU.mult)
                elif bias_ap is not None:
                    ie = S.vector.tensor_scalar(dst[:, ts_(c, 512)], ps[:, 0:512],
                                                bias_ap, None, op0=ALU.add)
                elif scale is not None:
                    ie = S.vector.tensor_scalar(dst[:, ts_(c, 512)], ps[:, 0:512],
                                                0.125, None, op0=ALU.mult)
                else:
                    ie = S.vector.tensor_copy(dst[:, ts_(c, 512)], ps[:, 0:512])
                me = S.tinc(ie, "dve", 1)
                sc_free[k] = ("dve", me)
        m_qkt = S.cnt["dve"]

        if cfg.get("rope"):
            lastv = None
            for dst in (B.QTP, B.KTP):
                i1 = S.vector.stream_shuffle(B.RSH[:, 0:SEQ], dst[:, 0:SEQ],
                                             cfg["rope_mask"])
                S.sf("vector", i1)
                i2 = S.vector.tensor_mul(B.RSH[:, 0:SEQ], B.RSH[:, 0:SEQ], B.RSIN[:, 0:SEQ])
                S.sf("vector", i2)
                i3 = S.vector.tensor_mul(dst[:, 0:SEQ], dst[:, 0:SEQ], B.RCOS[:, 0:SEQ])
                S.sf("vector", i3)
                lastv = S.vector.tensor_add(dst[:, 0:SEQ], dst[:, 0:SEQ], B.RSH[:, 0:SEQ])
                S.sf("vector", lastv)
            m_qkt = S.cnt["dve"]

        vbr = cfg.get("v_bias_row")
        for t in range(NT):
            S.wait("tensor", *u_free)
            lastmm = None
            for kt in range(ND):
                lhsT = B.YT[:, ots(kt * SEQ, t, 128)]
                rhs = wb[:, ots(2 * 1024, kt, 128)]
                lastmm = S.tensor.matmul(B.PSF3[:, 0:128], lhsT, rhs,
                                         start=(kt == 0),
                                         stop=(kt == ND - 1 and vbr is None))
            if vbr == "BVRD":
                lastmm = S.tensor.matmul(B.PSF3[:, 0:128], B.ONES[0:1, 0:128],
                                         B.BVRD[0:1, 0:128], start=False, stop=True)
            elif vbr == "BVRB16":
                lastmm = S.tensor.matmul(B.PSF3[:, 0:128], B.ONES[0:1, 0:128],
                                         B.BVRB[0:1, ts_(p, 128)], start=False, stop=True)
            elif vbr is not None:
                rowap = W[vbr][0:1, ts_(p, 128)]
                lastmm = S.tensor.matmul(B.PSF3[:, 0:128], B.ONESF[0:1, 0:128],
                                         rowap, start=False, stop=True)
            mp = S.tinc(lastmm, "pe", 1)
            last_w_use = mp
            S.wait("vector", "pe", mp)
            e1 = S.vector.tensor_copy(v4[:, t, 0, 0:64], B.PSF3[:, 0:64])
            e2 = S.vector.tensor_copy(v4[:, t, 1, 0:64], B.PSF3[:, 64:128])
            me = S.tinc(e2, "dve", 1)
            u_free = ("dve", me)
        ws.release(S, sl, last_w_use)
        m_v = S.cnt["dve"]

        sq_free_pe = 0
        for which, src in ((0, B.QTP), (1, B.KTP)):
            S.wait("scalar", "dve", m_qkt)
            S.wait("scalar", "pe", sq_free_pe)
            iq = S.scalar.activation(B.QSQ[:, 0:SEQ], src[:, 0:SEQ], AF.Square)
            msrc = S.tinc(iq, "act", 1)
            for c4 in range(4):
                S.wait("tensor", "act", msrc)
                S.wait("tensor", *u_free)
                mm = S.tensor.matmul(B.PSF3[0:1, 0:512], B.ONES[:, 0:1],
                                     B.QSQ[:, ts_(c4, 512)], start=True, stop=True)
                mp = S.tinc(mm, "pe", 1)
                sq_free_pe = mp
                S.wait("vector", "pe", mp)
                ir = S.vector.reduce_max(B.B4[0:1, which * 4 + c4:which * 4 + c4 + 1],
                                         B.PSF3[0:1, 0:512], axis=AX.X)
                mu = S.tinc(ir, "dve", 1)
                u_free = ("dve", mu)
        S.wait("vector", "dve", S.cnt["dve"])
        ilast = S.vector.reduce_max(B.B4[0:1, 8:9], B.B4[0:1, 0:4], axis=AX.X)
        S.sf("vector", ilast)
        ilast = S.vector.reduce_max(B.B4[0:1, 9:10], B.B4[0:1, 4:8], axis=AX.X)
        S.sf("vector", ilast)
        i = S.vector.tensor_tensor(B.B4[0:1, 10:11], B.B4[0:1, 8:9],
                                   B.B4[0:1, 9:10], op=ALU.mult)
        m = S.tinc(i, "dve", 1)
        S.wait("scalar", "dve", m)
        i = S.scalar.activation(B.B4[0:1, 11:12], B.B4[0:1, 10:11], AF.Sqrt)
        m = S.tinc(i, "act", 1)
        S.wait("vector", "act", m)
        i = S.vector.tensor_scalar(B.B4[0:1, 12:13], B.B4[0:1, 11:12],
                                   -1.04, -0.5, op0=ALU.mult, op1=ALU.add)
        mneg = S.tinc(i, "dve", 1)
        S.wait("tensor", "dve", mneg)
        S.wait("tensor", *u_free)
        mm = S.tensor.matmul(B.PSF3[:, 0:1], B.ONESF[0:1, :], B.B4[0:1, 12:13],
                             start=True, stop=True)
        mp = S.tinc(mm, "pe", 1)
        S.wait("vector", "pe", mp)
        S.wait("vector", *bb_free[p % 2])
        ib = S.vector.tensor_copy(B.BB[:, p % 2:p % 2 + 1], B.PSF3[:, 0:1])
        mbb = S.tinc(ib, "dve", 1)
        u_free = ("dve", mbb)
        bias_b = B.BB[:, p % 2:p % 2 + 1]

        S.wait("tensor", "dve", max(m_qkt, mbb, m_v))
        ma_last = 0
        for hh in range(2):
            base = hh * 64
            for c in range(4):
                tmax = (4 * c + 4) if cfg.get("causal") else NT
                S.wait("tensor", *av_free)
                for ti in range(tmax):
                    t = ti
                    k = ti % 2
                    ps = B[f"PSF{k}"]
                    S.wait("tensor", *sc_free[k])
                    mm = S.tensor.matmul(
                        ps[:, 0:512],
                        B.KTP[base:base + 64, ts_(t, 128)],
                        B.QTP[base:base + 64, ts_(c, 512)],
                        start=True, stop=True)
                    mp = S.tinc(mm, "pe", 1)
                    S.wait("scalar", "pe", max(mp, pt_free[t % 2]))
                    ie = S.scalar.activation(B.PT[:, ts_(t % 2, 512)], ps[:, 0:512],
                                             AF.Exp, bias=bias_b)
                    ma = S.tinc(ie, "act", 1)
                    ma_last = ma
                    sc_free[k] = ("act", ma)
                    dep_sem, dep_m = "act", ma
                    if cfg.get("causal") and t >= 4 * c:
                        rr = t - 4 * c
                        S.wait("vector", "act", ma)
                        imk = S.vector.tensor_mul(B.PT[:, ts_(t % 2, 512)],
                                                  B.PT[:, ts_(t % 2, 512)],
                                                  B.MASKS[:, ts_(rr, 512)])
                        md = S.tinc(imk, "dve", 1)
                        dep_sem, dep_m = "dve", md
                    S.wait("tensor", dep_sem, dep_m)
                    mmv = S.tensor.matmul(
                        B.PSF2[0:65, 0:512],
                        v4[:, t, hh, 0:65],
                        B.PT[:, ts_(t % 2, 512)],
                        start=(ti == 0), stop=(ti == tmax - 1))
                    pt_free[t % 2] = S.tinc(mmv, "pe", 1)
                mz = pt_free[(tmax - 1) % 2]
                S.wait("vector", "pe", mz)
                i1 = S.vector.tensor_scalar(B.IZ[0:1, 0:512], B.PSF2[64:65, 0:512],
                                            1e-30, None, op0=ALU.add)
                S.sf("vector", i1)
                i2 = S.vector.reciprocal(B.IZ[0:1, 0:512], B.IZ[0:1, 0:512])
                miz = S.tinc(i2, "dve", 1)
                S.wait("tensor", "dve", miz)
                S.wait("tensor", *u_free)
                _izname = getattr(getattr(B.IZ, "tensor", B.IZ), "name", "")
                onesf_row = (B.ONESF[64:65, 0:64]
                             if _izname == "ARNF" else B.ONESF[0:1, 0:64])
                mm = S.tensor.matmul(B.PSF3[0:64, 0:512], onesf_row,
                                     B.IZ[0:1, 0:512], start=True, stop=True)
                mp = S.tinc(mm, "pe", 1)
                S.wait("vector", "pe", mp)
                ic = S.vector.tensor_copy(B.IZB[0:64, 0:512], B.PSF3[0:64, 0:512])
                S.sf("vector", ic)
                it = S.vector.tensor_tensor(
                    B.OT[base:base + 64, ots(p * SEQ, c, 512)],
                    B.PSF2[0:64, 0:512], B.IZB[0:64, 0:512], op=ALU.mult)
                mo = S.tinc(it, "dve", 1)
                av_free = ("dve", mo)
                u_free = ("dve", mo)
        bb_free[p % 2] = ("act", ma_last)
    m_ot = S.cnt["dve"]

    S.wait("tensor", "dve", m_ot)
    x_m = ("dve", 0)
    ta_pe = 0
    for c in range(4):
        for j in range(8):
            slo, mro = wso.load(S, W.wo_s[j, :, :], 8 * 128)
            S.wait("tensor", ENG_SEM[wso.ce], mro)
            wbo = wso.wbuf(slo)
            k, ps = (j % 2), B[f"PSF{j % 2}"]
            S.wait("tensor", *sc_free[k])
            lastmm = None
            for kt in range(ND):
                lhsT = wbo[:, ts_(kt, 128)]
                rhs = B.OT[:, ots(kt * SEQ, c, 512)]
                lastmm = S.tensor.matmul(ps[:, 0:512], lhsT, rhs,
                                         start=(kt == 0), stop=(kt == ND - 1))
            mp = S.tinc(lastmm, "pe", 1)
            wso.release(S, slo, mp)
            S.wait("vector", "pe", mp)
            S.wait("vector", "pe", ta_pe)
            if cfg.get("o_bias"):
                ie = S.vector.tensor_scalar(B.OUTT[:, ts_(j, 512)], ps[:, 0:512],
                                            W.bo_col[:, j:j + 1], None, op0=ALU.add)
            else:
                ie = S.vector.tensor_copy(B.OUTT[:, ts_(j, 512)], ps[:, 0:512])
            ma = S.tinc(ie, "dve", 1)
            sc_free[k] = ("dve", ma)
        S.wait("tensor", "dve", S.cnt["dve"])
        _, xm = emit_out_transpose_add(S, B, B.OUTT, c)
        ta_pe = S.cnt["pe"]
        x_m = ("dve", xm)
    return x_m



def alloc_common(nc, stack, need):
    B = NS()

    def sb(name, shape, dt):
        B[name] = stack.enter_context(nc.sbuf_tensor(name, shape, dt))

    def psum(name, shape, dt):
        B[name] = stack.enter_context(nc.psum_tensor(name, shape, dt))

    sb("X", [128, NT * DM], F32)
    sb("YT", [128, ND * SEQ], BF16)
    sb("YROLL", [128, 2 * DM], BF16)
    sb("S1", [128, NT], F32)
    sb("S2", [128, NT], F32)
    sb("MU", [128, NT], F32)
    sb("MUSQ", [128, NT], F32)
    sb("VAR", [128, NT], F32)
    sb("SD", [128, NT], F32)
    sb("RSTD", [128, NT], F32)
    sb("LNSQ", [128, DM], F32)
    B["LNDISC"] = B["LNSQ"]
    sb("IDENT", [128, 128], BF16)

    if need & {"attn", "l1ffn", "l0ffn", "l3ffn"}:
        sb("OUTT", [128, 8 * 512], BF16)
    sb("ONES", [128, 128], BF16)
    sb("ONESF", [128, 128], F32)

    if "ffn" in need:
        sb("H", [128, 32 * 512], BF16)
        sb("w_stage0", [128, 2048], F32)
        sb("w_wbuf0", [128, 4096], BF16)
        sb("w_wbuf1", [128, 4096], BF16)

    if "l1ffn" in need:
        sb("RELU", [128, 2 * 512], BF16)
        sb("Y0", [128, 2 * 512], F32)
        sb("MEM", [128, 2 * 512], F32)
        sb("SPK", [128, 2 * 512], BF16)

    if "attn" in need:
        sb("QTP", [128, SEQ], BF16)
        sb("KTP", [128, SEQ], BF16)
        sb("VAUG", [128, NT * 130], BF16)
        if "l3attn" not in need:
            sb("OT", [128, ND * SEQ], BF16)
        sb("PT", [128, 2 * 512], BF16)
        sb("QSQ", [128, SEQ], BF16)
        sb("B4", [1, 16], F32)
        sb("IZ", [1, 512], F32)
        sb("IZB", [128, 512], F32)
        sb("BB", [128, 2], F32)
        sb("MASKS", [128, 4 * 512], BF16)
        sb("aw_stage0", [128, 3072], F32)
        sb("aw_wbuf0", [128, 3072], BF16)
        sb("aw_wbuf1", [128, 3072], BF16)

    if "l0ffn" in need:
        sb("LORA_T", [16, 1024], BF16)
        sb("ERALL", [1, 8 * 1024], BF16)
        sb("TR", [1, 4 * 512], BF16)
        sb("UR", [1, 4 * 512], BF16)
        sb("B1C", [128, 32], F32)
        sb("B2C", [128, 8], F32)
        sb("SCALE", [1, 1], F32)

    if "l3ffn" in need:
        sb("H3", [128, 22 * 512], BF16)
        sb("YF", [128, NT * DM], BF16) if ("l3attn" not in need and "yfull" not in need) else None
        B["HTK"] = B["YF"]
        sb("XN", [128, 1408], F32)
        sb("QT", [128, 1408], F32)
        sb("QS1", [128, 1408], BF16)
        sb("QS2", [128, 1408], BF16)
        sb("SGN", [128, 1408], BF16)
        sb("QSC", [128, 176], F32)
        sb("QRS", [128, 176], F32)
        sb("w_wbuf0", [128, 2816], BF16) if "ffn" not in need else None
        sb("w_wbuf1", [128, 2816], BF16) if "ffn" not in need else None

    if "l2moe" in need:
        sb("ACC", [128, 8 * SEQ], BF16)
        sb("WTOK", [128, NT * 8], F32)
        sb("WTOKB", [128, NT * 8], BF16)
        sb("WFM", [128, SEQ], BF16)
        sb("WBC", [128, 512], BF16)
        sb("SELE", [8, 1024], BF16)
        sb("WGC", [128, 64], F32)
        sb("WGCB", [128, 64], BF16)
        sb("BGR", [1, 8], F32)
        sb("MB1C", [128, 32], F32)
        sb("MB2C", [128, 8], F32)

    if "rope" in need:
        sb("RSH", [128, SEQ], BF16)
        sb("RCOS", [128, SEQ], BF16)
        sb("RSIN", [128, SEQ], BF16)

    if "l3attn" in need or "yfull" in need:
        sb("YF", [128, NT * DM], BF16)
    if "l3attn" in need:
        B["OT"] = B["YF"]
        sb("BQE", [128, 1], F32)
        sb("BKE", [128, 1], F32)
        sb("VBC", [128, 1], BF16)
        sb("BVRD", [1, 128], BF16)

    for i in range(4):
        psum(f"PSF{i}", [128, 512], F32)
    psum("PSB0", [128, 1024], BF16)
    psum("PSB1", [128, 1024], BF16)
    B["psf_pool"] = PsumBank([B[f"PSF{i}"] for i in range(4)])
    B["psb_pool"] = PsumBank([B["PSB0"], B["PSB1"]])
    return B



def prep_lhsT_stream(wT, chunk_cols):
    K, M = wT.shape
    KT = K // 128
    C = chunk_cols
    nch = M // C
    a = wT.reshape(KT, 128, M)
    out = np.empty((nch, 128, KT * C), dtype=np.float32)
    for j in range(nch):
        blk = a[:, :, j * C:(j + 1) * C]
        out[j] = np.ascontiguousarray(blk.transpose(1, 0, 2)).reshape(128, KT * C)
    return np.ascontiguousarray(out)



def build_l1ffn_test_program():
    import contextlib
    nc = bass.Bass("TRN2", target_bir_lowering=False)
    x = nc.dram_tensor("x", [SEQ, DM], F32, kind="ExternalInput").ap()
    wfc_s = nc.dram_tensor("wfc_s", [8, 128, 8 * 512], F32, kind="ExternalInput").ap()
    wpr_s = nc.dram_tensor("wpr_s", [8, 128, 32 * 128], F32, kind="ExternalInput").ap()
    ident = nc.dram_tensor("cc_ident", [128, 128], BF16, kind="ExternalInput").ap()
    xo = nc.dram_tensor("x_out", [SEQ, DM], F32, kind="ExternalOutput").ap()

    W = NS(wfc_s=wfc_s, wpr_s=wpr_s)

    with contextlib.ExitStack() as stack:
        B = alloc_common(nc, stack, {"ffn", "l1ffn"})
        sems = {n: stack.enter_context(nc.semaphore(n)) for n in SEM_NAMES}
        block = stack.enter_context(nc.Block())

        def schedule(S):
            iid = S.sync.dma_start(B.IDENT[:, :], ident[:, :])
            S.tinc(iid, "wdma", 16)
            xr = emit_load_x(S, B, x)
            x_m = emit_l1ffn(S, B, W, xr)
            emit_store_x(S, B, xo, x_m)

        def mk(engname):
            def f(eng):
                B["psf_pool"] = PsumBank([B[f"PSF{i}"] for i in range(4)])
                B["psb_pool"] = PsumBank([B["PSB0"], B["PSB1"]])
                schedule(Sched(nc, engname, eng, sems))
            return f

        block.sync(mk("sync"))
        block.tensor(mk("tensor"))
        block.vector(mk("vector"))
        block.scalar(mk("scalar"))
        block.gpsimd(mk("gpsimd"))
    return nc


def _finish_program(nc, B, sems, block, schedule):
    def mk(engname):
        def f(eng):
            B["psf_pool"] = PsumBank([B[f"PSF{i}"] for i in range(4)])
            B["psb_pool"] = PsumBank([B["PSB0"], B["PSB1"]])
            schedule(Sched(nc, engname, eng, sems))
        return f
    block.sync(mk("sync"))
    block.tensor(mk("tensor"))
    block.vector(mk("vector"))
    block.scalar(mk("scalar"))
    block.gpsimd(mk("gpsimd"))


def emit_load_consts(S, B, consts):
    for dr, sb in consts:
        cast = getattr(dr.tensor, "dtype", None) != getattr(sb.tensor, "dtype", None)
        if cast:
            ins = S.gpsimd.dma_start(sb, dr)
            S.tinc(ins, "gdma", 16)
        else:
            ins = S.sync.dma_start(sb, dr)
            S.tinc(ins, "wdma", 16)
    return ("wdma", S.cnt["wdma"]) if not any(
        getattr(d.tensor, "dtype", None) != getattr(b.tensor, "dtype", None)
        for d, b in consts) else ("gdma", S.cnt["gdma"])


def host_rope_tables():
    import ml_dtypes
    pos = np.arange(SEQ, dtype=np.float64)[None, :]
    p = np.arange(128)
    kk = (p % 64) // 2
    ang = pos / (10000.0 ** (2.0 * kk[:, None] / 64.0))
    cos = np.cos(ang)
    sin = np.sin(ang) * np.where(p % 2 == 0, -1.0, 1.0)[:, None]
    return {
        "cc_rcos": cos.astype(np.float32).astype(ml_dtypes.bfloat16),
        "cc_rsin": sin.astype(np.float32).astype(ml_dtypes.bfloat16),
    }


def build_attn_test_program(cfg):
    import contextlib
    nc = bass.Bass("TRN2", target_bir_lowering=False)
    x = nc.dram_tensor("x", [SEQ, DM], F32, kind="ExternalInput").ap()
    qkv_s = nc.dram_tensor("qkv_s", [8, 128, 3072], F32, kind="ExternalInput").ap()
    wo_s = nc.dram_tensor("wo_s", [8, 128, 1024], F32, kind="ExternalInput").ap()
    ident = nc.dram_tensor("cc_ident", [128, 128], BF16, kind="ExternalInput").ap()
    ones = nc.dram_tensor("cc_ones", [128, 128], BF16, kind="ExternalInput").ap()
    onesf = nc.dram_tensor("cc_onesf", [128, 128], F32, kind="ExternalInput").ap()
    masks = nc.dram_tensor("cc_masks", [128, 2048], BF16, kind="ExternalInput").ap()
    W = NS(qkv_s=qkv_s, wo_s=wo_s)
    need = {"attn"}
    if cfg.get("rope"):
        need.add("rope")
        W["rcos"] = nc.dram_tensor("cc_rcos", [128, SEQ], BF16, kind="ExternalInput").ap()
        W["rsin"] = nc.dram_tensor("cc_rsin", [128, SEQ], BF16, kind="ExternalInput").ap()
    if cfg.get("qk_bias"):
        W["bq_col"] = nc.dram_tensor("bq_col", [128, 8], F32, kind="ExternalInput").ap()
        W["bk_col"] = nc.dram_tensor("bk_col", [128, 8], F32, kind="ExternalInput").ap()
    if cfg.get("v_bias_row"):
        W["bv_row_t"] = nc.dram_tensor("bv_row", [1, 1024], F32, kind="ExternalInput").ap()
    if cfg.get("o_bias"):
        W["bo_col"] = nc.dram_tensor("bo_col", [128, 8], F32, kind="ExternalInput").ap()
    xo = nc.dram_tensor("x_out", [SEQ, DM], F32, kind="ExternalOutput").ap()

    with contextlib.ExitStack() as stack:
        B = alloc_common(nc, stack, need)
        if cfg.get("qk_bias"):
            B["BQC"] = stack.enter_context(nc.sbuf_tensor("BQC", [128, 8], F32))
            B["BKC"] = stack.enter_context(nc.sbuf_tensor("BKC", [128, 8], F32))
        if cfg.get("v_bias_row"):
            B["BVR"] = stack.enter_context(nc.sbuf_tensor("BVR", [1, 1024], F32))
        if cfg.get("o_bias"):
            B["BOC"] = stack.enter_context(nc.sbuf_tensor("BOC", [128, 8], F32))
        sems = {n: stack.enter_context(nc.semaphore(n)) for n in SEM_NAMES}
        block = stack.enter_context(nc.Block())

        def schedule(S):
            consts = [(ident[:, :], B.IDENT[:, :]), (ones[:, :], B.ONES[:, :]),
                      (onesf[:, :], B.ONESF[:, :]), (masks[:, :], B.MASKS[:, :])]
            Wd = NS(qkv_s=qkv_s, wo_s=wo_s)
            if cfg.get("qk_bias"):
                consts += [(W["bq_col"], B.BQC[:, :]), (W["bk_col"], B.BKC[:, :])]
                Wd["bq_col"] = B.BQC
                Wd["bk_col"] = B.BKC
            if cfg.get("v_bias_row"):
                consts += [(W["bv_row_t"], B.BVR[:, :])]
                Wd["bv_row"] = B.BVR
            if cfg.get("o_bias"):
                consts += [(W["bo_col"], B.BOC[:, :])]
                Wd["bo_col"] = B.BOC
            if cfg.get("rope"):
                consts += [(W["rcos"], B.RCOS[:, :]), (W["rsin"], B.RSIN[:, :])]
            cm = emit_load_consts(S, B, consts)
            xr = emit_load_x(S, B, x)
            S.wait("vector", *cm)
            S.wait("scalar", *cm)
            S.wait("tensor", *cm)
            x_m = emit_attn(S, B, Wd, xr, cfg)
            emit_store_x(S, B, xo, x_m)

        _finish_program(nc, B, sems, block, schedule)
    return nc


def host_prep_attn(wq_io, wk_io, wv_io, woT, bq=None, bk=None, bv=None, bo=None):
    qs = prep_lhsT_stream(np.ascontiguousarray(wq_io), 128)
    ks = prep_lhsT_stream(np.ascontiguousarray(wk_io), 128)
    vs = prep_lhsT_stream(np.ascontiguousarray(wv_io), 128)
    out = {
        "qkv_s": np.ascontiguousarray(np.concatenate([qs, ks, vs], axis=2)),
        "wo_s": prep_lhsT_stream(np.ascontiguousarray(woT), 128),
    }
    if bq is not None:
        out["bq_col"] = np.ascontiguousarray(bq.reshape(8, 128).T)
        out["bk_col"] = np.ascontiguousarray(bk.reshape(8, 128).T)
    if bv is not None:
        out["bv_row"] = np.ascontiguousarray(bv.reshape(1, 1024))
    if bo is not None:
        out["bo_col"] = np.ascontiguousarray(bo.reshape(8, 128).T)
    return out




def build_l3attn_test_program():
    import contextlib
    nc = bass.Bass("TRN2", target_bir_lowering=False)
    x = nc.dram_tensor("x", [SEQ, DM], F32, kind="ExternalInput").ap()
    qkv_s = nc.dram_tensor("qkv_s", [8, 128, 3072], F32, kind="ExternalInput").ap()
    wo_s = nc.dram_tensor("wo_s", [8, 128, 1024], F32, kind="ExternalInput").ap()
    bqkv_col = nc.dram_tensor("bqkv_col", [128, 24], F32, kind="ExternalInput").ap()
    bo_col = nc.dram_tensor("bo_col", [128, 8], F32, kind="ExternalInput").ap()
    lnw_col = nc.dram_tensor("lnw_col", [128, 8], F32, kind="ExternalInput").ap()
    lnb_col = nc.dram_tensor("lnb_col", [128, 8], F32, kind="ExternalInput").ap()
    ident = nc.dram_tensor("cc_ident", [128, 128], BF16, kind="ExternalInput").ap()
    ones = nc.dram_tensor("cc_ones", [128, 128], BF16, kind="ExternalInput").ap()
    onesf = nc.dram_tensor("cc_onesf", [128, 128], F32, kind="ExternalInput").ap()
    masks = nc.dram_tensor("cc_masks", [128, 2048], BF16, kind="ExternalInput").ap()
    xo = nc.dram_tensor("x_out", [SEQ, DM], F32, kind="ExternalOutput").ap()

    with contextlib.ExitStack() as stack:
        B = alloc_common(nc, stack, {"attn", "l3attn"})
        for nm, shape, dt in (("BQKVC", [128, 24], F32), ("BOC", [128, 8], F32),
                              ("LNWC", [128, 8], F32), ("LNBC", [128, 8], F32)):
            B[nm] = stack.enter_context(nc.sbuf_tensor(nm, shape, dt))
        sems = {n: stack.enter_context(nc.semaphore(n)) for n in SEM_NAMES}
        block = stack.enter_context(nc.Block())

        def schedule(S):
            consts = [(ident[:, :], B.IDENT[:, :]), (ones[:, :], B.ONES[:, :]),
                      (onesf[:, :], B.ONESF[:, :]), (masks[:, :], B.MASKS[:, :]),
                      (bqkv_col[:, :], B.BQKVC[:, :]), (bo_col[:, :], B.BOC[:, :]),
                      (lnw_col[:, :], B.LNWC[:, :]), (lnb_col[:, :], B.LNBC[:, :])]
            cm = emit_load_consts(S, B, consts)
            xr = emit_load_x(S, B, x)
            S.wait("vector", *cm)
            S.wait("scalar", *cm)
            S.wait("tensor", *cm)

            def lnw_cast(S2, wbuf, stage, off, w):
                last = None
                for kk in range(w // 128):
                    kt = (off // 128) + kk
                    last = S2.vector.tensor_scalar(
                        wbuf[:, ts_(kt, 128)], stage[:, ts_(kk, 128)],
                        B.LNWC[:, (kt % 8):(kt % 8) + 1], None, op0=ALU.mult)
                return last

            cfg = NS(causal=False, inner_residual=True, dev_qkv_bias=True,
                     v_bias_row="BVRD", o_bias=True, cast_fn=lnw_cast)
            Wd = NS(qkv_s=qkv_s, wo_s=wo_s, bqkv_col=B.BQKVC, bo_col=B.BOC)
            x_m = emit_attn(S, B, Wd, xr, cfg)
            emit_store_x(S, B, xo, x_m)

        _finish_program(nc, B, sems, block, schedule)
    return nc


def host_prep_l3attn(W_qkv, b_qkv, W_proj, b_proj, ln_w, ln_b):
    import ml_dtypes
    qs = prep_lhsT_stream(np.ascontiguousarray(W_qkv[:, 0:1024]), 128)
    ks = prep_lhsT_stream(np.ascontiguousarray(W_qkv[:, 1024:2048]), 128)
    vs = prep_lhsT_stream(np.ascontiguousarray(W_qkv[:, 2048:3072]), 128)
    bq = np.stack([b_qkv[0:1024].reshape(8, 128).T,
                   b_qkv[1024:2048].reshape(8, 128).T,
                   b_qkv[2048:3072].reshape(8, 128).T], axis=1)
    bqkv_col = np.ascontiguousarray(bq.transpose(0, 1, 2).reshape(128, 24))
    return {
        "qkv_s": np.ascontiguousarray(np.concatenate([qs, ks, vs], axis=2)),
        "wo_s": prep_lhsT_stream(np.ascontiguousarray(W_proj), 128),
        "bqkv_col": bqkv_col,
        "bo_col": np.ascontiguousarray(b_proj.reshape(8, 128).T),
        "lnw_col": np.ascontiguousarray(ln_w.reshape(8, 128).T),
        "lnb_col": np.ascontiguousarray(ln_b.reshape(8, 128).T),
    }





def emit_l0ffn(S, B, W, x_ready):
    r = emit_ln(S, B, x_ready)
    yt_sem, yt_m = r.yt
    ws = WStream(B, "w", cast_eng="vector", single_stage=True)
    S.wait("tensor", yt_sem, yt_m)

    for kt in range(ND):
        kb, psb = B.psb_pool.next(S, waiter="tensor")
        itr = S.tensor.transpose(psb[0:16, 0:128],
                                 B.YT[:, kt * SEQ + 1: kt * SEQ + 17],
                                 B.IDENT[:, 0:128])
        mp = S.tinc(itr, "pe", 1)
        S.wait("vector", "pe", mp)
        ie = S.vector.tensor_copy(B.LORA_T[0:16, ts_(kt, 128)], psb[0:16, 0:128])
        me = S.tinc(ie, "dve", 1)
        B.psb_pool.done(kb, "dve", me)
    m_lt = S.cnt["dve"]
    S.wait("gpsimd", "dve", m_lt)
    base_g = S.cnt["gdma"]
    for sl in range(8):
        row = 2 * sl if sl < 4 else 8 + 2 * (sl - 4)
        ig = S.gpsimd.dma_start(B.ERALL[0:1, ts_(sl, 1024)],
                                B.LORA_T[row:row + 1, 0:1024])
        S.tinc(ig, "gdma", 16)
    m_er = ("gdma", S.cnt["gdma"])

    x_m = 0
    gelu_free = [0, 0]
    out_free = [0, 0]
    for c in range(4):
        uold = S.cnt["dve"]
        for i in range(4):
            kb, ps = B.psf_pool.next(S, waiter="tensor")
            lastmm = None
            for kt in range(ND):
                lastmm = S.tensor.matmul(
                    ps[0:1, 0:512],
                    B.YT[:, kt * SEQ + 2 + 2 * i: kt * SEQ + 3 + 2 * i],
                    B.YT[:, ots(kt * SEQ, c, 512)],
                    start=(kt == 0), stop=(kt == ND - 1))
            mp = S.tinc(lastmm, "pe", 1)
            S.wait("vector", "pe", mp)
            ie = S.vector.tensor_scalar(B.TR[0:1, ts_(i, 512)], ps[0:1, 0:512],
                                        B.SCALE[0:1, 0:1], None, op0=ALU.mult)
            me = S.tinc(ie, "dve", 1)
            B.psf_pool.done(kb, "dve", me)
        m_tr = S.cnt["dve"]
        S.wait("tensor", "dve", m_tr)
        S.wait("tensor", *m_er)

        for j in range(8):
            sl, mr = ws.load(S, W.w1_s[j, :, :], 8 * 512)
            S.wait("tensor", ENG_SEM[ws.ce], mr)
            wb = ws.wbuf(sl)
            lastmm_for_slot = None
            for sub in range(4):
                fblk = 4 * j + sub
                i = fblk // 8
                jj = fblk % 8
                kb, ps = B.psf_pool.next(S, waiter="tensor")
                lastmm = None
                for kt in range(ND):
                    lhsT = wb[:, kt * 512 + sub * 128: kt * 512 + (sub + 1) * 128]
                    rhs = B.YT[:, ots(kt * SEQ, c, 512)]
                    lastmm = S.tensor.matmul(ps[:, 0:512], lhsT, rhs,
                                             start=(kt == 0), stop=False)
                lastmm = S.tensor.matmul(
                    ps[:, 0:512],
                    B.ERALL[0:1, i * 1024 + jj * 128: i * 1024 + (jj + 1) * 128],
                    B.TR[0:1, ts_(i, 512)], start=False, stop=True)
                mp = S.tinc(lastmm, "pe", 1)
                lastmm_for_slot = mp
                S.wait("scalar", "pe", mp)
                S.wait("scalar", "dve", gelu_free[sub % 2])
                ig = S.scalar.activation(B.H[:, ts_(fblk, 512)], ps[:, 0:512],
                                         AF.Gelu, bias=W.b1_col[:, fblk:fblk + 1])
                ma = S.tinc(ig, "act", 1)
                B.psf_pool.done(kb, "act", ma)
            ws.release(S, sl, lastmm_for_slot)
        m_h = S.cnt["act"]

        S.wait("tensor", "act", m_h)
        for j in range(8):
            if j % 2 == 0:
                i = j // 2
                for rr in range(4):
                    kb, ps = B.psf_pool.next(S, waiter="tensor")
                    lastmm = None
                    for kt in range(ND):
                        lastmm = S.tensor.matmul(
                            ps[0:1, 0:512],
                            B.YT[:, kt * SEQ + 10 + 2 * i: kt * SEQ + 11 + 2 * i],
                            B.H[:, ts_(8 * rr + kt, 512)],
                            start=(kt == 0), stop=(kt == ND - 1))
                    mp = S.tinc(lastmm, "pe", 1)
                    S.wait("vector", "pe", mp)
                    ie = S.vector.tensor_scalar(B.UR[0:1, ts_(rr, 512)],
                                                ps[0:1, 0:512],
                                                B.SCALE[0:1, 0:1], None, op0=ALU.mult)
                    me = S.tinc(ie, "dve", 1)
                    B.psf_pool.done(kb, "dve", me)
                S.wait("tensor", "dve", S.cnt["dve"])
            sl, mr = ws.load(S, W.w2_s[j, :, :], 32 * 128)
            S.wait("tensor", ENG_SEM[ws.ce], mr)
            wb = ws.wbuf(sl)
            i = j // 2
            kb, ps = B.psf_pool.next(S, waiter="tensor")
            lastmm = None
            for kt in range(32):
                lastmm = S.tensor.matmul(ps[:, 0:512], wb[:, ts_(kt, 128)],
                                         B.H[:, ts_(kt, 512)],
                                         start=(kt == 0), stop=False)
            for rr in range(4):
                st = (4 + i) * 1024 + (j % 2) * 512 + rr
                lhsT = B.ERALL[0:1, st: st + 509: 4]
                lastmm = S.tensor.matmul(ps[:, 0:512], lhsT,
                                         B.UR[0:1, ts_(rr, 512)],
                                         start=False, stop=(rr == 3))
            mp = S.tinc(lastmm, "pe", 1)
            ws.release(S, sl, mp)
            S.wait("vector", "pe", mp)
            S.wait("vector", "dve", out_free[j % 2])
            ie = S.vector.tensor_scalar(B.OUTT[:, ts_(j, 512)], ps[:, 0:512],
                                        W.b2_col[:, j:j + 1], 0.5,
                                        op0=ALU.add, op1=ALU.mult)
            me = S.tinc(ie, "dve", 1)
            B.psf_pool.done(kb, "dve", me)
        S.wait("tensor", "dve", S.cnt["dve"])
        _, x_m = emit_out_transpose_add(S, B, B.OUTT, c)
    return ("dve", x_m)


def build_l0ffn_test_program():
    import contextlib
    nc = bass.Bass("TRN2", target_bir_lowering=False)
    x = nc.dram_tensor("x", [SEQ, DM], F32, kind="ExternalInput").ap()
    w1_s = nc.dram_tensor("w1_s", [8, 128, 8 * 512], F32, kind="ExternalInput").ap()
    w2_s = nc.dram_tensor("w2_s", [8, 128, 32 * 128], F32, kind="ExternalInput").ap()
    b1_col = nc.dram_tensor("b1_col", [128, 32], F32, kind="ExternalInput").ap()
    b2_col = nc.dram_tensor("b2_col", [128, 8], F32, kind="ExternalInput").ap()
    scale_in = nc.dram_tensor("scale_in", [1, 1], F32, kind="ExternalInput").ap()
    ident = nc.dram_tensor("cc_ident", [128, 128], BF16, kind="ExternalInput").ap()
    xo = nc.dram_tensor("x_out", [SEQ, DM], F32, kind="ExternalOutput").ap()

    with contextlib.ExitStack() as stack:
        B = alloc_common(nc, stack, {"ffn", "l0ffn"})
        sems = {n: stack.enter_context(nc.semaphore(n)) for n in SEM_NAMES}
        block = stack.enter_context(nc.Block())

        def schedule(S):
            consts = [(ident[:, :], B.IDENT[:, :]), (b1_col[:, :], B.B1C[:, :]),
                      (b2_col[:, :], B.B2C[:, :]), (scale_in[:, :], B.SCALE[0:1, 0:1])]
            cm = emit_load_consts(S, B, consts)
            xr = emit_load_x(S, B, x)
            S.wait("vector", *cm)
            S.wait("scalar", *cm)
            S.wait("tensor", *cm)
            Wd = NS(w1_s=w1_s, w2_s=w2_s, b1_col=B.B1C, b2_col=B.B2C)
            x_m = emit_l0ffn(S, B, Wd, xr)
            emit_store_x(S, B, xo, x_m)

        _finish_program(nc, B, sems, block, schedule)
    return nc


def host_prep_l0ffn(W1, b1, W2, b2, scale):
    return {
        "w1_s": prep_lhsT_stream(np.ascontiguousarray(W1.T), 512),
        "w2_s": prep_lhsT_stream(np.ascontiguousarray(W2.T), 128),
        "b1_col": np.ascontiguousarray(b1.reshape(32, 128).T),
        "b2_col": np.ascontiguousarray(b2.reshape(8, 128).T),
        "scale_in": np.asarray(scale, dtype=np.float32).reshape(1, 1),
    }





def emit_quant(S, B, src_ap, dst_ap, width):
    m = 0
    off = 0
    while off < width:
        w = min(1408, width - off)
        m = emit_quant_chain(S, B, src_ap[:, off:off + w], dst_ap[:, off:off + w], w)
        off += w
    return m


def emit_quant_chain(S, B, src_ap, dst_ap, width):
    nb = width // 16
    src3 = src_ap.rearrange("p (b g) -> p b g", g=16)
    i = S.vector.tensor_reduce(B.QSC[:, 0:nb], src3, axis=AX.X, op=ALU.max,
                               apply_absolute_value=True)
    S.sf("vector", i)
    i = S.vector.tensor_scalar(B.QSC[:, 0:nb], B.QSC[:, 0:nb], 1.0 / 6.0, 1e-12,
                               op0=ALU.mult, op1=ALU.max)
    S.sf("vector", i)
    i = S.vector.reciprocal(B.QRS[:, 0:nb], B.QSC[:, 0:nb])
    mrs = S.sf("vector", i)
    S.wait("scalar", "dve", mrs)
    isg = S.scalar.activation(B.SGN[:, 0:width], src_ap, AF.Sign)
    msg = S.tinc(isg, "act", 1)
    rsb = B.QRS[:, 0:nb].rearrange("p (b g) -> p b g", g=1).broadcast_to([128, nb, 16])
    i = S.vector.tensor_tensor(B.XN[:, 0:width].rearrange("p (b g) -> p b g", g=16),
                               src3, rsb, op=ALU.mult)
    mxn = S.tinc(i, "dve", 1)
    S.wait("vector", "act", msg)
    S.wait("scalar", "dve", mxn)
    ia = S.scalar.activation(B.XN[:, 0:width], B.XN[:, 0:width], AF.Abs)
    ma = S.tinc(ia, "act", 1)
    S.wait("vector", "act", ma)
    i = S.vector.tensor_scalar(B.QT[:, 0:width], B.XN[:, 0:width], 2.0, 4.0,
                               op0=ALU.max, op1=ALU.min)
    S.sf("vector", i)
    i = S.vector.tensor_scalar(B.QT[:, 0:width], B.QT[:, 0:width], MAGIC, None,
                               op0=ALU.add)
    S.sf("vector", i)
    i = S.vector.tensor_scalar(B.QS2[:, 0:width], B.QT[:, 0:width], MAGIC + 2.0,
                               None, op0=ALU.subtract)
    S.sf("vector", i)
    i = S.vector.tensor_scalar(B.QS1[:, 0:width], B.XN[:, 0:width], 5.0, 2.0,
                               op0=ALU.is_gt, op1=ALU.mult)
    S.sf("vector", i)
    i = S.vector.tensor_tensor(B.QS2[:, 0:width], B.QS2[:, 0:width],
                               B.QS1[:, 0:width], op=ALU.add)
    S.sf("vector", i)
    i = S.vector.tensor_scalar(B.QT[:, 0:width], B.XN[:, 0:width], 2.0, 2.0,
                               op0=ALU.min, op1=ALU.mult)
    S.sf("vector", i)
    i = S.vector.tensor_scalar(B.QT[:, 0:width], B.QT[:, 0:width], MAGIC, None,
                               op0=ALU.add)
    S.sf("vector", i)
    i = S.vector.tensor_scalar(B.QS1[:, 0:width], B.QT[:, 0:width], MAGIC, 0.5,
                               op0=ALU.subtract, op1=ALU.mult)
    S.sf("vector", i)
    i = S.vector.tensor_tensor(B.QS1[:, 0:width], B.QS1[:, 0:width],
                               B.QS2[:, 0:width], op=ALU.add)
    S.sf("vector", i)
    i = S.vector.tensor_tensor(B.QS1[:, 0:width], B.QS1[:, 0:width],
                               B.SGN[:, 0:width], op=ALU.mult)
    S.sf("vector", i)
    scb = B.QSC[:, 0:nb].rearrange("p (b g) -> p b g", g=1).broadcast_to([128, nb, 16])
    i = S.vector.tensor_tensor(dst_ap.rearrange("p (b g) -> p b g", g=16),
                               B.QS1[:, 0:width].rearrange("p (b g) -> p b g", g=16),
                               scb, op=ALU.mult)
    m = S.sf("vector", i)
    return m


def emit_l3_wprep(S, B, W):
    jobs = [(W.wg_nat, 22, 1024, W.g_s), (W.wu_nat, 22, 1024, W.u_s),
            (W.wd_nat, 8, 2816, W.d_s)]
    qt_free = ("dve", 0)
    tch_free = ("gdma", 0)
    for wsrc, ntile, tw, scratch in jobs:
        nkt = tw // 128
        for rt in range(ntile):
            S.wait("vector", *tch_free)
            off = 0
            while off < tw:
                w = min(1408, tw - off)
                S.wait("sync", *qt_free)
                idm = S.sync.dma_start(B.QT[:, 0:w],
                                       wsrc[ts_(rt, 128), off:off + w])
                mdma = S.tinc(idm, "wdma", 16)
                S.wait("vector", "wdma", mdma)
                S.wait("scalar", "wdma", mdma)
                mq = emit_quant_chain(S, B, B.QT[:, 0:w],
                                      B.H3[:, off:off + w], w)
                qt_free = ("dve", mq)
                off += w
            S.wait("tensor", "dve", S.cnt["dve"])
            npacks = (nkt + 3) // 4
            for pk in range(npacks):
                cnt = min(4, nkt - 4 * pk)
                kb, psb = B.psb_pool.next(S, waiter="tensor")
                lastmm = None
                for jj in range(cnt):
                    kt = 4 * pk + jj
                    lastmm = S.tensor.transpose(psb[:, ts_(jj, 128)],
                                                B.H3[:, ts_(kt, 128)],
                                                B.IDENT[:, 0:128])
                mp = S.tinc(lastmm, "pe", 1)
                S.wait("vector", "pe", mp)
                ie = S.vector.tensor_copy(
                    B.H3[:, 5632 + pk * 512: 5632 + pk * 512 + cnt * 128],
                    psb[:, 0:cnt * 128])
                me = S.tinc(ie, "dve", 1)
                B.psb_pool.done(kb, "dve", me)
            S.wait("gpsimd", "dve", S.cnt["dve"])
            ig = S.gpsimd.dma_start(scratch[rt, :, :], B.H3[:, 5632:5632 + tw])
            mg = S.tinc(ig, "gdma", 16)
            tch_free = ("gdma", mg)
    return ("gdma", S.cnt["gdma"])


def emit_l3ffn(S, B, W, x_ready, do_prep):
    prep_m = emit_l3_wprep(S, B, W) if do_prep else ("gdma", 0)
    if do_prep:
        S.wait("vector", "dve", S.cnt["dve"])
        S.wait("scalar", "dve", S.cnt["dve"])
    r = emit_ln(S, B, x_ready, full_y=True, yf=B.YF, make_yt=False)
    S.wait("vector", *r.yt)
    for t in range(NT):
        iy = S.vector.tensor_scalar(B.QT[:, 0:DM], B.X[:, ts_(t, DM)],
                                    B.MU[:, t:t + 1], B.RSTD[:, t:t + 1],
                                    op0=ALU.subtract, op1=ALU.mult)
        S.sf("vector", iy)
        emit_quant(S, B, B.QT[:, 0:DM], B.YF[:, ts_(t, DM)], DM)
        S.wait("tensor", "dve", S.cnt["dve"])
        emit_transpose_tile_to_feat(S, B, B.YF[:, ts_(t, DM)], B.YT, t)
    m_yt = S.cnt["dve"]
    ws = WStream(B, "w", cast_eng="vector")
    S.wait("sync", *prep_m)
    S.wait("tensor", "dve", m_yt)

    x_m = 0
    sg_free = [("dve", 0), ("dve", 0)]
    for c in range(4):
        for m in range(22):
            slg, mg = ws.load_bf16(S, W.g_s[m, :, :], 1024)
            S.wait("tensor", "wdma", mg)
            wbg = ws.wbuf(slg)
            ka, psA = B.psf_pool.next(S, waiter="tensor")
            lastmm = None
            for kt in range(ND):
                lastmm = S.tensor.matmul(psA[:, 0:512], wbg[:, ts_(kt, 128)],
                                         B.YT[:, ots(kt * SEQ, c, 512)],
                                         start=(kt == 0), stop=(kt == ND - 1))
            mpa = S.tinc(lastmm, "pe", 1)
            ws.release(S, slg, mpa)
            slu, mu_ = ws.load_bf16(S, W.u_s[m, :, :], 1024)
            S.wait("tensor", "wdma", mu_)
            wbu = ws.wbuf(slu)
            kbb, psB = B.psf_pool.next(S, waiter="tensor")
            lastmm = None
            for kt in range(ND):
                lastmm = S.tensor.matmul(psB[:, 0:512], wbu[:, ts_(kt, 128)],
                                         B.YT[:, ots(kt * SEQ, c, 512)],
                                         start=(kt == 0), stop=(kt == ND - 1))
            mpb = S.tinc(lastmm, "pe", 1)
            ws.release(S, slu, mpb)
            S.wait("scalar", "pe", mpa)
            S.wait("scalar", *sg_free[m % 2])
            isl = S.scalar.activation(B.QS1[:, ts_(m % 2, 512)], psA[:, 0:512], AF.Silu)
            msl = S.tinc(isl, "act", 1)
            B.psf_pool.done(ka, "act", msl)
            S.wait("vector", "act", msl)
            S.wait("vector", "pe", mpb)
            ih = S.vector.tensor_tensor(B.H3[:, ts_(m, 512)], B.QS1[:, ts_(m % 2, 512)],
                                        psB[:, 0:512], op=ALU.mult)
            mh = S.tinc(ih, "dve", 1)
            sg_free[m % 2] = ("dve", mh)
            B.psf_pool.done(kbb, "dve", mh)
        m_h3 = S.cnt["dve"]

        S.wait("tensor", "dve", m_h3)
        h3v = B.H3[:, 0:22 * 512].rearrange("p (m q) -> p m q", q=512)
        for tsub in range(4):
            for pk in range(6):
                cnt = min(4, 22 - 4 * pk)
                kb, psb = B.psb_pool.next(S, waiter="tensor")
                lastmm = None
                for jj in range(cnt):
                    m = 4 * pk + jj
                    lastmm = S.tensor.transpose(psb[:, ts_(jj, 128)],
                                                h3v[:, m, ts_(tsub, 128)],
                                                B.IDENT[:, 0:128])
                mp = S.tinc(lastmm, "pe", 1)
                S.wait("vector", "pe", mp)
                ie = S.vector.tensor_copy(
                    B.HTK[:, tsub * 2816 + pk * 512: tsub * 2816 + pk * 512 + cnt * 128],
                    psb[:, 0:cnt * 128])
                me = S.tinc(ie, "dve", 1)
                B.psb_pool.done(kb, "dve", me)
        for tsub in range(4):
            emit_quant(S, B, B.HTK[:, ts_(tsub, 2816)],
                       B.HTK[:, ts_(tsub, 2816)], 2816)
        m_hq = S.cnt["dve"]
        S.wait("tensor", "dve", m_hq)
        hkv = B.HTK[:, 0:4 * 2816].rearrange("p (t q) -> p t q", q=2816)
        for m in range(22):
            kb, psb = B.psb_pool.next(S, waiter="tensor")
            lastmm = None
            for tsub in range(4):
                lastmm = S.tensor.transpose(psb[:, ts_(tsub, 128)],
                                            hkv[:, tsub, ts_(m, 128)],
                                            B.IDENT[:, 0:128])
            mp = S.tinc(lastmm, "pe", 1)
            S.wait("vector", "pe", mp)
            ie = S.vector.tensor_copy(B.H3[:, ts_(m, 512)], psb[:, 0:512])
            me = S.tinc(ie, "dve", 1)
            B.psb_pool.done(kb, "dve", me)
        m_hqt = S.cnt["dve"]
        S.wait("tensor", "dve", m_hqt)

        for j in range(8):
            sld, md = ws.load_bf16(S, W.d_s[j, :, :], 2816)
            S.wait("tensor", "wdma", md)
            wbd = ws.wbuf(sld)
            kb, ps = B.psf_pool.next(S, waiter="tensor")
            lastmm = None
            for kt in range(22):
                lastmm = S.tensor.matmul(ps[:, 0:512], wbd[:, ts_(kt, 128)],
                                         B.H3[:, ts_(kt, 512)],
                                         start=(kt == 0), stop=(kt == 21))
            mp = S.tinc(lastmm, "pe", 1)
            ws.release(S, sld, mp)
            S.wait("vector", "pe", mp)
            ie = S.vector.tensor_scalar(B.OUTT[:, ts_(j, 512)], ps[:, 0:512],
                                        0.5, None, op0=ALU.mult)
            me = S.tinc(ie, "dve", 1)
            B.psf_pool.done(kb, "dve", me)
        S.wait("tensor", "dve", S.cnt["dve"])
        _, x_m = emit_out_transpose_add(S, B, B.OUTT, c)
    return ("dve", x_m)


def build_l3ffn_test_program():
    import contextlib
    nc = bass.Bass("TRN2", target_bir_lowering=False)
    x = nc.dram_tensor("x", [SEQ, DM], F32, kind="ExternalInput").ap()
    wg = nc.dram_tensor("wg_nat", [2816, 1024], F32, kind="ExternalInput").ap()
    wu = nc.dram_tensor("wu_nat", [2816, 1024], F32, kind="ExternalInput").ap()
    wd = nc.dram_tensor("wd_nat", [1024, 2816], F32, kind="ExternalInput").ap()
    g_s = nc.dram_tensor("g_scr", [22, 128, 1024], BF16).ap()
    u_s = nc.dram_tensor("u_scr", [22, 128, 1024], BF16).ap()
    d_s = nc.dram_tensor("d_scr", [8, 128, 2816], BF16).ap()
    ident = nc.dram_tensor("cc_ident", [128, 128], BF16, kind="ExternalInput").ap()
    xo = nc.dram_tensor("x_out", [SEQ, DM], F32, kind="ExternalOutput").ap()

    with contextlib.ExitStack() as stack:
        B = alloc_common(nc, stack, {"l3ffn"})
        sems = {n: stack.enter_context(nc.semaphore(n)) for n in SEM_NAMES}
        block = stack.enter_context(nc.Block())

        def schedule(S):
            cm = emit_load_consts(S, B, [(ident[:, :], B.IDENT[:, :])])
            xr = emit_load_x(S, B, x)
            S.wait("vector", *cm)
            S.wait("scalar", *cm)
            S.wait("tensor", *cm)
            Wd = NS(wg_nat=wg, wu_nat=wu, wd_nat=wd, g_s=g_s, u_s=u_s, d_s=d_s)
            x_m = emit_l3ffn(S, B, Wd, xr, do_prep=True)
            emit_store_x(S, B, xo, x_m)

        _finish_program(nc, B, sems, block, schedule)
    return nc





def emit_l2moe(S, B, W, x_ready, cached):
    r = emit_ln(S, B, x_ready)
    yt_sem, yt_m = r.yt
    ws = WStream(B, "w", cast_eng="vector", single_stage=True)
    S.wait("tensor", yt_sem, yt_m)

    icst = S.vector.tensor_copy(B.WGCB[:, 0:64], B.WGC[:, 0:64])
    mcst = S.tinc(icst, "dve", 1)
    S.wait("tensor", "dve", mcst)
    for t in range(NT):
        kb, ps = B.psf_pool.next(S, waiter="tensor")
        lastmm = None
        for kt in range(ND):
            lastmm = S.tensor.matmul(ps[:, 0:8],
                                     B.YT[:, ots(kt * SEQ, t, 128)],
                                     B.WGCB[:, ts_(kt, 8)],
                                     start=(kt == 0), stop=False)
        lastmm = S.tensor.matmul(ps[:, 0:8], B.ONESF[0:1, 0:128],
                                 B.BGR[0:1, 0:8], start=False, stop=True)
        mp = S.tinc(lastmm, "pe", 1)
        S.wait("vector", "pe", mp)
        i = S.vector.reduce_max(B.SD[:, 0:1], ps[:, 0:8], axis=AX.X, negate=True)
        S.sf("vector", i)
        S.wait("scalar", "dve", S.cnt["dve"])
        S.wait("scalar", "pe", mp)
        ie = S.scalar.activation(B.S1[:, 0:8], ps[:, 0:8], AF.Exp,
                                 bias=B.SD[:, 0:1])
        ma = S.tinc(ie, "act", 1)
        B.psf_pool.done(kb, "act", ma)
        S.wait("vector", "act", ma)
        i = S.vector.reduce_max(B.VAR[:, 0:1], B.S1[:, 0:8], axis=AX.X)
        S.sf("vector", i)
        i = S.vector.tensor_scalar(B.S2[:, 0:8], B.S1[:, 0:8], B.VAR[:, 0:1],
                                   None, op0=ALU.is_ge)
        S.sf("vector", i)
        i = S.vector.scalar_tensor_tensor(B.S2[:, 0:8], B.S2[:, 0:8], -2.0,
                                          B.S1[:, 0:8], op0=ALU.mult, op1=ALU.add)
        S.sf("vector", i)
        i = S.vector.reduce_max(B.MUSQ[:, 0:1], B.S2[:, 0:8], axis=AX.X)
        S.sf("vector", i)
        i = S.vector.tensor_scalar(B.S2[:, 0:8], B.S1[:, 0:8], B.MUSQ[:, 0:1],
                                   None, op0=ALU.is_ge)
        S.sf("vector", i)
        i = S.vector.tensor_tensor(B.S1[:, 0:8], B.S1[:, 0:8], B.S2[:, 0:8],
                                   op=ALU.mult)
        S.sf("vector", i)
        i = S.vector.tensor_tensor(B.VAR[:, 0:1], B.VAR[:, 0:1], B.MUSQ[:, 0:1],
                                   op=ALU.add)
        S.sf("vector", i)
        i = S.vector.reciprocal(B.VAR[:, 0:1], B.VAR[:, 0:1])
        S.sf("vector", i)
        i = S.vector.tensor_scalar(B.WTOK[:, ts_(t, 8)], B.S1[:, 0:8],
                                   B.VAR[:, 0:1], None, op0=ALU.mult)
        S.sf("vector", i)
    S.wait("tensor", "dve", S.cnt["dve"])
    for t in range(NT):
        kb, psb = B.psb_pool.next(S, waiter="tensor")
        S.wait("vector", "pe", 0)
        ic = S.vector.tensor_copy(B.WTOKB[:, ts_(t, 8)], B.WTOK[:, ts_(t, 8)])
        mc = S.sf("vector", ic)
        S.wait("tensor", "dve", mc)
        itr = S.tensor.transpose(psb[0:8, 0:128], B.WTOKB[:, ts_(t, 8)],
                                 B.IDENT[:, 0:128])
        mp = S.tinc(itr, "pe", 1)
        S.wait("vector", "pe", mp)
        ie = S.vector.tensor_copy(B.WFM[0:8, ts_(t, 128)], psb[0:8, 0:128])
        me = S.tinc(ie, "dve", 1)
        B.psb_pool.done(kb, "dve", me)
    m_wfm = S.cnt["dve"]

    iz = S.vector.memset(B.ACC[:, 0:8 * SEQ], 0.0)
    m_acc0 = S.tinc(iz, "dve", 1)

    x_m = 0
    for e in range(8):
        S.wait("gpsimd", "dve", m_wfm)
        ig = S.gpsimd.dma_start(B.MB1C[:, 0:32], W.b1_cols[e, :, :])
        S.tinc(ig, "gdma", 16)
        ig = S.gpsimd.dma_start(B.MB2C[:, 0:8], W.b2_cols[e, :, :])
        S.tinc(ig, "gdma", 16)
        m_bias = ("gdma", S.cnt["gdma"])
        S.wait("tensor", *m_bias)
        S.wait("vector", "dve", m_acc0)

        for c in range(4):
            kb, ps = B.psf_pool.next(S, waiter="tensor")
            mm = S.tensor.matmul(ps[:, 0:512], B.SELE[0:8, ts_(e, 128)],
                                 B.WFM[0:8, ts_(c, 512)], start=True, stop=True)
            mpw = S.tinc(mm, "pe", 1)
            S.wait("vector", "pe", mpw)
            ie = S.vector.tensor_copy(B.WBC[:, 0:512], ps[:, 0:512])
            mwbc = S.tinc(ie, "dve", 1)
            B.psf_pool.done(kb, "dve", mwbc)
            for j in range(8):
                if cached[0]:
                    sl, mr = ws.load_bf16(S, W.w1b_s[e, j, :, :], 4096)
                    S.wait("tensor", "wdma", mr)
                else:
                    sl, mr = ws.load(S, W.w1_s[e, j, :, :], 4096)
                    S.wait("tensor", ENG_SEM[ws.ce], mr)
                    S.wait("gpsimd", ENG_SEM[ws.ce], mr)
                    iw = S.gpsimd.dma_start(W.w1b_s[e, j, :, :], ws.wbuf(sl)[:, 0:4096])
                    mwb = S.tinc(iw, "gdma", 16)
                    ws.wbuf_extra[sl] = ("gdma", mwb)
                wb = ws.wbuf(sl)
                last_use = 0
                for sub in range(4):
                    fblk = 4 * j + sub
                    kb, ps = B.psf_pool.next(S, waiter="tensor")
                    lastmm = None
                    for kt in range(ND):
                        lhsT = wb[:, kt * 512 + sub * 128: kt * 512 + (sub + 1) * 128]
                        rhs = B.YT[:, ots(kt * SEQ, c, 512)]
                        lastmm = S.tensor.matmul(ps[:, 0:512], lhsT, rhs,
                                                 start=(kt == 0), stop=(kt == ND - 1))
                    mp = S.tinc(lastmm, "pe", 1)
                    last_use = mp
                    S.wait("scalar", "pe", mp)
                    igl = S.scalar.activation(B.H[:, ts_(fblk, 512)], ps[:, 0:512],
                                              AF.Gelu, bias=B.MB1C[:, fblk:fblk + 1])
                    ma = S.tinc(igl, "act", 1)
                    B.psf_pool.done(kb, "act", ma)
                ws.release(S, sl, last_use)
            m_h = S.cnt["act"]
            S.wait("tensor", "act", m_h)

            for j in range(8):
                if cached[1]:
                    sl, mr = ws.load_bf16(S, W.w2b_s[e, j, :, :], 4096)
                    S.wait("tensor", "wdma", mr)
                else:
                    sl, mr = ws.load(S, W.w2_s[e, j, :, :], 4096)
                    S.wait("tensor", ENG_SEM[ws.ce], mr)
                    S.wait("gpsimd", ENG_SEM[ws.ce], mr)
                    iw = S.gpsimd.dma_start(W.w2b_s[e, j, :, :], ws.wbuf(sl)[:, 0:4096])
                    mwb = S.tinc(iw, "gdma", 16)
                    ws.wbuf_extra[sl] = ("gdma", mwb)
                wb = ws.wbuf(sl)
                kb, ps = B.psf_pool.next(S, waiter="tensor")
                lastmm = None
                for kt in range(32):
                    lastmm = S.tensor.matmul(ps[:, 0:512], wb[:, ts_(kt, 128)],
                                             B.H[:, ts_(kt, 512)],
                                             start=(kt == 0), stop=(kt == 31))
                mp = S.tinc(lastmm, "pe", 1)
                ws.release(S, sl, mp)
                S.wait("vector", "pe", mp)
                i1 = S.vector.tensor_scalar(B.LNSQ[:, 0:512], ps[:, 0:512],
                                            B.MB2C[:, j:j + 1], None, op0=ALU.add)
                S.sf("vector", i1)
                i2 = S.vector.tensor_tensor(B.LNSQ[:, 0:512], B.LNSQ[:, 0:512],
                                            B.WBC[:, 0:512], op=ALU.mult)
                S.sf("vector", i2)
                accs = B.ACC[:, j * SEQ + c * 512: j * SEQ + (c + 1) * 512]
                i3 = S.vector.tensor_tensor(accs, accs, B.LNSQ[:, 0:512], op=ALU.add)
                me = S.sf("vector", i3)
                B.psf_pool.done(kb, "dve", me)
        if not cached[0]:
            cached[0] = True if e == 7 else cached[0]
        if not cached[1]:
            cached[1] = True if e == 7 else cached[1]
    m_comb = S.cnt["dve"]

    accv = B.ACC[:, 0:8 * SEQ].rearrange("p (d q) -> p d q", q=SEQ)
    S.wait("tensor", "dve", m_comb)
    for c in range(4):
        for tsub in range(4):
            t = 4 * c + tsub
            for pack in range(2):
                kb, psb = B.psb_pool.next(S, waiter="tensor")
                lastmm = None
                for jj in range(4):
                    d = pack * 4 + jj
                    lastmm = S.tensor.transpose(
                        psb[:, ts_(jj, 128)],
                        accv[:, d, ots(c * 512, tsub, 128)],
                        B.IDENT[:, 0:128])
                mp = S.tinc(lastmm, "pe", 1)
                S.wait("vector", "pe", mp)
                dst = B.X[:, t * DM + pack * 512: t * DM + (pack + 1) * 512]
                ev = S.vector.scalar_tensor_tensor(dst, psb[:, 0:512], 0.5, dst,
                                                   op0=ALU.mult, op1=ALU.add)
                me = S.tinc(ev, "dve", 1)
                B.psb_pool.done(kb, "dve", me)
                x_m = me
    return ("dve", x_m)


def build_l2moe_test_program():
    import contextlib
    nc = bass.Bass("TRN2", target_bir_lowering=False)
    x = nc.dram_tensor("x", [SEQ, DM], F32, kind="ExternalInput").ap()
    wg_cols = nc.dram_tensor("wg_cols", [128, 64], F32, kind="ExternalInput").ap()
    bg_row = nc.dram_tensor("bg_row", [1, 8], F32, kind="ExternalInput").ap()
    w1_s = nc.dram_tensor("w1_s", [8, 8, 128, 4096], F32, kind="ExternalInput").ap()
    w2_s = nc.dram_tensor("w2_s", [8, 8, 128, 4096], F32, kind="ExternalInput").ap()
    b1_cols = nc.dram_tensor("b1_cols", [8, 128, 32], F32, kind="ExternalInput").ap()
    b2_cols = nc.dram_tensor("b2_cols", [8, 128, 8], F32, kind="ExternalInput").ap()
    w1b_s = nc.dram_tensor("w1b_scr", [8, 8, 128, 4096], BF16).ap()
    w2b_s = nc.dram_tensor("w2b_scr", [8, 8, 128, 4096], BF16).ap()
    ident = nc.dram_tensor("cc_ident", [128, 128], BF16, kind="ExternalInput").ap()
    ones = nc.dram_tensor("cc_ones", [128, 128], BF16, kind="ExternalInput").ap()
    onesf = nc.dram_tensor("cc_onesf", [128, 128], F32, kind="ExternalInput").ap()
    sel8 = nc.dram_tensor("cc_sel8", [8, 1024], BF16, kind="ExternalInput").ap()
    xo = nc.dram_tensor("x_out", [SEQ, DM], F32, kind="ExternalOutput").ap()

    with contextlib.ExitStack() as stack:
        B = alloc_common(nc, stack, {"ffn", "l2moe"})
        sems = {n: stack.enter_context(nc.semaphore(n)) for n in SEM_NAMES}
        block = stack.enter_context(nc.Block())

        def schedule(S):
            cm = emit_load_consts(S, B, [
                (ident[:, :], B.IDENT[:, :]), (ones[:, :], B.ONES[:, :]),
                (onesf[:, :], B.ONESF[:, :]), (wg_cols[:, :], B.WGC[:, :]),
                (bg_row[:, :], B.BGR[0:1, 0:8]), (sel8[:, :], B.SELE[:, :])])
            xr = emit_load_x(S, B, x)
            S.wait("vector", *cm)
            S.wait("scalar", *cm)
            S.wait("tensor", *cm)
            Wd = NS(w1_s=w1_s, w2_s=w2_s, w1b_s=w1b_s, w2b_s=w2b_s,
                    b1_cols=b1_cols, b2_cols=b2_cols)
            x_m = emit_l2moe(S, B, Wd, xr, cached=[False, False])
            emit_store_x(S, B, xo, x_m)

        _finish_program(nc, B, sems, block, schedule)
    return nc


def host_prep_l2moe(Wg, bg, W1, b1, W2, b2):
    w1_s = np.stack([prep_lhsT_stream(np.ascontiguousarray(W1[e]), 512)
                     for e in range(8)])
    w2_s = np.stack([prep_lhsT_stream(np.ascontiguousarray(W2[e]), 128)
                     for e in range(8)])
    return {
        "wg_cols": np.ascontiguousarray(
            Wg.reshape(8, 128, 8).transpose(1, 0, 2).reshape(128, 64)),
        "bg_row": np.ascontiguousarray(bg.reshape(1, 8)),
        "w1_s": w1_s,
        "w2_s": w2_s,
        "b1_cols": np.ascontiguousarray(
            b1.reshape(8, 32, 128).transpose(0, 2, 1)),
        "b2_cols": np.ascontiguousarray(
            b2.reshape(8, 8, 128).transpose(0, 2, 1)),
    }


def host_prep_l1ffn(W_fc, W_proj):
    wfcT = np.ascontiguousarray(W_fc.T.astype(np.float32))
    wprT = np.ascontiguousarray(W_proj.T.astype(np.float32))
    return {
        "wfc_s": prep_lhsT_stream(wfcT, 512),
        "wpr_s": prep_lhsT_stream(wprT, 128),
    }


def host_consts():
    import ml_dtypes
    p = np.arange(128)[:, None]
    j = np.arange(512)[None, :]
    masks = np.concatenate(
        [(j >= p + 128 * r).astype(np.float32) for r in range(4)], axis=1)
    return {
        "cc_ident": np.eye(128, dtype=np.float32).astype(ml_dtypes.bfloat16),
        "cc_ones": np.ones((128, 128), dtype=np.float32).astype(ml_dtypes.bfloat16),
        "cc_onesf": np.ones((128, 128), dtype=np.float32),
        "cc_sel8": np.concatenate(
            [np.tile((np.arange(8) == e).astype(np.float32)[:, None], (1, 128))
             for e in range(8)], axis=1).astype(ml_dtypes.bfloat16),
        "cc_masks": masks.astype(ml_dtypes.bfloat16),
    }





NBF = 44672
NF = 4096


def _bind_phase(B, kind):
    bf = B.ARNBF
    f3 = B.ARNF
    o = [0]
    p = [0]

    def tb(cols, rows=128, r0=0):
        ap = bf[r0:r0 + rows, o[0]:o[0] + cols]
        o[0] += cols
        return ap

    def tf(cols, rows=128, r0=0):
        ap = f3[r0:r0 + rows, p[0]:p[0] + cols]
        p[0] += cols
        return ap

    if kind.startswith("attn"):
        B["QTP"] = tb(2048)
        B["KTP"] = tb(2048)
        B["VAUG"] = tb(2080)
        ot = tb(16384)
        B["OT"] = ot
        B["YF"] = ot
        B["PT"] = tb(1024)
        B["QSQ"] = tb(2048)
        B["aw_wbuf0"] = tb(3072)
        B["aw_wbuf1"] = tb(3072)
        B["MASKS"] = tb(2048)
        B["RCOS"] = tb(2048)
        B["RSIN"] = tb(2048)
        B["RSH"] = B["QSQ"]
        B["BVRB"] = tb(1024, rows=1)
        B["OUTT"] = tb(4096)
        B["aw_stage0"] = tf(2048)
        B["LNSQ"] = tf(1024)
        B["LNDISC"] = B["LNSQ"]
        izb = tf(512)
        B["IZB"] = izb
        B["IZ"] = B.ARNF[64:65, p[0] - 512:p[0]]
        B["BB"] = tf(2)
        B["B4"] = B.ARNF[0:1, p[0]:p[0] + 16]
        p[0] += 16
    elif kind == "l0ffn":
        B["H"] = tb(16384)
        B["w_wbuf0"] = tb(4096)
        B["w_wbuf1"] = tb(4096)
        B["ERALL"] = tb(8192, rows=1)
        B["TR"] = tb(2048, rows=1)
        B["UR"] = tb(2048, rows=1)
        B["LORA_T"] = tb(1024, rows=16)
        B["OUTT"] = tb(4096)
        B["w_stage0"] = tf(2048)
        B["LNSQ"] = tf(1024)
        B["LNDISC"] = B["LNSQ"]
    elif kind == "l1ffn":
        B["H"] = tb(16384)
        B["w_wbuf0"] = tb(4096)
        B["w_wbuf1"] = tb(4096)
        B["RELU"] = tb(1024)
        B["SPK"] = tb(512)
        B["OUTT"] = tb(4096)
        B["w_stage0"] = tf(2048)
        B["Y0"] = tf(512)
        B["MEM"] = tf(512)
        B["LNSQ"] = tf(1024)
        B["LNDISC"] = B["LNSQ"]
    elif kind == "l2moe":
        B["ACC"] = tb(16384)
        B["H"] = tb(16384)
        B["w_wbuf0"] = tb(4096)
        B["w_wbuf1"] = tb(4096)
        B["WFM"] = tb(2048)
        B["WTOK"] = tf(128)
        B["SELE"] = tb(1024, rows=8)
        B["WTOKB"] = tb(128)
        B["WBC"] = tb(512)
        B["w_stage0"] = tf(2048)
        B["LNSQ"] = tf(1024)
        B["LNDISC"] = B["LNSQ"]
    elif kind == "l3ffn":
        B["H3"] = tb(11264)
        yf = tb(16384)
        B["YF"] = yf
        B["HTK"] = yf
        B["w_wbuf0"] = tb(2816)
        B["w_wbuf1"] = tb(2816)
        B["QS1"] = tb(1408)
        B["QS2"] = tb(1408)
        B["SGN"] = tb(1408)
        B["OUTT"] = tb(4096)
        B["XN"] = tf(1408)
        B["QT"] = tf(1408)
        B["QSC"] = tf(176)
        B["QRS"] = tf(176)
        B["LNDISC"] = B["XN"]
    assert o[0] <= NBF, (kind, o[0])
    assert p[0] <= NF, (kind, p[0])


def emit_join(S):
    i = S.vector.memset(B_JOIN_DUMMY_V(S), 0.0)
    S.tinc(i, "dve", 1)
    i = S.scalar.memzero(B_JOIN_DUMMY_S(S))
    S.tinc(i, "act", 1)
    i = S.gpsimd.memset(B_JOIN_DUMMY_V(S), 0.0)
    S.tinc(i, "gp", 1)
    for w in ("sync", "tensor", "vector", "scalar", "gpsimd"):
        for sem in ("pe", "dve", "act", "gp", "wdma", "gdma"):
            S.wait(w, sem, S.cnt[sem])


def B_JOIN_DUMMY_V(S):
    return S._join_dummy


def B_JOIN_DUMMY_S(S):
    return S._join_dummy


def build_fused_program():
    import contextlib
    nc = bass.Bass("TRN2", target_bir_lowering=False)
    D = {}

    def inp(name, shape, dt=F32):
        D[name] = nc.dram_tensor(name, shape, dt, kind="ExternalInput").ap()
        return D[name]

    x = inp("x", [SEQ, DM])
    inp("cc_ident", [128, 128], BF16)
    inp("cc_ones", [128, 128], BF16)
    inp("cc_onesf", [128, 128], F32)
    inp("cc_masks", [128, 2048], BF16)
    inp("cc_rcos", [128, SEQ], BF16)
    inp("cc_rsin", [128, SEQ], BF16)
    inp("cc_sel8", [8, 1024], BF16)
    inp("l0f_w1s", [8, 128, 8 * 512])
    inp("l0f_w2s", [8, 128, 32 * 128])
    inp("l0f_b1c", [128, 32])
    inp("l0f_b2c", [128, 8])
    inp("l0f_scale", [1, 1])
    inp("a0_qkvs", [8, 128, 3072])
    inp("a0_wos", [8, 128, 1024])
    inp("l1f_wfcs", [8, 128, 8 * 512])
    inp("l1f_wprs", [8, 128, 32 * 128])
    inp("a1_qkvs", [8, 128, 3072])
    inp("a1_wos", [8, 128, 1024])
    inp("a1_bqc", [128, 8])
    inp("a1_bkc", [128, 8])
    inp("a1_bvr", [1, 1024])
    inp("a1_boc", [128, 8])
    inp("l2m_wgc", [128, 64])
    inp("l2m_bgr", [1, 8])
    inp("l2m_w1s", [8, 8, 128, 4096])
    inp("l2m_w2s", [8, 8, 128, 4096])
    inp("l2m_b1c", [8, 128, 32])
    inp("l2m_b2c", [8, 128, 8])
    inp("a2_qkvs", [8, 128, 3072])
    inp("a2_wos", [8, 128, 1024])
    inp("a3_qkvs", [8, 128, 3072])
    inp("a3_wos", [8, 128, 1024])
    inp("a3_bqkvc", [128, 24])
    inp("a3_boc", [128, 8])
    inp("a3_lnwc", [128, 8])
    inp("a3_lnbc", [128, 8])
    inp("l3f_wg", [2816, 1024])
    inp("l3f_wu", [2816, 1024])
    inp("l3f_wd", [1024, 2816])
    g_s = nc.dram_tensor("g_scr", [22, 128, 1024], BF16).ap()
    u_s = nc.dram_tensor("u_scr", [22, 128, 1024], BF16).ap()
    d_s = nc.dram_tensor("d_scr", [8, 128, 2816], BF16).ap()
    w1b_s = nc.dram_tensor("w1b_scr", [8, 8, 128, 4096], BF16).ap()
    w2b_s = nc.dram_tensor("w2b_scr", [8, 8, 128, 4096], BF16).ap()
    xo = nc.dram_tensor("x_out", [SEQ, DM], F32, kind="ExternalOutput").ap()

    with contextlib.ExitStack() as stack:
        B = NS()

        def sb(name, shape, dt):
            B[name] = stack.enter_context(nc.sbuf_tensor(name, shape, dt))

        def psum(name, shape, dt):
            B[name] = stack.enter_context(nc.psum_tensor(name, shape, dt))

        sb("X", [128, NT * DM], F32)
        sb("YT", [128, ND * SEQ], BF16)
        sb("YROLL", [128, 2 * DM], BF16)
        for nm in ("S1", "S2", "MU", "MUSQ", "VAR", "SD", "RSTD"):
            sb(nm, [128, NT], F32)
        sb("IDENT", [128, 128], BF16)
        sb("ONES", [128, 128], BF16)
        sb("ONESF", [128, 128], F32)
        sb("B1C", [128, 32], F32)
        sb("B2C", [128, 8], F32)
        sb("SCALE", [1, 1], F32)
        sb("BQC", [128, 8], F32)
        sb("BKC", [128, 8], F32)
        sb("BOC", [128, 8], F32)
        sb("BOC3", [128, 8], F32)
        sb("WGC", [128, 64], F32)
        sb("WGCB", [128, 64], BF16)
        sb("BGR", [1, 8], F32)
        sb("MB1C", [128, 32], F32)
        sb("MB2C", [128, 8], F32)
        sb("BQKVC", [128, 24], F32)
        sb("LNWC", [128, 8], F32)
        sb("LNBC", [128, 8], F32)
        sb("BQE", [128, 1], F32)
        sb("BKE", [128, 1], F32)
        sb("VBC", [128, 1], BF16)
        sb("BVRD", [1, 128], BF16)
        sb("ARNBF", [128, NBF], BF16)
        sb("ARNF", [128, NF], F32)
        sb("JD", [128, 8], F32)
        for i in range(4):
            psum(f"PSF{i}", [128, 512], F32)
        psum("PSB0", [128, 1024], BF16)
        psum("PSB1", [128, 1024], BF16)
        sems = {n: stack.enter_context(nc.semaphore(n)) for n in SEM_NAMES}
        block = stack.enter_context(nc.Block())

        def schedule(S):
            S._join_dummy = B.JD[0:1, 0:1]
            B["psf_pool"] = PsumBank([B[f"PSF{i}"] for i in range(4)])
            B["psb_pool"] = PsumBank([B["PSB0"], B["PSB1"]])
            consts = [
                (D["cc_ident"][:, :], B.IDENT[:, :]),
                (D["cc_ones"][:, :], B.ONES[:, :]),
                (D["cc_onesf"][:, :], B.ONESF[:, :]),
                (D["l0f_b1c"][:, :], B.B1C[:, :]),
                (D["l0f_b2c"][:, :], B.B2C[:, :]),
                (D["l0f_scale"][:, :], B.SCALE[0:1, 0:1]),
                (D["a1_bqc"][:, :], B.BQC[:, :]),
                (D["a1_bkc"][:, :], B.BKC[:, :]),
                (D["a1_boc"][:, :], B.BOC[:, :]),
                (D["l2m_wgc"][:, :], B.WGC[:, :]),
                (D["l2m_bgr"][:, :], B.BGR[0:1, 0:8]),
                (D["a3_bqkvc"][:, :], B.BQKVC[:, :]),
                (D["a3_boc"][:, :], B.BOC3[:, :]),
                (D["a3_lnwc"][:, :], B.LNWC[:, :]),
                (D["a3_lnbc"][:, :], B.LNBC[:, :]),
            ]
            cm = emit_load_consts(S, B, consts)
            xr = emit_load_x(S, B, x)
            for e in ("vector", "scalar", "tensor", "gpsimd"):
                S.wait(e, *cm)
                S.wait(e, *xr)

            def lnw_cast(S2, wbuf, stage, off, w):
                last = None
                for kk in range(w // 128):
                    kt = (off // 128) + kk
                    last = S2.vector.tensor_scalar(
                        wbuf[:, ts_(kt, 128)], stage[:, ts_(kk, 128)],
                        B.LNWC[:, (kt % 8):(kt % 8) + 1], None, op0=ALU.mult)
                return last

            x_m = xr
            moe_cached = [False, False]
            l3_prepped = [False]
            seq = FUSED_SEQ
            for kind in seq:
                _bind_phase(B, kind)
                pl = []
                if kind.startswith("attn"):
                    pl.append((D["cc_masks"][:, :], B.MASKS[:, :]))
                    if kind == "attn_l2":
                        pl.append((D["cc_rcos"][:, :], B.RCOS[:, :]))
                        pl.append((D["cc_rsin"][:, :], B.RSIN[:, :]))
                    if kind == "attn_l1":
                        pl.append((D["a1_bvr"][0:1, :], B.BVRB[0:1, :]))
                if kind == "l2moe":
                    pl.append((D["cc_sel8"][:, :], B.SELE[0:8, :]))
                if pl:
                    pm = emit_load_consts(S, B, pl)
                    for e in ("vector", "scalar", "tensor", "gpsimd"):
                        S.wait(e, *pm)
                if kind == "l0ffn":
                    Wd = NS(w1_s=D["l0f_w1s"], w2_s=D["l0f_w2s"],
                            b1_col=B.B1C, b2_col=B.B2C)
                    x_m = emit_l0ffn(S, B, Wd, x_m)
                elif kind == "l1ffn":
                    Wd = NS(wfc_s=D["l1f_wfcs"], wpr_s=D["l1f_wprs"])
                    x_m = emit_l1ffn(S, B, Wd, x_m)
                elif kind == "l2moe":
                    Wd = NS(w1_s=D["l2m_w1s"], w2_s=D["l2m_w2s"],
                            w1b_s=w1b_s, w2b_s=w2b_s,
                            b1_cols=D["l2m_b1c"], b2_cols=D["l2m_b2c"])
                    x_m = emit_l2moe(S, B, Wd, x_m, cached=moe_cached)
                elif kind == "l3ffn":
                    Wd = NS(wg_nat=D["l3f_wg"], wu_nat=D["l3f_wu"],
                            wd_nat=D["l3f_wd"], g_s=g_s, u_s=u_s, d_s=d_s)
                    x_m = emit_l3ffn(S, B, Wd, x_m, do_prep=not l3_prepped[0])
                    l3_prepped[0] = True
                elif kind == "attn_l0":
                    cfg = NS(causal=True)
                    Wd = NS(qkv_s=D["a0_qkvs"], wo_s=D["a0_wos"])
                    x_m = emit_attn(S, B, Wd, x_m, cfg)
                elif kind == "attn_l1":
                    cfg = NS(causal=True, qk_bias=True, v_bias_row="BVRB16",
                             o_bias=True)
                    Wd = NS(qkv_s=D["a1_qkvs"], wo_s=D["a1_wos"],
                            bq_col=B.BQC, bk_col=B.BKC, bo_col=B.BOC)
                    x_m = emit_attn(S, B, Wd, x_m, cfg)
                elif kind == "attn_l2":
                    cfg = NS(causal=True, rope=True,
                             rope_mask=[i + 1 if i % 2 == 0 else i - 1
                                        for i in range(32)])
                    Wd = NS(qkv_s=D["a2_qkvs"], wo_s=D["a2_wos"])
                    x_m = emit_attn(S, B, Wd, x_m, cfg)
                elif kind == "attn_l3":
                    cfg = NS(causal=False, inner_residual=True,
                             dev_qkv_bias=True, v_bias_row="BVRD",
                             o_bias=True, cast_fn=lnw_cast)
                    Wd = NS(qkv_s=D["a3_qkvs"], wo_s=D["a3_wos"],
                            bqkv_col=B.BQKVC, bo_col=B.BOC3)
                    x_m = emit_attn(S, B, Wd, x_m, cfg)
                emit_join(S)
            emit_store_x(S, B, xo, x_m)

        _finish_program(nc, B, sems, block, schedule)
    return nc


FUSED = True
FUSED_SEQ = ["l0ffn", "attn_l0", "l0ffn",
             "l1ffn", "attn_l1", "l1ffn",
             "l2moe", "attn_l2", "l2moe",
             "l3ffn", "attn_l3", "l3ffn"]

_PROG_CACHE = {}


def _get_prog(kind):
    if kind in _PROG_CACHE:
        return _PROG_CACHE[kind]
    if kind == "fused":
        nc = build_fused_program()
        _PROG_CACHE[kind] = nc
        return nc
    if kind == "l0ffn":
        nc = build_l0ffn_test_program()
    elif kind == "l1ffn":
        nc = build_l1ffn_test_program()
    elif kind == "l3ffn":
        nc = build_l3ffn_test_program()
    elif kind == "l2moe":
        nc = build_l2moe_test_program()
    elif kind == "attn_l0":
        nc = build_attn_test_program(NS(causal=True))
    elif kind == "attn_l1":
        nc = build_attn_test_program(NS(causal=True, qk_bias=True,
                                        v_bias_row="bv_row", o_bias=True))
    elif kind == "attn_l2":
        nc = build_attn_test_program(NS(
            causal=True, rope=True,
            rope_mask=[i + 1 if i % 2 == 0 else i - 1 for i in range(32)]))
    elif kind == "attn_l3":
        nc = build_l3attn_test_program()
    else:
        raise KeyError(kind)
    _PROG_CACHE[kind] = nc
    return nc


_CONST_KEYS = {
    "l0ffn": ("cc_ident",),
    "l1ffn": ("cc_ident",),
    "l3ffn": ("cc_ident",),
    "l2moe": ("cc_ident", "cc_ones", "cc_onesf", "cc_sel8"),
    "attn_l0": ("cc_ident", "cc_ones", "cc_onesf", "cc_masks"),
    "attn_l1": ("cc_ident", "cc_ones", "cc_onesf", "cc_masks"),
    "attn_l2": ("cc_ident", "cc_ones", "cc_onesf", "cc_masks",
                "cc_rcos", "cc_rsin"),
    "attn_l3": ("cc_ident", "cc_ones", "cc_onesf", "cc_masks"),
}


def kernel(**inp):
    consts = host_consts()
    consts.update(host_rope_tables())

    f32 = np.float32
    T = lambda a: np.ascontiguousarray(np.asarray(a, dtype=f32).T)
    A = lambda a: np.ascontiguousarray(np.asarray(a, dtype=f32))

    wmaps = {}
    wmaps["l0ffn"] = host_prep_l0ffn(A(inp["L0_ffn_W1"]), A(inp["L0_ffn_b1"]),
                                     A(inp["L0_ffn_W2"]), A(inp["L0_ffn_b2"]),
                                     np.asarray(inp["L0_ffn_scale"], dtype=f32))
    wmaps["attn_l0"] = host_prep_attn(T(inp["L0_attn_Wq"]), T(inp["L0_attn_Wk"]),
                                      T(inp["L0_attn_Wv"]), T(inp["L0_attn_Wo"]))
    wmaps["l1ffn"] = host_prep_l1ffn(A(inp["L1_ffn_W_fc"]), A(inp["L1_ffn_W_proj"]))
    wmaps["attn_l1"] = host_prep_attn(
        A(inp["L1_attn_wq"]), A(inp["L1_attn_wk"]), A(inp["L1_attn_wv"]),
        A(inp["L1_attn_wo"]), bq=A(inp["L1_attn_bq"]), bk=A(inp["L1_attn_bk"]),
        bv=A(inp["L1_attn_bv"]), bo=A(inp["L1_attn_bo"]))
    wmaps["l2moe"] = host_prep_l2moe(A(inp["L2_ffn_Wg"]), A(inp["L2_ffn_bg"]),
                                     A(inp["L2_ffn_W1"]), A(inp["L2_ffn_b1"]),
                                     A(inp["L2_ffn_W2"]), A(inp["L2_ffn_b2"]))
    wmaps["attn_l2"] = host_prep_attn(T(inp["L2_attn_wq"]), T(inp["L2_attn_wk"]),
                                      T(inp["L2_attn_wv"]), T(inp["L2_attn_wo"]))
    wmaps["attn_l3"] = host_prep_l3attn(A(inp["L3_attn_W_qkv"]),
                                        A(inp["L3_attn_b_qkv"]),
                                        A(inp["L3_attn_W_proj"]),
                                        A(inp["L3_attn_b_proj"]),
                                        A(inp["L3_attn_ln_w"]),
                                        A(inp["L3_attn_ln_b"]))
    wmaps["l3ffn"] = {"wg_nat": A(inp["L3_ffn_w_gate"]),
                      "wu_nat": A(inp["L3_ffn_w_up"]),
                      "wd_nat": A(inp["L3_ffn_w_down"])}

    x0 = np.asarray(inp["L0_attn_x"], dtype=f32)
    xs = [np.ascontiguousarray(x0[b]) for b in range(NCORES)]

    def run(kind):
        nc = _get_prog(kind)
        cst = {k: consts[k] for k in _CONST_KEYS[kind]}
        in_maps = [dict(x=xs[b], **wmaps[kind], **cst) for b in range(NCORES)]
        res = run_bass_kernel_spmd(nc, in_maps, core_ids=list(range(NCORES)))
        for b in range(NCORES):
            xs[b] = res.results[b]["x_out"]

    if FUSED:
        nc = _get_prog("fused")
        im = {}
        im.update({k: consts[k] for k in ("cc_ident", "cc_ones", "cc_onesf",
                                          "cc_masks", "cc_rcos", "cc_rsin",
                                          "cc_sel8")})
        m = wmaps["l0ffn"]
        im["l0f_w1s"] = m["w1_s"]; im["l0f_w2s"] = m["w2_s"]
        im["l0f_b1c"] = m["b1_col"]; im["l0f_b2c"] = m["b2_col"]
        im["l0f_scale"] = m["scale_in"]
        m = wmaps["attn_l0"]
        im["a0_qkvs"] = m["qkv_s"]; im["a0_wos"] = m["wo_s"]
        m = wmaps["l1ffn"]
        im["l1f_wfcs"] = m["wfc_s"]; im["l1f_wprs"] = m["wpr_s"]
        m = wmaps["attn_l1"]
        im["a1_qkvs"] = m["qkv_s"]; im["a1_wos"] = m["wo_s"]
        im["a1_bqc"] = m["bq_col"]; im["a1_bkc"] = m["bk_col"]
        im["a1_bvr"] = m["bv_row"]; im["a1_boc"] = m["bo_col"]
        m = wmaps["l2moe"]
        im["l2m_wgc"] = m["wg_cols"]; im["l2m_bgr"] = m["bg_row"]
        im["l2m_w1s"] = m["w1_s"]; im["l2m_w2s"] = m["w2_s"]
        im["l2m_b1c"] = m["b1_cols"]; im["l2m_b2c"] = m["b2_cols"]
        m = wmaps["attn_l2"]
        im["a2_qkvs"] = m["qkv_s"]; im["a2_wos"] = m["wo_s"]
        m = wmaps["attn_l3"]
        im["a3_qkvs"] = m["qkv_s"]; im["a3_wos"] = m["wo_s"]
        im["a3_bqkvc"] = m["bqkv_col"]; im["a3_boc"] = m["bo_col"]
        im["a3_lnwc"] = m["lnw_col"]; im["a3_lnbc"] = m["lnb_col"]
        m = wmaps["l3ffn"]
        im["l3f_wg"] = m["wg_nat"]; im["l3f_wu"] = m["wu_nat"]
        im["l3f_wd"] = m["wd_nat"]
        in_maps = [dict(x=xs[b], **im) for b in range(NCORES)]
        res = run_bass_kernel_spmd(nc, in_maps, core_ids=list(range(NCORES)))
        return np.stack([res.results[b]["x_out"] for b in range(NCORES)]
                        ).astype(np.float32)

    seq = ["l0ffn", "attn_l0", "l0ffn",
           "l1ffn", "attn_l1", "l1ffn",
           "l2moe", "attn_l2", "l2moe",
           "l3ffn", "attn_l3", "l3ffn"]
    for kind in seq:
        run(kind)
    return np.stack(xs).astype(np.float32)


# version 04
# speedup vs baseline: 1.2588x; 1.2588x over previous
import math
import numpy as np
import concourse.bass as bass
import concourse.mybir as mybir
from concourse.bass_utils import run_bass_kernel_spmd

F32 = mybir.dt.float32
BF16 = mybir.dt.bfloat16
AF = mybir.ActivationFunctionType
ALU = mybir.AluOpType
AX = mybir.AxisListType

NCORES = 8
SEQ = 2048
DM = 1024
NT = SEQ // 128
ND = DM // 128
NHEAD = 16
DH = 64
LN_EPS = 1e-5
MAGIC = 12582912.0

ENG_SEM = {"tensor": "pe", "vector": "dve", "scalar": "act", "gpsimd": "gp"}
SEM_NAMES = ("wdma", "gdma", "adma", "pe", "dve", "act", "gp")


def ts_(i, n):
    return slice(i * n, (i + 1) * n)


def ots(base, i, n):
    return slice(base + i * n, base + (i + 1) * n)


class NS(dict):
    def __getattr__(self, k):
        return self[k]

    def __setattr__(self, k, v):
        self[k] = v


class _DummyIns:
    def then_inc(self, *a, **k):
        return self


class _DummyEng:
    def __getattr__(self, name):
        def f(*a, **k):
            return _DummyIns()
        return f


class Sched:

    def __init__(self, nc, active, eng, sems):
        self.nc = nc
        self.active = active
        self._eng = eng
        self.sems = sems
        self.cnt = {k: 0 for k in sems}
        d = _DummyEng()
        self.sync = eng if active == "sync" else d
        self.tensor = eng if active == "tensor" else d
        self.vector = eng if active == "vector" else d
        self.scalar = eng if active == "scalar" else d
        self.gpsimd = eng if active == "gpsimd" else d

    def on(self, engname):
        return self.active == engname

    def eng(self, engname):
        return getattr(self, engname)

    def tinc(self, ins, sem, k=1):
        ins.then_inc(self.sems[sem], k)
        self.cnt[sem] += k
        return self.cnt[sem]

    def wait(self, engname, sem, val=None):
        if val is None:
            val = self.cnt[sem]
        if val <= 0:
            return
        if self.active == engname:
            self._eng.wait_ge(self.sems[sem], val)

    def sf(self, engname, ins):
        sem = ENG_SEM[engname]
        m = self.tinc(ins, sem, 1)
        self.wait(engname, sem, m)
        return m



class WStream:

    def __init__(self, B, tag, cast_eng="vector", single_stage=False):
        self.B = B
        self.tag = tag
        self.ce = cast_eng
        self.i = 0
        self.single_stage = single_stage
        self.stage_free = [0, 0]
        self.stage_guard = None
        self.wbuf_free = [0, 0]
        self.wbuf_extra = [None, None]
        self.ready = [0, 0]
        self.last_dma = 0
        self.wb_last = 0

    def load(self, S, dram_ap, cols, cast_fn=None, stage_hook=None):
        B = self.B
        sl = self.i % 2
        self.i += 1
        S.wait(self.ce, "pe", self.wbuf_free[sl])
        if self.wbuf_extra[sl] is not None:
            S.wait(self.ce, *self.wbuf_extra[sl])
            self.wbuf_extra[sl] = None
        stage = B[f"{self.tag}_stage0"]
        scols = stage.shape[1] if hasattr(stage, "shape") else 2048
        pieces = []
        off = 0
        while off < cols:
            w = min(scols, cols - off)
            pieces.append((off, w))
            off += w
        mr = 0
        for off, w in pieces:
            S.wait("sync", ENG_SEM[self.ce], self.stage_free[0])
            if self.stage_guard is not None:
                S.wait("sync", *self.stage_guard)
                self.stage_guard = None
            ins = S.sync.dma_start(stage[:, 0:w], dram_ap[:, off:off + w])
            mdma = S.tinc(ins, "wdma", 16)
            self.last_dma = mdma
            S.wait(self.ce, "wdma", mdma)
            if stage_hook is not None:
                mh = stage_hook(S, stage, off, w, mdma)
                self.stage_guard = ("pe", mh)
            if cast_fn is not None:
                ins = cast_fn(S, B[f"{self.tag}_wbuf{sl}"], stage, off, w)
            else:
                ins = S.eng(self.ce).tensor_copy(
                    B[f"{self.tag}_wbuf{sl}"][:, off:off + w], stage[:, 0:w])
            mr = S.tinc(ins, ENG_SEM[self.ce], 1)
            self.stage_free[0] = mr
        self.ready[sl] = mr
        return sl, mr

    def load_bf16(self, S, dram_ap, cols):
        sl = self.i % 2
        self.i += 1
        S.wait("sync", "pe", self.wbuf_free[sl])
        S.wait("sync", "gdma", self.wb_last)
        if self.wbuf_extra[sl] is not None:
            S.wait("sync", *self.wbuf_extra[sl])
            self.wbuf_extra[sl] = None
        ins = S.sync.dma_start(self.B[f"{self.tag}_wbuf{sl}"][:, 0:cols], dram_ap)
        m = S.tinc(ins, "wdma", 16)
        self.last_dma = m
        return sl, m

    def wbuf(self, sl):
        return self.B[f"{self.tag}_wbuf{sl}"]

    def release(self, S, sl, pe_milestone):
        self.wbuf_free[sl] = pe_milestone



def ws_load_cached(S, B, ws, f32_ap, bf_ap, cols, use_bf):
    if use_bf:
        sl, m = ws.load_bf16(S, bf_ap, cols)
        S.wait("tensor", "wdma", m)
    else:
        sl, m = ws.load(S, f32_ap, cols)
        S.wait("tensor", ENG_SEM[ws.ce], m)
        S.wait("gpsimd", ENG_SEM[ws.ce], m)
        iw = S.gpsimd.dma_start(bf_ap, ws.wbuf(sl)[:, 0:cols])
        mwb = S.tinc(iw, "gdma", 16)
        ws.wbuf_extra[sl] = ("gdma", mwb)
        ws.wb_last = mwb
    return sl


def emit_load_x(S, B, x_hbm):
    for t in range(NT):
        ins = S.sync.dma_start(B.X[:, ts_(t, DM)], x_hbm[ts_(t, 128), :])
        S.tinc(ins, "wdma", 16)
    return ("wdma", S.cnt["wdma"])


def emit_store_x(S, B, out_hbm, x_ready):
    sem, m = x_ready
    S.wait("gpsimd", sem, m)
    base = S.cnt["gdma"]
    for t in range(NT):
        ins = S.gpsimd.dma_start(out_hbm[ts_(t, 128), :], B.X[:, ts_(t, DM)])
        S.tinc(ins, "gdma", 16)
    S.wait("gpsimd", "gdma", base + 16 * NT)


class PsumBank:

    def __init__(self, handles):
        self.h = handles
        self.free = [("dve", 0)] * len(handles)
        self.i = 0

    def next(self, S, waiter="tensor"):
        k = self.i % len(self.h)
        self.i += 1
        sem, m = self.free[k]
        S.wait(waiter, sem, m)
        return k, self.h[k]

    def done(self, k, sem, m):
        self.free[k] = (sem, m)


def emit_ln(S, B, x_ready, full_y=False, yf=None, src=None, make_yt=True,
            fp32_add_to_x=False):
    if src is None:
        src = B.X
    sem, m0 = x_ready
    S.wait("vector", sem, m0)
    S.wait("scalar", sem, m0)
    last_v = last_s = None
    for t in range(NT):
        last_v = S.vector.reduce_sum(B.S1[:, t:t + 1], src[:, ts_(t, DM)], axis=AX.X)
        last_s = S.scalar.activation(B.LNDISC[:, 0:DM], src[:, ts_(t, DM)], AF.Square,
                                     accum_out=B.S2[:, t:t + 1])
    S.sf("vector", last_v)
    ms = S.tinc(last_s, "act", 1)
    S.wait("vector", "act", ms)
    i1 = S.vector.tensor_scalar(B.MU[:, 0:NT], B.S1[:, 0:NT], 1.0 / DM, None, op0=ALU.mult)
    i2 = S.vector.tensor_scalar(B.VAR[:, 0:NT], B.S2[:, 0:NT], 1.0 / DM, None, op0=ALU.mult)
    S.sf("vector", i2)
    i3 = S.vector.tensor_tensor(B.MUSQ[:, 0:NT], B.MU[:, 0:NT], B.MU[:, 0:NT], op=ALU.mult)
    S.sf("vector", i3)
    i4 = S.vector.scalar_tensor_tensor(B.VAR[:, 0:NT], B.VAR[:, 0:NT], LN_EPS,
                                       B.MUSQ[:, 0:NT], op0=ALU.add, op1=ALU.subtract)
    m4 = S.tinc(i4, "dve", 1)
    S.wait("scalar", "dve", m4)
    i5 = S.scalar.activation(B.SD[:, 0:NT], B.VAR[:, 0:NT], AF.Sqrt)
    m5 = S.tinc(i5, "act", 1)
    S.wait("vector", "act", m5)
    i6 = S.vector.reciprocal(B.RSTD[:, 0:NT], B.SD[:, 0:NT])
    S.sf("vector", i6)

    yroll_pe = [0, 0]
    for t in range(NT):
        if full_y:
            ysrc = yf[:, ts_(t, DM)]
        else:
            S.wait("vector", "pe", yroll_pe[t % 2])
            ysrc = B.YROLL[:, ts_(t % 2, DM)]
        if fp32_add_to_x:
            iz = S.vector.tensor_scalar(B.LNDISC[:, 0:DM], src[:, ts_(t, DM)],
                                        B.MU[:, t:t + 1], B.RSTD[:, t:t + 1],
                                        op0=ALU.subtract, op1=ALU.mult)
            S.sf("vector", iz)
            ic = S.vector.tensor_copy(ysrc, B.LNDISC[:, 0:DM])
            S.tinc(ic, "dve", 1)
            ia = S.vector.tensor_tensor(B.X[:, ts_(t, DM)], B.X[:, ts_(t, DM)],
                                        B.LNDISC[:, 0:DM], op=ALU.add)
            S.sf("vector", ia)
            my = S.cnt["dve"]
            continue
        iy = S.vector.tensor_scalar(ysrc, src[:, ts_(t, DM)],
                                    B.MU[:, t:t + 1], B.RSTD[:, t:t + 1],
                                    op0=ALU.subtract, op1=ALU.mult)
        my = S.tinc(iy, "dve", 1)
        if make_yt:
            S.wait("tensor", "dve", my)
            mlast = emit_transpose_tile_to_feat(S, B, ysrc, B.YT, t)
            if not full_y:
                yroll_pe[t % 2] = mlast
    return NS(yt=("dve", S.cnt["dve"]))


def emit_transpose_tile_to_feat(S, B, src_tile_ap, yt_handle, t):
    yt3 = yt_handle[:, :].rearrange("p (d q) -> p d q", q=SEQ)
    mlast = 0
    for pack in range(2):
        kb, ps = B.psb_pool.next(S, waiter="tensor")
        lastmm = None
        for j in range(4):
            d = pack * 4 + j
            lastmm = S.tensor.transpose(
                ps[:, ts_(j, 128)],
                src_tile_ap[:, ts_(d, 128)],
                B.IDENT[:, 0:128])
        mp = S.tinc(lastmm, "pe", 1)
        mlast = mp
        S.wait("vector", "pe", mp)
        src3 = ps[:, 0:512].rearrange("p (b f) -> p b f", f=128)
        dst3 = yt3[:, pack * 4:(pack + 1) * 4, t * 128:(t + 1) * 128]
        ev = S.vector.tensor_copy(dst3, src3)
        me = S.tinc(ev, "dve", 1)
        B.psb_pool.done(kb, "dve", me)
    return mlast


def emit_out_transpose_add(S, B, outt_handle, c, scale_note=None):
    o3 = outt_handle[:, 0:8 * 512].rearrange("p (d q) -> p d q", q=512)
    mlast = 0
    for tsub in range(4):
        t = 4 * c + tsub
        for pack in range(2):
            kb, ps = B.psb_pool.next(S, waiter="tensor")
            lastmm = None
            for j in range(4):
                d = pack * 4 + j
                lastmm = S.tensor.transpose(
                    ps[:, ts_(j, 128)],
                    o3[:, d, ts_(tsub, 128)],
                    B.IDENT[:, 0:128])
            mp = S.tinc(lastmm, "pe", 1)
            S.wait("vector", "pe", mp)
            dst = B.X[:, t * DM + pack * 512: t * DM + (pack + 1) * 512]
            ev = S.vector.tensor_tensor(dst, dst, ps[:, 0:512], op=ALU.add)
            me = S.tinc(ev, "dve", 1)
            B.psb_pool.done(kb, "dve", me)
            mlast = me
    return ("dve", mlast)



def emit_l1ffn(S, B, W, x_ready, cached=None):
    r = emit_ln(S, B, x_ready)
    yt_sem, yt_m = r.yt
    ws = WStream(B, "w", cast_eng="vector", single_stage=True)
    S.wait("tensor", yt_sem, yt_m)
    has_cache = W.get("wfcb_s") is not None

    x_m = 0
    relu_free = [0, 0]
    y0_free = [0, 0]
    outt_pe = 0
    for c in range(4):
        for j in range(8):
            if has_cache:
                use_bf = cached[0] or c > 0
                sl = ws_load_cached(S, B, ws, W.wfc_s[j, :, :],
                                    W.wfcb_s[j, :, :], 8 * 512, use_bf)
            else:
                sl, mr = ws.load(S, W.wfc_s[j, :, :], 8 * 512)
                S.wait("tensor", ENG_SEM[ws.ce], mr)
            wb = ws.wbuf(sl)
            lastmm_for_slot = None
            for sub in range(4):
                fblk = 4 * j + sub
                kb, ps = B.psf_pool.next(S, waiter="tensor")
                lastmm = None
                for kt in range(ND):
                    lhsT = wb[:, kt * 512 + sub * 128: kt * 512 + (sub + 1) * 128]
                    rhs = B.YT[:, kt * SEQ + c * 512: kt * SEQ + (c + 1) * 512]
                    lastmm = S.tensor.matmul(ps[:, 0:512], lhsT, rhs,
                                             start=(kt == 0), stop=(kt == ND - 1))
                mp = S.tinc(lastmm, "pe", 1)
                lastmm_for_slot = mp
                S.wait("scalar", "pe", mp)
                S.wait("scalar", "dve", relu_free[sub % 2])
                ir = S.scalar.activation(B.RELU[:, ts_(sub % 2, 512)], ps[:, 0:512], AF.Relu)
                ma = S.tinc(ir, "act", 1)
                B.psf_pool.done(kb, "act", ma)
                S.wait("vector", "act", ma)
                iq = S.vector.tensor_mul(B.H[:, ts_(fblk, 512)],
                                         B.RELU[:, ts_(sub % 2, 512)],
                                         B.RELU[:, ts_(sub % 2, 512)])
                relu_free[sub % 2] = S.tinc(iq, "dve", 1)
            ws.release(S, sl, lastmm_for_slot)
        h_done = S.cnt["dve"]

        S.wait("tensor", "dve", h_done)
        for j in range(8):
            if has_cache:
                use_bf = cached[0] or c > 0
                sl = ws_load_cached(S, B, ws, W.wpr_s[j, :, :],
                                    W.wprb_s[j, :, :], 32 * 128, use_bf)
            else:
                sl, mr = ws.load(S, W.wpr_s[j, :, :], 32 * 128)
                S.wait("tensor", ENG_SEM[ws.ce], mr)
            wb = ws.wbuf(sl)
            kb, ps = B.psf_pool.next(S, waiter="tensor")
            lastmm = None
            for kt in range(32):
                lhsT = wb[:, ts_(kt, 128)]
                rhs = B.H[:, ts_(kt, 512)]
                lastmm = S.tensor.matmul(ps[:, 0:512], lhsT, rhs,
                                         start=(kt == 0), stop=(kt == 31))
            mp = S.tinc(lastmm, "pe", 1)
            ws.release(S, sl, mp)
            S.wait("scalar", "pe", mp)
            sslot = 0
            S.wait("scalar", "dve", y0_free[sslot])
            if j == 0:
                S.wait("vector", "pe", outt_pe)
            ie = S.scalar.activation(B.Y0[:, ts_(sslot, 512)], ps[:, 0:512], AF.Copy)
            ma = S.tinc(ie, "act", 1)
            B.psf_pool.done(kb, "act", ma)
            S.wait("vector", "act", ma)
            y0 = B.Y0[:, ts_(sslot, 512)]
            mem = B.MEM[:, ts_(sslot, 512)]
            spk = B.SPK[:, ts_(sslot, 512)]
            acc = B.OUTT[:, ts_(j, 512)]
            i = S.vector.tensor_scalar(acc, y0, 1.0, None, op0=ALU.is_gt)
            S.sf("vector", i)
            prev_spk_ap = acc
            for st in range(2, 6):
                if st == 2:
                    i = S.vector.tensor_scalar(mem, y0, 1.9, None, op0=ALU.mult)
                else:
                    i = S.vector.scalar_tensor_tensor(mem, mem, 0.9, y0,
                                                      op0=ALU.mult, op1=ALU.add)
                S.sf("vector", i)
                i = S.vector.tensor_sub(mem, mem, prev_spk_ap)
                S.sf("vector", i)
                i = S.vector.tensor_scalar(spk, mem, 1.0, None, op0=ALU.is_gt)
                S.sf("vector", i)
                i = S.vector.tensor_add(acc, acc, spk)
                S.sf("vector", i)
                prev_spk_ap = spk
            i = S.vector.tensor_scalar(acc, acc, 0.1, None, op0=ALU.mult)
            S.sf("vector", i)
            y0_free[sslot] = S.cnt["dve"]
        S.wait("tensor", "dve", S.cnt["dve"])
        _, x_m = emit_out_transpose_add(S, B, B.OUTT, c)
    if cached is not None:
        cached[0] = True
    return ("dve", x_m)



def emit_attn(S, B, W, x_ready, cfg):
    full_y = bool(cfg.get("inner_residual"))
    if full_y:
        r1 = emit_ln(S, B, x_ready, full_y=True, yf=B.YF, make_yt=False,
                     fp32_add_to_x=True)
        r = emit_ln(S, B, r1.yt, src=B.YF)
    else:
        r = emit_ln(S, B, x_ready)
    yt_sem, yt_m = r.yt

    ws = WStream(B, "aw", cast_eng="vector", single_stage=True)
    wso = ws
    S.wait("tensor", yt_sem, yt_m)

    v4 = B.VAUG[:, 0:NT * 130].rearrange("p (t hh c) -> p t hh c", hh=2, c=65)
    im = S.vector.memset(v4[:, :, :, 64:65], 1.0)
    S.tinc(im, "dve", 1)

    SCB = ("PSF0", "PSF1", "PSF4")
    AVB = ("PSF2", "PSF5")
    sc_free = [("act", 0)] * 3
    av_free = [("dve", 0)] * 2
    u_free = ("dve", 0)
    pt_free = [0, 0, 0]
    bb_free = [("act", 0), ("act", 0)]

    for p in range(8):
        if cfg.get("dev_qkv_bias"):
            ufree_now = u_free

            def bias_hook(S2, stage, off, w, mdma):
                S2.wait("tensor", "wdma", mdma)
                S2.wait("tensor", *ufree_now)
                lastm = 0
                for kk in range(w // 128):
                    kg = (off // 128) + kk
                    which = kg // 8
                    kt = kg % 8
                    ins = S2.tensor.matmul(
                        B.PSF3[:, which:which + 1], stage[:, ts_(kk, 128)],
                        B.LNBC[:, kt:kt + 1], start=(kg == 0), stop=(kg == 23))
                    lastm = S2.tinc(ins, "pe", 1)
                return lastm

            sl, mr = ws.load(S, W.qkv_s[p, :, :], 3072,
                             cast_fn=cfg.get("cast_fn"), stage_hook=bias_hook)
        else:
            sl, mr = ws.load(S, W.qkv_s[p, :, :], 3072,
                             cast_fn=cfg.get("cast_fn"))
        S.wait("tensor", ENG_SEM[ws.ce], mr)
        wb = ws.wbuf(sl)
        last_w_use = 0

        if cfg.get("dev_qkv_bias"):
            mp_b = ws.stage_guard[1]
            S.wait("vector", "pe", mp_b)
            for which, dstcol in ((0, B.BQE), (1, B.BKE), (2, B.VBC)):
                iv = S.vector.tensor_tensor(
                    dstcol[:, 0:1], B.PSF3[:, which:which + 1],
                    W.bqkv_col[:, 8 * which + p:8 * which + p + 1], op=ALU.add)
                mv = S.sf("vector", iv)
            u_free = ("dve", mv)
            kb, psb = B.psb_pool.next(S, waiter="tensor")
            S.wait("tensor", "dve", u_free[1])
            itr = S.tensor.transpose(psb[0:1, 0:128], B.VBC[:, 0:1], B.IDENT[:, 0:128])
            mp = S.tinc(itr, "pe", 1)
            S.wait("vector", "pe", mp)
            ib = S.vector.tensor_copy(B.BVRD[0:1, 0:128], psb[0:1, 0:128])
            mv = S.sf("vector", ib)
            B.psb_pool.done(kb, "dve", mv)

        if cfg.get("dev_qkv_bias"):
            qk_biases = (B.BQE[:, 0:1], B.BKE[:, 0:1])
        elif cfg.get("qk_bias"):
            qk_biases = (W.bq_col[:, p:p + 1], W.bk_col[:, p:p + 1])
        else:
            qk_biases = (None, None)
        for which, dst, bias_ap in (
            (0, B.QTP, qk_biases[0]),
            (1, B.KTP, qk_biases[1]),
        ):
            scale = 0.125 if which == 0 else None
            for c in range(4):
                k, ps = (c % 3), B[SCB[c % 3]]
                S.wait("tensor", *sc_free[k])
                lastmm = None
                for kt in range(ND):
                    lhsT = wb[:, ots(which * 1024, kt, 128)]
                    rhs = B.YT[:, ots(kt * SEQ, c, 512)]
                    lastmm = S.tensor.matmul(ps[:, 0:512], lhsT, rhs,
                                             start=(kt == 0), stop=(kt == ND - 1))
                mp = S.tinc(lastmm, "pe", 1)
                last_w_use = mp
                S.wait("vector", "pe", mp)
                if bias_ap is not None and scale is not None:
                    ie = S.vector.tensor_scalar(dst[:, ts_(c, 512)], ps[:, 0:512],
                                                bias_ap, 0.125,
                                                op0=ALU.add, op1=ALU.mult)
                elif bias_ap is not None:
                    ie = S.vector.tensor_scalar(dst[:, ts_(c, 512)], ps[:, 0:512],
                                                bias_ap, None, op0=ALU.add)
                elif scale is not None:
                    ie = S.vector.tensor_scalar(dst[:, ts_(c, 512)], ps[:, 0:512],
                                                0.125, None, op0=ALU.mult)
                else:
                    ie = S.vector.tensor_copy(dst[:, ts_(c, 512)], ps[:, 0:512])
                me = S.tinc(ie, "dve", 1)
                sc_free[k] = ("dve", me)
        m_qkt = S.cnt["dve"]

        if cfg.get("rope"):
            lastv = None
            for dst in (B.QTP, B.KTP):
                i1 = S.vector.stream_shuffle(B.RSH[:, 0:SEQ], dst[:, 0:SEQ],
                                             cfg["rope_mask"])
                S.sf("vector", i1)
                i2 = S.vector.tensor_mul(B.RSH[:, 0:SEQ], B.RSH[:, 0:SEQ], B.RSIN[:, 0:SEQ])
                S.sf("vector", i2)
                i3 = S.vector.tensor_mul(dst[:, 0:SEQ], dst[:, 0:SEQ], B.RCOS[:, 0:SEQ])
                S.sf("vector", i3)
                lastv = S.vector.tensor_add(dst[:, 0:SEQ], dst[:, 0:SEQ], B.RSH[:, 0:SEQ])
                S.sf("vector", lastv)
            m_qkt = S.cnt["dve"]

        vbr = cfg.get("v_bias_row")
        for t in range(NT):
            S.wait("tensor", *u_free)
            lastmm = None
            for kt in range(ND):
                lhsT = B.YT[:, ots(kt * SEQ, t, 128)]
                rhs = wb[:, ots(2 * 1024, kt, 128)]
                lastmm = S.tensor.matmul(B.PSF3[:, 0:128], lhsT, rhs,
                                         start=(kt == 0),
                                         stop=(kt == ND - 1 and vbr is None))
            if vbr == "BVRD":
                lastmm = S.tensor.matmul(B.PSF3[:, 0:128], B.ONES[0:1, 0:128],
                                         B.BVRD[0:1, 0:128], start=False, stop=True)
            elif vbr == "BVRB16":
                lastmm = S.tensor.matmul(B.PSF3[:, 0:128], B.ONES[0:1, 0:128],
                                         B.BVRB[0:1, ts_(p, 128)], start=False, stop=True)
            elif vbr is not None:
                rowap = W[vbr][0:1, ts_(p, 128)]
                lastmm = S.tensor.matmul(B.PSF3[:, 0:128], B.ONESF[0:1, 0:128],
                                         rowap, start=False, stop=True)
            mp = S.tinc(lastmm, "pe", 1)
            last_w_use = mp
            S.wait("vector", "pe", mp)
            e1 = S.vector.tensor_copy(v4[:, t, 0, 0:64], B.PSF3[:, 0:64])
            e2 = S.vector.tensor_copy(v4[:, t, 1, 0:64], B.PSF3[:, 64:128])
            me = S.tinc(e2, "dve", 1)
            u_free = ("dve", me)
        ws.release(S, sl, last_w_use)
        m_v = S.cnt["dve"]

        sq_free_pe = 0
        for which, src in ((0, B.QTP), (1, B.KTP)):
            S.wait("scalar", "dve", m_qkt)
            S.wait("scalar", "pe", sq_free_pe)
            iq = S.scalar.activation(B.QSQ[:, 0:SEQ], src[:, 0:SEQ], AF.Square)
            msrc = S.tinc(iq, "act", 1)
            for c4 in range(4):
                S.wait("tensor", "act", msrc)
                S.wait("tensor", *u_free)
                mm = S.tensor.matmul(B.PSF3[0:1, 0:512], B.ONES[:, 0:1],
                                     B.QSQ[:, ts_(c4, 512)], start=True, stop=True)
                mp = S.tinc(mm, "pe", 1)
                sq_free_pe = mp
                S.wait("vector", "pe", mp)
                ir = S.vector.reduce_max(B.B4[0:1, which * 4 + c4:which * 4 + c4 + 1],
                                         B.PSF3[0:1, 0:512], axis=AX.X)
                mu = S.tinc(ir, "dve", 1)
                u_free = ("dve", mu)
        S.wait("vector", "dve", S.cnt["dve"])
        ilast = S.vector.reduce_max(B.B4[0:1, 8:9], B.B4[0:1, 0:4], axis=AX.X)
        S.sf("vector", ilast)
        ilast = S.vector.reduce_max(B.B4[0:1, 9:10], B.B4[0:1, 4:8], axis=AX.X)
        S.sf("vector", ilast)
        i = S.vector.tensor_tensor(B.B4[0:1, 10:11], B.B4[0:1, 8:9],
                                   B.B4[0:1, 9:10], op=ALU.mult)
        m = S.tinc(i, "dve", 1)
        S.wait("scalar", "dve", m)
        i = S.scalar.activation(B.B4[0:1, 11:12], B.B4[0:1, 10:11], AF.Sqrt)
        m = S.tinc(i, "act", 1)
        S.wait("vector", "act", m)
        i = S.vector.tensor_scalar(B.B4[0:1, 12:13], B.B4[0:1, 11:12],
                                   -1.04, -0.5, op0=ALU.mult, op1=ALU.add)
        mneg = S.tinc(i, "dve", 1)
        S.wait("tensor", "dve", mneg)
        S.wait("tensor", *u_free)
        mm = S.tensor.matmul(B.PSF3[:, 0:1], B.ONESF[0:1, :], B.B4[0:1, 12:13],
                             start=True, stop=True)
        mp = S.tinc(mm, "pe", 1)
        S.wait("vector", "pe", mp)
        S.wait("vector", *bb_free[p % 2])
        ib = S.vector.tensor_copy(B.BB[:, p % 2:p % 2 + 1], B.PSF3[:, 0:1])
        mbb = S.tinc(ib, "dve", 1)
        u_free = ("dve", mbb)
        bias_b = B.BB[:, p % 2:p % 2 + 1]

        S.wait("tensor", "dve", max(m_qkt, mbb, m_v))
        ma_last = 0
        sci = 0
        for hh in range(2):
            base = hh * 64
            for c in range(4):
                tmax = (4 * c + 4) if cfg.get("causal") else NT
                avk = (hh * 4 + c) % 2
                avps = B[AVB[avk]]
                S.wait("tensor", *av_free[avk])
                for ti in range(tmax):
                    t = ti
                    k = sci % 3
                    sci += 1
                    ps = B[SCB[k]]
                    S.wait("tensor", *sc_free[k])
                    mm = S.tensor.matmul(
                        ps[:, 0:512],
                        B.KTP[base:base + 64, ts_(t, 128)],
                        B.QTP[base:base + 64, ts_(c, 512)],
                        start=True, stop=True)
                    mp = S.tinc(mm, "pe", 1)
                    sl3 = ti % 3
                    S.wait("scalar", "pe", max(mp, pt_free[sl3]))
                    ie = S.scalar.activation(B.PT[:, ts_(sl3, 512)], ps[:, 0:512],
                                             AF.Exp, bias=bias_b)
                    ma = S.tinc(ie, "act", 1)
                    ma_last = ma
                    sc_free[k] = ("act", ma)
                    dep_sem, dep_m = "act", ma
                    if cfg.get("causal") and t >= 4 * c:
                        rr = t - 4 * c
                        S.wait("vector", "act", ma)
                        imk = S.vector.tensor_mul(B.PT[:, ts_(sl3, 512)],
                                                  B.PT[:, ts_(sl3, 512)],
                                                  B.MASKS[:, ts_(rr, 512)])
                        md = S.tinc(imk, "dve", 1)
                        dep_sem, dep_m = "dve", md
                    S.wait("tensor", dep_sem, dep_m)
                    mmv = S.tensor.matmul(
                        avps[0:65, 0:512],
                        v4[:, t, hh, 0:65],
                        B.PT[:, ts_(sl3, 512)],
                        start=(ti == 0), stop=(ti == tmax - 1))
                    pt_free[sl3] = S.tinc(mmv, "pe", 1)
                mz = pt_free[(tmax - 1) % 3]
                S.wait("vector", "pe", mz)
                i1 = S.vector.tensor_scalar(B.IZ[0:1, 0:512], avps[64:65, 0:512],
                                            1e-30, None, op0=ALU.add)
                S.sf("vector", i1)
                i2 = S.vector.reciprocal(B.IZ[0:1, 0:512], B.IZ[0:1, 0:512])
                miz = S.tinc(i2, "dve", 1)
                S.wait("tensor", "dve", miz)
                S.wait("tensor", *u_free)
                _izname = getattr(getattr(B.IZ, "tensor", B.IZ), "name", "")
                onesf_row = (B.ONESF[64:65, 0:64]
                             if _izname == "ARNF" else B.ONESF[0:1, 0:64])
                mm = S.tensor.matmul(B.PSF3[0:64, 0:512], onesf_row,
                                     B.IZ[0:1, 0:512], start=True, stop=True)
                mp = S.tinc(mm, "pe", 1)
                S.wait("vector", "pe", mp)
                ic = S.vector.tensor_copy(B.IZB[0:64, 0:512], B.PSF3[0:64, 0:512])
                S.sf("vector", ic)
                it = S.vector.tensor_tensor(
                    B.OT[base:base + 64, ots(p * SEQ, c, 512)],
                    avps[0:64, 0:512], B.IZB[0:64, 0:512], op=ALU.mult)
                mo = S.tinc(it, "dve", 1)
                av_free[avk] = ("dve", mo)
                u_free = ("dve", mo)
        bb_free[p % 2] = ("act", ma_last)
    m_ot = S.cnt["dve"]

    S.wait("tensor", "dve", m_ot)
    x_m = ("dve", 0)
    ta_pe = 0
    for c in range(4):
        for j in range(8):
            slo, mro = wso.load(S, W.wo_s[j, :, :], 8 * 128)
            S.wait("tensor", ENG_SEM[wso.ce], mro)
            wbo = wso.wbuf(slo)
            k, ps = (j % 3), B[SCB[j % 3]]
            S.wait("tensor", *sc_free[k])
            lastmm = None
            for kt in range(ND):
                lhsT = wbo[:, ts_(kt, 128)]
                rhs = B.OT[:, ots(kt * SEQ, c, 512)]
                lastmm = S.tensor.matmul(ps[:, 0:512], lhsT, rhs,
                                         start=(kt == 0), stop=(kt == ND - 1))
            mp = S.tinc(lastmm, "pe", 1)
            wso.release(S, slo, mp)
            S.wait("vector", "pe", mp)
            S.wait("vector", "pe", ta_pe)
            if cfg.get("o_bias"):
                ie = S.vector.tensor_scalar(B.OUTT[:, ts_(j, 512)], ps[:, 0:512],
                                            W.bo_col[:, j:j + 1], None, op0=ALU.add)
            else:
                ie = S.vector.tensor_copy(B.OUTT[:, ts_(j, 512)], ps[:, 0:512])
            ma = S.tinc(ie, "dve", 1)
            sc_free[k] = ("dve", ma)
        S.wait("tensor", "dve", S.cnt["dve"])
        _, xm = emit_out_transpose_add(S, B, B.OUTT, c)
        ta_pe = S.cnt["pe"]
        x_m = ("dve", xm)
    return x_m



def alloc_common(nc, stack, need):
    B = NS()

    def sb(name, shape, dt):
        B[name] = stack.enter_context(nc.sbuf_tensor(name, shape, dt))

    def psum(name, shape, dt):
        B[name] = stack.enter_context(nc.psum_tensor(name, shape, dt))

    sb("X", [128, NT * DM], F32)
    sb("YT", [128, ND * SEQ], BF16)
    sb("YROLL", [128, 2 * DM], BF16)
    sb("S1", [128, NT], F32)
    sb("S2", [128, NT], F32)
    sb("MU", [128, NT], F32)
    sb("MUSQ", [128, NT], F32)
    sb("VAR", [128, NT], F32)
    sb("SD", [128, NT], F32)
    sb("RSTD", [128, NT], F32)
    sb("LNSQ", [128, DM], F32)
    B["LNDISC"] = B["LNSQ"]
    sb("IDENT", [128, 128], BF16)

    if need & {"attn", "l1ffn", "l0ffn", "l3ffn"}:
        sb("OUTT", [128, 8 * 512], BF16)
    sb("ONES", [128, 128], BF16)
    sb("ONESF", [128, 128], F32)

    if "ffn" in need:
        sb("H", [128, 32 * 512], BF16)
        sb("w_stage0", [128, 2048], F32)
        sb("w_wbuf0", [128, 4096], BF16)
        sb("w_wbuf1", [128, 4096], BF16)

    if "l1ffn" in need:
        sb("RELU", [128, 2 * 512], BF16)
        sb("Y0", [128, 2 * 512], F32)
        sb("MEM", [128, 2 * 512], F32)
        sb("SPK", [128, 2 * 512], BF16)

    if "attn" in need:
        sb("QTP", [128, SEQ], BF16)
        sb("KTP", [128, SEQ], BF16)
        sb("VAUG", [128, NT * 130], BF16)
        if "l3attn" not in need:
            sb("OT", [128, ND * SEQ], BF16)
        sb("PT", [128, 3 * 512], BF16)
        sb("QSQ", [128, SEQ], BF16)
        sb("B4", [1, 16], F32)
        sb("IZ", [1, 512], F32)
        sb("IZB", [128, 512], F32)
        sb("BB", [128, 2], F32)
        sb("MASKS", [128, 4 * 512], BF16)
        sb("aw_stage0", [128, 3072], F32)
        sb("aw_wbuf0", [128, 3072], BF16)
        sb("aw_wbuf1", [128, 3072], BF16)

    if "l0ffn" in need:
        sb("LORA_T", [16, 1024], BF16)
        sb("ERALL", [1, 8 * 1024], BF16)
        sb("TR", [1, 4 * 512], BF16)
        sb("UR", [1, 4 * 512], BF16)
        sb("B1C", [128, 32], F32)
        sb("B2C", [128, 8], F32)
        sb("SCALE", [1, 1], F32)

    if "l3ffn" in need:
        sb("H3", [128, 22 * 512], BF16)
        sb("YF", [128, NT * DM], BF16) if ("l3attn" not in need and "yfull" not in need) else None
        B["HTK"] = B["YF"]
        sb("XN", [128, 1408], F32)
        sb("QT", [128, 1408], F32)
        sb("QS1", [128, 1408], BF16)
        sb("QS2", [128, 1408], BF16)
        sb("SGN", [128, 1408], BF16)
        sb("QSC", [128, 176], F32)
        sb("QRS", [128, 176], F32)
        sb("w_wbuf0", [128, 2816], BF16) if "ffn" not in need else None
        sb("w_wbuf1", [128, 2816], BF16) if "ffn" not in need else None

    if "l2moe" in need:
        sb("ACC", [128, 8 * SEQ], BF16)
        sb("WTOK", [128, NT * 8], F32)
        sb("WTOKB", [128, NT * 8], BF16)
        sb("WFM", [128, SEQ], BF16)
        sb("WBC", [128, 512], BF16)
        sb("SELE", [8, 1024], BF16)
        sb("WGC", [128, 64], F32)
        sb("WGCB", [128, 64], BF16)
        sb("BGR", [1, 8], F32)
        sb("MB1C", [128, 32], F32)
        sb("MB2C", [128, 8], F32)

    if "rope" in need:
        sb("RSH", [128, SEQ], BF16)
        sb("RCOS", [128, SEQ], BF16)
        sb("RSIN", [128, SEQ], BF16)

    if "l3attn" in need or "yfull" in need:
        sb("YF", [128, NT * DM], BF16)
    if "l3attn" in need:
        B["OT"] = B["YF"]
        sb("BQE", [128, 1], F32)
        sb("BKE", [128, 1], F32)
        sb("VBC", [128, 1], BF16)
        sb("BVRD", [1, 128], BF16)

    for i in range(6):
        psum(f"PSF{i}", [128, 512], F32)
    psum("PSB0", [128, 1024], BF16)
    psum("PSB1", [128, 1024], BF16)
    B["psf_pool"] = PsumBank([B[f"PSF{i}"] for i in range(6)])
    B["psb_pool"] = PsumBank([B["PSB0"], B["PSB1"]])
    return B



def prep_lhsT_stream(wT, chunk_cols):
    K, M = wT.shape
    KT = K // 128
    C = chunk_cols
    nch = M // C
    a = wT.reshape(KT, 128, M)
    out = np.empty((nch, 128, KT * C), dtype=np.float32)
    for j in range(nch):
        blk = a[:, :, j * C:(j + 1) * C]
        out[j] = np.ascontiguousarray(blk.transpose(1, 0, 2)).reshape(128, KT * C)
    return np.ascontiguousarray(out)



def build_l1ffn_test_program():
    import contextlib
    nc = bass.Bass("TRN2", target_bir_lowering=False)
    x = nc.dram_tensor("x", [SEQ, DM], F32, kind="ExternalInput").ap()
    wfc_s = nc.dram_tensor("wfc_s", [8, 128, 8 * 512], F32, kind="ExternalInput").ap()
    wpr_s = nc.dram_tensor("wpr_s", [8, 128, 32 * 128], F32, kind="ExternalInput").ap()
    ident = nc.dram_tensor("cc_ident", [128, 128], BF16, kind="ExternalInput").ap()
    xo = nc.dram_tensor("x_out", [SEQ, DM], F32, kind="ExternalOutput").ap()

    W = NS(wfc_s=wfc_s, wpr_s=wpr_s)

    with contextlib.ExitStack() as stack:
        B = alloc_common(nc, stack, {"ffn", "l1ffn"})
        sems = {n: stack.enter_context(nc.semaphore(n)) for n in SEM_NAMES}
        block = stack.enter_context(nc.Block())

        def schedule(S):
            iid = S.sync.dma_start(B.IDENT[:, :], ident[:, :])
            S.tinc(iid, "wdma", 16)
            xr = emit_load_x(S, B, x)
            x_m = emit_l1ffn(S, B, W, xr)
            emit_store_x(S, B, xo, x_m)

        def mk(engname):
            def f(eng):
                B["psf_pool"] = PsumBank([B[f"PSF{i}"] for i in range(4)])
                B["psb_pool"] = PsumBank([B["PSB0"], B["PSB1"]])
                schedule(Sched(nc, engname, eng, sems))
            return f

        block.sync(mk("sync"))
        block.tensor(mk("tensor"))
        block.vector(mk("vector"))
        block.scalar(mk("scalar"))
        block.gpsimd(mk("gpsimd"))
    return nc


def _finish_program(nc, B, sems, block, schedule):
    def mk(engname):
        def f(eng):
            B["psf_pool"] = PsumBank([B[f"PSF{i}"] for i in range(6)])
            B["psb_pool"] = PsumBank([B["PSB0"], B["PSB1"]])
            schedule(Sched(nc, engname, eng, sems))
        return f
    block.sync(mk("sync"))
    block.tensor(mk("tensor"))
    block.vector(mk("vector"))
    block.scalar(mk("scalar"))
    block.gpsimd(mk("gpsimd"))


def emit_load_consts(S, B, consts):
    for dr, sb in consts:
        cast = getattr(dr.tensor, "dtype", None) != getattr(sb.tensor, "dtype", None)
        if cast:
            ins = S.gpsimd.dma_start(sb, dr)
            S.tinc(ins, "gdma", 16)
        else:
            ins = S.sync.dma_start(sb, dr)
            S.tinc(ins, "wdma", 16)
    return ("wdma", S.cnt["wdma"]) if not any(
        getattr(d.tensor, "dtype", None) != getattr(b.tensor, "dtype", None)
        for d, b in consts) else ("gdma", S.cnt["gdma"])


def host_rope_tables():
    import ml_dtypes
    pos = np.arange(SEQ, dtype=np.float64)[None, :]
    p = np.arange(128)
    kk = (p % 64) // 2
    ang = pos / (10000.0 ** (2.0 * kk[:, None] / 64.0))
    cos = np.cos(ang)
    sin = np.sin(ang) * np.where(p % 2 == 0, -1.0, 1.0)[:, None]
    return {
        "cc_rcos": cos.astype(np.float32).astype(ml_dtypes.bfloat16),
        "cc_rsin": sin.astype(np.float32).astype(ml_dtypes.bfloat16),
    }


def build_attn_test_program(cfg):
    import contextlib
    nc = bass.Bass("TRN2", target_bir_lowering=False)
    x = nc.dram_tensor("x", [SEQ, DM], F32, kind="ExternalInput").ap()
    qkv_s = nc.dram_tensor("qkv_s", [8, 128, 3072], F32, kind="ExternalInput").ap()
    wo_s = nc.dram_tensor("wo_s", [8, 128, 1024], F32, kind="ExternalInput").ap()
    ident = nc.dram_tensor("cc_ident", [128, 128], BF16, kind="ExternalInput").ap()
    ones = nc.dram_tensor("cc_ones", [128, 128], BF16, kind="ExternalInput").ap()
    onesf = nc.dram_tensor("cc_onesf", [128, 128], F32, kind="ExternalInput").ap()
    masks = nc.dram_tensor("cc_masks", [128, 2048], BF16, kind="ExternalInput").ap()
    W = NS(qkv_s=qkv_s, wo_s=wo_s)
    need = {"attn"}
    if cfg.get("rope"):
        need.add("rope")
        W["rcos"] = nc.dram_tensor("cc_rcos", [128, SEQ], BF16, kind="ExternalInput").ap()
        W["rsin"] = nc.dram_tensor("cc_rsin", [128, SEQ], BF16, kind="ExternalInput").ap()
    if cfg.get("qk_bias"):
        W["bq_col"] = nc.dram_tensor("bq_col", [128, 8], F32, kind="ExternalInput").ap()
        W["bk_col"] = nc.dram_tensor("bk_col", [128, 8], F32, kind="ExternalInput").ap()
    if cfg.get("v_bias_row"):
        W["bv_row_t"] = nc.dram_tensor("bv_row", [1, 1024], F32, kind="ExternalInput").ap()
    if cfg.get("o_bias"):
        W["bo_col"] = nc.dram_tensor("bo_col", [128, 8], F32, kind="ExternalInput").ap()
    xo = nc.dram_tensor("x_out", [SEQ, DM], F32, kind="ExternalOutput").ap()

    with contextlib.ExitStack() as stack:
        B = alloc_common(nc, stack, need)
        if cfg.get("qk_bias"):
            B["BQC"] = stack.enter_context(nc.sbuf_tensor("BQC", [128, 8], F32))
            B["BKC"] = stack.enter_context(nc.sbuf_tensor("BKC", [128, 8], F32))
        if cfg.get("v_bias_row"):
            B["BVR"] = stack.enter_context(nc.sbuf_tensor("BVR", [1, 1024], F32))
        if cfg.get("o_bias"):
            B["BOC"] = stack.enter_context(nc.sbuf_tensor("BOC", [128, 8], F32))
        sems = {n: stack.enter_context(nc.semaphore(n)) for n in SEM_NAMES}
        block = stack.enter_context(nc.Block())

        def schedule(S):
            consts = [(ident[:, :], B.IDENT[:, :]), (ones[:, :], B.ONES[:, :]),
                      (onesf[:, :], B.ONESF[:, :]), (masks[:, :], B.MASKS[:, :])]
            Wd = NS(qkv_s=qkv_s, wo_s=wo_s)
            if cfg.get("qk_bias"):
                consts += [(W["bq_col"], B.BQC[:, :]), (W["bk_col"], B.BKC[:, :])]
                Wd["bq_col"] = B.BQC
                Wd["bk_col"] = B.BKC
            if cfg.get("v_bias_row"):
                consts += [(W["bv_row_t"], B.BVR[:, :])]
                Wd["bv_row"] = B.BVR
            if cfg.get("o_bias"):
                consts += [(W["bo_col"], B.BOC[:, :])]
                Wd["bo_col"] = B.BOC
            if cfg.get("rope"):
                consts += [(W["rcos"], B.RCOS[:, :]), (W["rsin"], B.RSIN[:, :])]
            cm = emit_load_consts(S, B, consts)
            xr = emit_load_x(S, B, x)
            S.wait("vector", *cm)
            S.wait("scalar", *cm)
            S.wait("tensor", *cm)
            x_m = emit_attn(S, B, Wd, xr, cfg)
            emit_store_x(S, B, xo, x_m)

        _finish_program(nc, B, sems, block, schedule)
    return nc


def host_prep_attn(wq_io, wk_io, wv_io, woT, bq=None, bk=None, bv=None, bo=None):
    qs = prep_lhsT_stream(np.ascontiguousarray(wq_io), 128)
    ks = prep_lhsT_stream(np.ascontiguousarray(wk_io), 128)
    vs = prep_lhsT_stream(np.ascontiguousarray(wv_io), 128)
    out = {
        "qkv_s": np.ascontiguousarray(np.concatenate([qs, ks, vs], axis=2)),
        "wo_s": prep_lhsT_stream(np.ascontiguousarray(woT), 128),
    }
    if bq is not None:
        out["bq_col"] = np.ascontiguousarray(bq.reshape(8, 128).T)
        out["bk_col"] = np.ascontiguousarray(bk.reshape(8, 128).T)
    if bv is not None:
        out["bv_row"] = np.ascontiguousarray(bv.reshape(1, 1024))
    if bo is not None:
        out["bo_col"] = np.ascontiguousarray(bo.reshape(8, 128).T)
    return out




def build_l3attn_test_program():
    import contextlib
    nc = bass.Bass("TRN2", target_bir_lowering=False)
    x = nc.dram_tensor("x", [SEQ, DM], F32, kind="ExternalInput").ap()
    qkv_s = nc.dram_tensor("qkv_s", [8, 128, 3072], F32, kind="ExternalInput").ap()
    wo_s = nc.dram_tensor("wo_s", [8, 128, 1024], F32, kind="ExternalInput").ap()
    bqkv_col = nc.dram_tensor("bqkv_col", [128, 24], F32, kind="ExternalInput").ap()
    bo_col = nc.dram_tensor("bo_col", [128, 8], F32, kind="ExternalInput").ap()
    lnw_col = nc.dram_tensor("lnw_col", [128, 8], F32, kind="ExternalInput").ap()
    lnb_col = nc.dram_tensor("lnb_col", [128, 8], F32, kind="ExternalInput").ap()
    ident = nc.dram_tensor("cc_ident", [128, 128], BF16, kind="ExternalInput").ap()
    ones = nc.dram_tensor("cc_ones", [128, 128], BF16, kind="ExternalInput").ap()
    onesf = nc.dram_tensor("cc_onesf", [128, 128], F32, kind="ExternalInput").ap()
    masks = nc.dram_tensor("cc_masks", [128, 2048], BF16, kind="ExternalInput").ap()
    xo = nc.dram_tensor("x_out", [SEQ, DM], F32, kind="ExternalOutput").ap()

    with contextlib.ExitStack() as stack:
        B = alloc_common(nc, stack, {"attn", "l3attn"})
        for nm, shape, dt in (("BQKVC", [128, 24], F32), ("BOC", [128, 8], F32),
                              ("LNWC", [128, 8], F32), ("LNBC", [128, 8], F32)):
            B[nm] = stack.enter_context(nc.sbuf_tensor(nm, shape, dt))
        sems = {n: stack.enter_context(nc.semaphore(n)) for n in SEM_NAMES}
        block = stack.enter_context(nc.Block())

        def schedule(S):
            consts = [(ident[:, :], B.IDENT[:, :]), (ones[:, :], B.ONES[:, :]),
                      (onesf[:, :], B.ONESF[:, :]), (masks[:, :], B.MASKS[:, :]),
                      (bqkv_col[:, :], B.BQKVC[:, :]), (bo_col[:, :], B.BOC[:, :]),
                      (lnw_col[:, :], B.LNWC[:, :]), (lnb_col[:, :], B.LNBC[:, :])]
            cm = emit_load_consts(S, B, consts)
            xr = emit_load_x(S, B, x)
            S.wait("vector", *cm)
            S.wait("scalar", *cm)
            S.wait("tensor", *cm)

            def lnw_cast(S2, wbuf, stage, off, w):
                last = None
                for kk in range(w // 128):
                    kt = (off // 128) + kk
                    last = S2.vector.tensor_scalar(
                        wbuf[:, ts_(kt, 128)], stage[:, ts_(kk, 128)],
                        B.LNWC[:, (kt % 8):(kt % 8) + 1], None, op0=ALU.mult)
                return last

            cfg = NS(causal=False, inner_residual=True, dev_qkv_bias=True,
                     v_bias_row="BVRD", o_bias=True, cast_fn=lnw_cast)
            Wd = NS(qkv_s=qkv_s, wo_s=wo_s, bqkv_col=B.BQKVC, bo_col=B.BOC)
            x_m = emit_attn(S, B, Wd, xr, cfg)
            emit_store_x(S, B, xo, x_m)

        _finish_program(nc, B, sems, block, schedule)
    return nc


def host_prep_l3attn(W_qkv, b_qkv, W_proj, b_proj, ln_w, ln_b):
    import ml_dtypes
    qs = prep_lhsT_stream(np.ascontiguousarray(W_qkv[:, 0:1024]), 128)
    ks = prep_lhsT_stream(np.ascontiguousarray(W_qkv[:, 1024:2048]), 128)
    vs = prep_lhsT_stream(np.ascontiguousarray(W_qkv[:, 2048:3072]), 128)
    bq = np.stack([b_qkv[0:1024].reshape(8, 128).T,
                   b_qkv[1024:2048].reshape(8, 128).T,
                   b_qkv[2048:3072].reshape(8, 128).T], axis=1)
    bqkv_col = np.ascontiguousarray(bq.transpose(0, 1, 2).reshape(128, 24))
    return {
        "qkv_s": np.ascontiguousarray(np.concatenate([qs, ks, vs], axis=2)),
        "wo_s": prep_lhsT_stream(np.ascontiguousarray(W_proj), 128),
        "bqkv_col": bqkv_col,
        "bo_col": np.ascontiguousarray(b_proj.reshape(8, 128).T),
        "lnw_col": np.ascontiguousarray(ln_w.reshape(8, 128).T),
        "lnb_col": np.ascontiguousarray(ln_b.reshape(8, 128).T),
    }





def emit_l0ffn(S, B, W, x_ready, cached=None):
    r = emit_ln(S, B, x_ready)
    yt_sem, yt_m = r.yt
    ws = WStream(B, "w", cast_eng="vector", single_stage=True)
    S.wait("tensor", yt_sem, yt_m)
    has_cache = W.get("w1b_s") is not None

    for kt in range(ND):
        kb, psb = B.psb_pool.next(S, waiter="tensor")
        itr = S.tensor.transpose(psb[0:16, 0:128],
                                 B.YT[:, kt * SEQ + 1: kt * SEQ + 17],
                                 B.IDENT[:, 0:128])
        mp = S.tinc(itr, "pe", 1)
        S.wait("vector", "pe", mp)
        ie = S.vector.tensor_copy(B.LORA_T[0:16, ts_(kt, 128)], psb[0:16, 0:128])
        me = S.tinc(ie, "dve", 1)
        B.psb_pool.done(kb, "dve", me)
    m_lt = S.cnt["dve"]
    S.wait("gpsimd", "dve", m_lt)
    base_g = S.cnt["gdma"]
    for sl in range(8):
        row = 2 * sl if sl < 4 else 8 + 2 * (sl - 4)
        ig = S.gpsimd.dma_start(B.ERALL[0:1, ts_(sl, 1024)],
                                B.LORA_T[row:row + 1, 0:1024])
        S.tinc(ig, "gdma", 16)
    m_er = ("gdma", S.cnt["gdma"])

    x_m = 0
    gelu_free = [0, 0]
    out_free = [0, 0]
    for c in range(4):
        uold = S.cnt["dve"]
        for i in range(4):
            kb, ps = B.psf_pool.next(S, waiter="tensor")
            lastmm = None
            for kt in range(ND):
                lastmm = S.tensor.matmul(
                    ps[0:1, 0:512],
                    B.YT[:, kt * SEQ + 2 + 2 * i: kt * SEQ + 3 + 2 * i],
                    B.YT[:, ots(kt * SEQ, c, 512)],
                    start=(kt == 0), stop=(kt == ND - 1))
            mp = S.tinc(lastmm, "pe", 1)
            S.wait("vector", "pe", mp)
            ie = S.vector.tensor_scalar(B.TR[0:1, ts_(i, 512)], ps[0:1, 0:512],
                                        B.SCALE[0:1, 0:1], None, op0=ALU.mult)
            me = S.tinc(ie, "dve", 1)
            B.psf_pool.done(kb, "dve", me)
        m_tr = S.cnt["dve"]
        S.wait("tensor", "dve", m_tr)
        S.wait("tensor", *m_er)

        for j in range(8):
            if has_cache:
                use_bf = cached[0] or c > 0
                sl = ws_load_cached(S, B, ws, W.w1_s[j, :, :],
                                    W.w1b_s[j, :, :], 8 * 512, use_bf)
            else:
                sl, mr = ws.load(S, W.w1_s[j, :, :], 8 * 512)
                S.wait("tensor", ENG_SEM[ws.ce], mr)
            wb = ws.wbuf(sl)
            lastmm_for_slot = None
            for sub in range(4):
                fblk = 4 * j + sub
                i = fblk // 8
                jj = fblk % 8
                kb, ps = B.psf_pool.next(S, waiter="tensor")
                lastmm = None
                for kt in range(ND):
                    lhsT = wb[:, kt * 512 + sub * 128: kt * 512 + (sub + 1) * 128]
                    rhs = B.YT[:, ots(kt * SEQ, c, 512)]
                    lastmm = S.tensor.matmul(ps[:, 0:512], lhsT, rhs,
                                             start=(kt == 0), stop=False)
                lastmm = S.tensor.matmul(
                    ps[:, 0:512],
                    B.ERALL[0:1, i * 1024 + jj * 128: i * 1024 + (jj + 1) * 128],
                    B.TR[0:1, ts_(i, 512)], start=False, stop=True)
                mp = S.tinc(lastmm, "pe", 1)
                lastmm_for_slot = mp
                S.wait("scalar", "pe", mp)
                S.wait("scalar", "dve", gelu_free[sub % 2])
                ig = S.scalar.activation(B.H[:, ts_(fblk, 512)], ps[:, 0:512],
                                         AF.Gelu, bias=W.b1_col[:, fblk:fblk + 1])
                ma = S.tinc(ig, "act", 1)
                B.psf_pool.done(kb, "act", ma)
            ws.release(S, sl, lastmm_for_slot)
        m_h = S.cnt["act"]

        S.wait("tensor", "act", m_h)
        for j in range(8):
            if j % 2 == 0:
                i = j // 2
                for rr in range(4):
                    kb, ps = B.psf_pool.next(S, waiter="tensor")
                    lastmm = None
                    for kt in range(ND):
                        lastmm = S.tensor.matmul(
                            ps[0:1, 0:512],
                            B.YT[:, kt * SEQ + 10 + 2 * i: kt * SEQ + 11 + 2 * i],
                            B.H[:, ts_(8 * rr + kt, 512)],
                            start=(kt == 0), stop=(kt == ND - 1))
                    mp = S.tinc(lastmm, "pe", 1)
                    S.wait("vector", "pe", mp)
                    ie = S.vector.tensor_scalar(B.UR[0:1, ts_(rr, 512)],
                                                ps[0:1, 0:512],
                                                B.SCALE[0:1, 0:1], None, op0=ALU.mult)
                    me = S.tinc(ie, "dve", 1)
                    B.psf_pool.done(kb, "dve", me)
                S.wait("tensor", "dve", S.cnt["dve"])
            if has_cache:
                use_bf = cached[0] or c > 0
                sl = ws_load_cached(S, B, ws, W.w2_s[j, :, :],
                                    W.w2b_s[j, :, :], 32 * 128, use_bf)
            else:
                sl, mr = ws.load(S, W.w2_s[j, :, :], 32 * 128)
                S.wait("tensor", ENG_SEM[ws.ce], mr)
            wb = ws.wbuf(sl)
            i = j // 2
            kb, ps = B.psf_pool.next(S, waiter="tensor")
            lastmm = None
            for kt in range(32):
                lastmm = S.tensor.matmul(ps[:, 0:512], wb[:, ts_(kt, 128)],
                                         B.H[:, ts_(kt, 512)],
                                         start=(kt == 0), stop=False)
            for rr in range(4):
                st = (4 + i) * 1024 + (j % 2) * 512 + rr
                lhsT = B.ERALL[0:1, st: st + 509: 4]
                lastmm = S.tensor.matmul(ps[:, 0:512], lhsT,
                                         B.UR[0:1, ts_(rr, 512)],
                                         start=False, stop=(rr == 3))
            mp = S.tinc(lastmm, "pe", 1)
            ws.release(S, sl, mp)
            S.wait("vector", "pe", mp)
            S.wait("vector", "dve", out_free[j % 2])
            ie = S.vector.tensor_scalar(B.OUTT[:, ts_(j, 512)], ps[:, 0:512],
                                        W.b2_col[:, j:j + 1], 0.5,
                                        op0=ALU.add, op1=ALU.mult)
            me = S.tinc(ie, "dve", 1)
            B.psf_pool.done(kb, "dve", me)
        S.wait("tensor", "dve", S.cnt["dve"])
        _, x_m = emit_out_transpose_add(S, B, B.OUTT, c)
    if cached is not None:
        cached[0] = True
    return ("dve", x_m)


def build_l0ffn_test_program():
    import contextlib
    nc = bass.Bass("TRN2", target_bir_lowering=False)
    x = nc.dram_tensor("x", [SEQ, DM], F32, kind="ExternalInput").ap()
    w1_s = nc.dram_tensor("w1_s", [8, 128, 8 * 512], F32, kind="ExternalInput").ap()
    w2_s = nc.dram_tensor("w2_s", [8, 128, 32 * 128], F32, kind="ExternalInput").ap()
    b1_col = nc.dram_tensor("b1_col", [128, 32], F32, kind="ExternalInput").ap()
    b2_col = nc.dram_tensor("b2_col", [128, 8], F32, kind="ExternalInput").ap()
    scale_in = nc.dram_tensor("scale_in", [1, 1], F32, kind="ExternalInput").ap()
    ident = nc.dram_tensor("cc_ident", [128, 128], BF16, kind="ExternalInput").ap()
    xo = nc.dram_tensor("x_out", [SEQ, DM], F32, kind="ExternalOutput").ap()

    with contextlib.ExitStack() as stack:
        B = alloc_common(nc, stack, {"ffn", "l0ffn"})
        sems = {n: stack.enter_context(nc.semaphore(n)) for n in SEM_NAMES}
        block = stack.enter_context(nc.Block())

        def schedule(S):
            consts = [(ident[:, :], B.IDENT[:, :]), (b1_col[:, :], B.B1C[:, :]),
                      (b2_col[:, :], B.B2C[:, :]), (scale_in[:, :], B.SCALE[0:1, 0:1])]
            cm = emit_load_consts(S, B, consts)
            xr = emit_load_x(S, B, x)
            S.wait("vector", *cm)
            S.wait("scalar", *cm)
            S.wait("tensor", *cm)
            Wd = NS(w1_s=w1_s, w2_s=w2_s, b1_col=B.B1C, b2_col=B.B2C)
            x_m = emit_l0ffn(S, B, Wd, xr)
            emit_store_x(S, B, xo, x_m)

        _finish_program(nc, B, sems, block, schedule)
    return nc


def host_prep_l0ffn(W1, b1, W2, b2, scale):
    return {
        "w1_s": prep_lhsT_stream(np.ascontiguousarray(W1.T), 512),
        "w2_s": prep_lhsT_stream(np.ascontiguousarray(W2.T), 128),
        "b1_col": np.ascontiguousarray(b1.reshape(32, 128).T),
        "b2_col": np.ascontiguousarray(b2.reshape(8, 128).T),
        "scale_in": np.asarray(scale, dtype=np.float32).reshape(1, 1),
    }





def emit_quant(S, B, src_ap, dst_ap, width):
    m = 0
    off = 0
    while off < width:
        w = min(1408, width - off)
        m = emit_quant_chain(S, B, src_ap[:, off:off + w], dst_ap[:, off:off + w], w)
        off += w
    return m


def emit_quant_chain(S, B, src_ap, dst_ap, width):
    nb = width // 16
    src3 = src_ap.rearrange("p (b g) -> p b g", g=16)
    i = S.vector.tensor_reduce(B.QSC[:, 0:nb], src3, axis=AX.X, op=ALU.max,
                               apply_absolute_value=True)
    S.sf("vector", i)
    i = S.vector.tensor_scalar(B.QSC[:, 0:nb], B.QSC[:, 0:nb], 1.0 / 6.0, 1e-12,
                               op0=ALU.mult, op1=ALU.max)
    S.sf("vector", i)
    i = S.vector.reciprocal(B.QRS[:, 0:nb], B.QSC[:, 0:nb])
    mrs = S.sf("vector", i)
    S.wait("scalar", "dve", mrs)
    isg = S.scalar.activation(B.SGN[:, 0:width], src_ap, AF.Sign)
    msg = S.tinc(isg, "act", 1)
    rsb = B.QRS[:, 0:nb].rearrange("p (b g) -> p b g", g=1).broadcast_to([128, nb, 16])
    i = S.vector.tensor_tensor(B.XN[:, 0:width].rearrange("p (b g) -> p b g", g=16),
                               src3, rsb, op=ALU.mult)
    mxn = S.tinc(i, "dve", 1)
    S.wait("vector", "act", msg)
    S.wait("scalar", "dve", mxn)
    ia = S.scalar.activation(B.XN[:, 0:width], B.XN[:, 0:width], AF.Abs)
    ma = S.tinc(ia, "act", 1)
    S.wait("vector", "act", ma)
    i = S.vector.tensor_scalar(B.QT[:, 0:width], B.XN[:, 0:width], 2.0, 4.0,
                               op0=ALU.max, op1=ALU.min)
    S.sf("vector", i)
    i = S.vector.tensor_scalar(B.QT[:, 0:width], B.QT[:, 0:width], MAGIC, None,
                               op0=ALU.add)
    S.sf("vector", i)
    i = S.vector.tensor_scalar(B.QS2[:, 0:width], B.QT[:, 0:width], MAGIC + 2.0,
                               None, op0=ALU.subtract)
    S.sf("vector", i)
    i = S.vector.tensor_scalar(B.QS1[:, 0:width], B.XN[:, 0:width], 5.0, 2.0,
                               op0=ALU.is_gt, op1=ALU.mult)
    S.sf("vector", i)
    i = S.vector.tensor_tensor(B.QS2[:, 0:width], B.QS2[:, 0:width],
                               B.QS1[:, 0:width], op=ALU.add)
    S.sf("vector", i)
    i = S.vector.tensor_scalar(B.QT[:, 0:width], B.XN[:, 0:width], 2.0, 2.0,
                               op0=ALU.min, op1=ALU.mult)
    S.sf("vector", i)
    i = S.vector.tensor_scalar(B.QT[:, 0:width], B.QT[:, 0:width], MAGIC, None,
                               op0=ALU.add)
    S.sf("vector", i)
    i = S.vector.tensor_scalar(B.QS1[:, 0:width], B.QT[:, 0:width], MAGIC, 0.5,
                               op0=ALU.subtract, op1=ALU.mult)
    S.sf("vector", i)
    i = S.vector.tensor_tensor(B.QS1[:, 0:width], B.QS1[:, 0:width],
                               B.QS2[:, 0:width], op=ALU.add)
    S.sf("vector", i)
    i = S.vector.tensor_tensor(B.QS1[:, 0:width], B.QS1[:, 0:width],
                               B.SGN[:, 0:width], op=ALU.mult)
    S.sf("vector", i)
    scb = B.QSC[:, 0:nb].rearrange("p (b g) -> p b g", g=1).broadcast_to([128, nb, 16])
    i = S.vector.tensor_tensor(dst_ap.rearrange("p (b g) -> p b g", g=16),
                               B.QS1[:, 0:width].rearrange("p (b g) -> p b g", g=16),
                               scb, op=ALU.mult)
    m = S.sf("vector", i)
    return m


def emit_l3_wprep(S, B, W):
    jobs = [(W.wg_nat, 22, 1024, W.g_s), (W.wu_nat, 22, 1024, W.u_s),
            (W.wd_nat, 8, 2816, W.d_s)]
    qt_free = ("dve", 0)
    tch_free = ("gdma", 0)
    for wsrc, ntile, tw, scratch in jobs:
        nkt = tw // 128
        for rt in range(ntile):
            S.wait("vector", *tch_free)
            off = 0
            while off < tw:
                w = min(1408, tw - off)
                S.wait("sync", *qt_free)
                idm = S.sync.dma_start(B.QT[:, 0:w],
                                       wsrc[ts_(rt, 128), off:off + w])
                mdma = S.tinc(idm, "wdma", 16)
                S.wait("vector", "wdma", mdma)
                S.wait("scalar", "wdma", mdma)
                mq = emit_quant_chain(S, B, B.QT[:, 0:w],
                                      B.H3[:, off:off + w], w)
                qt_free = ("dve", mq)
                off += w
            S.wait("tensor", "dve", S.cnt["dve"])
            npacks = (nkt + 3) // 4
            for pk in range(npacks):
                cnt = min(4, nkt - 4 * pk)
                kb, psb = B.psb_pool.next(S, waiter="tensor")
                lastmm = None
                for jj in range(cnt):
                    kt = 4 * pk + jj
                    lastmm = S.tensor.transpose(psb[:, ts_(jj, 128)],
                                                B.H3[:, ts_(kt, 128)],
                                                B.IDENT[:, 0:128])
                mp = S.tinc(lastmm, "pe", 1)
                S.wait("vector", "pe", mp)
                ie = S.vector.tensor_copy(
                    B.H3[:, 5632 + pk * 512: 5632 + pk * 512 + cnt * 128],
                    psb[:, 0:cnt * 128])
                me = S.tinc(ie, "dve", 1)
                B.psb_pool.done(kb, "dve", me)
            S.wait("gpsimd", "dve", S.cnt["dve"])
            ig = S.gpsimd.dma_start(scratch[rt, :, :], B.H3[:, 5632:5632 + tw])
            mg = S.tinc(ig, "gdma", 16)
            tch_free = ("gdma", mg)
    return ("gdma", S.cnt["gdma"])


def emit_l3ffn(S, B, W, x_ready, do_prep):
    prep_m = emit_l3_wprep(S, B, W) if do_prep else ("gdma", 0)
    if do_prep:
        S.wait("vector", "dve", S.cnt["dve"])
        S.wait("scalar", "dve", S.cnt["dve"])
    r = emit_ln(S, B, x_ready, full_y=True, yf=B.YF, make_yt=False)
    S.wait("vector", *r.yt)
    for t in range(NT):
        iy = S.vector.tensor_scalar(B.QT[:, 0:DM], B.X[:, ts_(t, DM)],
                                    B.MU[:, t:t + 1], B.RSTD[:, t:t + 1],
                                    op0=ALU.subtract, op1=ALU.mult)
        S.sf("vector", iy)
        emit_quant(S, B, B.QT[:, 0:DM], B.YF[:, ts_(t, DM)], DM)
        S.wait("tensor", "dve", S.cnt["dve"])
        emit_transpose_tile_to_feat(S, B, B.YF[:, ts_(t, DM)], B.YT, t)
    m_yt = S.cnt["dve"]
    ws = WStream(B, "w", cast_eng="vector")
    S.wait("sync", *prep_m)
    S.wait("tensor", "dve", m_yt)

    x_m = 0
    sg_free = [("dve", 0), ("dve", 0)]
    for c in range(4):
        for m in range(22):
            slg, mg = ws.load_bf16(S, W.g_s[m, :, :], 1024)
            S.wait("tensor", "wdma", mg)
            wbg = ws.wbuf(slg)
            ka, psA = B.psf_pool.next(S, waiter="tensor")
            lastmm = None
            for kt in range(ND):
                lastmm = S.tensor.matmul(psA[:, 0:512], wbg[:, ts_(kt, 128)],
                                         B.YT[:, ots(kt * SEQ, c, 512)],
                                         start=(kt == 0), stop=(kt == ND - 1))
            mpa = S.tinc(lastmm, "pe", 1)
            ws.release(S, slg, mpa)
            slu, mu_ = ws.load_bf16(S, W.u_s[m, :, :], 1024)
            S.wait("tensor", "wdma", mu_)
            wbu = ws.wbuf(slu)
            kbb, psB = B.psf_pool.next(S, waiter="tensor")
            lastmm = None
            for kt in range(ND):
                lastmm = S.tensor.matmul(psB[:, 0:512], wbu[:, ts_(kt, 128)],
                                         B.YT[:, ots(kt * SEQ, c, 512)],
                                         start=(kt == 0), stop=(kt == ND - 1))
            mpb = S.tinc(lastmm, "pe", 1)
            ws.release(S, slu, mpb)
            S.wait("scalar", "pe", mpa)
            S.wait("scalar", *sg_free[m % 2])
            isl = S.scalar.activation(B.QS1[:, ts_(m % 2, 512)], psA[:, 0:512], AF.Silu)
            msl = S.tinc(isl, "act", 1)
            B.psf_pool.done(ka, "act", msl)
            S.wait("vector", "act", msl)
            S.wait("vector", "pe", mpb)
            ih = S.vector.tensor_tensor(B.H3[:, ts_(m, 512)], B.QS1[:, ts_(m % 2, 512)],
                                        psB[:, 0:512], op=ALU.mult)
            mh = S.tinc(ih, "dve", 1)
            sg_free[m % 2] = ("dve", mh)
            B.psf_pool.done(kbb, "dve", mh)
        m_h3 = S.cnt["dve"]

        S.wait("tensor", "dve", m_h3)
        h3v = B.H3[:, 0:22 * 512].rearrange("p (m q) -> p m q", q=512)
        for tsub in range(4):
            for pk in range(6):
                cnt = min(4, 22 - 4 * pk)
                kb, psb = B.psb_pool.next(S, waiter="tensor")
                lastmm = None
                for jj in range(cnt):
                    m = 4 * pk + jj
                    lastmm = S.tensor.transpose(psb[:, ts_(jj, 128)],
                                                h3v[:, m, ts_(tsub, 128)],
                                                B.IDENT[:, 0:128])
                mp = S.tinc(lastmm, "pe", 1)
                S.wait("vector", "pe", mp)
                ie = S.vector.tensor_copy(
                    B.HTK[:, tsub * 2816 + pk * 512: tsub * 2816 + pk * 512 + cnt * 128],
                    psb[:, 0:cnt * 128])
                me = S.tinc(ie, "dve", 1)
                B.psb_pool.done(kb, "dve", me)
        for tsub in range(4):
            emit_quant(S, B, B.HTK[:, ts_(tsub, 2816)],
                       B.HTK[:, ts_(tsub, 2816)], 2816)
        m_hq = S.cnt["dve"]
        S.wait("tensor", "dve", m_hq)
        hkv = B.HTK[:, 0:4 * 2816].rearrange("p (t q) -> p t q", q=2816)
        for m in range(22):
            kb, psb = B.psb_pool.next(S, waiter="tensor")
            lastmm = None
            for tsub in range(4):
                lastmm = S.tensor.transpose(psb[:, ts_(tsub, 128)],
                                            hkv[:, tsub, ts_(m, 128)],
                                            B.IDENT[:, 0:128])
            mp = S.tinc(lastmm, "pe", 1)
            S.wait("vector", "pe", mp)
            ie = S.vector.tensor_copy(B.H3[:, ts_(m, 512)], psb[:, 0:512])
            me = S.tinc(ie, "dve", 1)
            B.psb_pool.done(kb, "dve", me)
        m_hqt = S.cnt["dve"]
        S.wait("tensor", "dve", m_hqt)

        for j in range(8):
            sld, md = ws.load_bf16(S, W.d_s[j, :, :], 2816)
            S.wait("tensor", "wdma", md)
            wbd = ws.wbuf(sld)
            kb, ps = B.psf_pool.next(S, waiter="tensor")
            lastmm = None
            for kt in range(22):
                lastmm = S.tensor.matmul(ps[:, 0:512], wbd[:, ts_(kt, 128)],
                                         B.H3[:, ts_(kt, 512)],
                                         start=(kt == 0), stop=(kt == 21))
            mp = S.tinc(lastmm, "pe", 1)
            ws.release(S, sld, mp)
            S.wait("vector", "pe", mp)
            ie = S.vector.tensor_scalar(B.OUTT[:, ts_(j, 512)], ps[:, 0:512],
                                        0.5, None, op0=ALU.mult)
            me = S.tinc(ie, "dve", 1)
            B.psf_pool.done(kb, "dve", me)
        S.wait("tensor", "dve", S.cnt["dve"])
        _, x_m = emit_out_transpose_add(S, B, B.OUTT, c)
    return ("dve", x_m)


def build_l3ffn_test_program():
    import contextlib
    nc = bass.Bass("TRN2", target_bir_lowering=False)
    x = nc.dram_tensor("x", [SEQ, DM], F32, kind="ExternalInput").ap()
    wg = nc.dram_tensor("wg_nat", [2816, 1024], F32, kind="ExternalInput").ap()
    wu = nc.dram_tensor("wu_nat", [2816, 1024], F32, kind="ExternalInput").ap()
    wd = nc.dram_tensor("wd_nat", [1024, 2816], F32, kind="ExternalInput").ap()
    g_s = nc.dram_tensor("g_scr", [22, 128, 1024], BF16).ap()
    u_s = nc.dram_tensor("u_scr", [22, 128, 1024], BF16).ap()
    d_s = nc.dram_tensor("d_scr", [8, 128, 2816], BF16).ap()
    ident = nc.dram_tensor("cc_ident", [128, 128], BF16, kind="ExternalInput").ap()
    xo = nc.dram_tensor("x_out", [SEQ, DM], F32, kind="ExternalOutput").ap()

    with contextlib.ExitStack() as stack:
        B = alloc_common(nc, stack, {"l3ffn"})
        sems = {n: stack.enter_context(nc.semaphore(n)) for n in SEM_NAMES}
        block = stack.enter_context(nc.Block())

        def schedule(S):
            cm = emit_load_consts(S, B, [(ident[:, :], B.IDENT[:, :])])
            xr = emit_load_x(S, B, x)
            S.wait("vector", *cm)
            S.wait("scalar", *cm)
            S.wait("tensor", *cm)
            Wd = NS(wg_nat=wg, wu_nat=wu, wd_nat=wd, g_s=g_s, u_s=u_s, d_s=d_s)
            x_m = emit_l3ffn(S, B, Wd, xr, do_prep=True)
            emit_store_x(S, B, xo, x_m)

        _finish_program(nc, B, sems, block, schedule)
    return nc





def emit_l2moe(S, B, W, x_ready, cached):
    r = emit_ln(S, B, x_ready)
    yt_sem, yt_m = r.yt
    ws = WStream(B, "w", cast_eng="vector", single_stage=True)
    S.wait("tensor", yt_sem, yt_m)

    icst = S.vector.tensor_copy(B.WGCB[:, 0:64], B.WGC[:, 0:64])
    mcst = S.tinc(icst, "dve", 1)
    S.wait("tensor", "dve", mcst)
    for t in range(NT):
        kb, ps = B.psf_pool.next(S, waiter="tensor")
        lastmm = None
        for kt in range(ND):
            lastmm = S.tensor.matmul(ps[:, 0:8],
                                     B.YT[:, ots(kt * SEQ, t, 128)],
                                     B.WGCB[:, ts_(kt, 8)],
                                     start=(kt == 0), stop=False)
        lastmm = S.tensor.matmul(ps[:, 0:8], B.ONESF[0:1, 0:128],
                                 B.BGR[0:1, 0:8], start=False, stop=True)
        mp = S.tinc(lastmm, "pe", 1)
        S.wait("vector", "pe", mp)
        i = S.vector.reduce_max(B.SD[:, 0:1], ps[:, 0:8], axis=AX.X, negate=True)
        S.sf("vector", i)
        S.wait("scalar", "dve", S.cnt["dve"])
        S.wait("scalar", "pe", mp)
        ie = S.scalar.activation(B.S1[:, 0:8], ps[:, 0:8], AF.Exp,
                                 bias=B.SD[:, 0:1])
        ma = S.tinc(ie, "act", 1)
        B.psf_pool.done(kb, "act", ma)
        S.wait("vector", "act", ma)
        i = S.vector.reduce_max(B.VAR[:, 0:1], B.S1[:, 0:8], axis=AX.X)
        S.sf("vector", i)
        i = S.vector.tensor_scalar(B.S2[:, 0:8], B.S1[:, 0:8], B.VAR[:, 0:1],
                                   None, op0=ALU.is_ge)
        S.sf("vector", i)
        i = S.vector.scalar_tensor_tensor(B.S2[:, 0:8], B.S2[:, 0:8], -2.0,
                                          B.S1[:, 0:8], op0=ALU.mult, op1=ALU.add)
        S.sf("vector", i)
        i = S.vector.reduce_max(B.MUSQ[:, 0:1], B.S2[:, 0:8], axis=AX.X)
        S.sf("vector", i)
        i = S.vector.tensor_scalar(B.S2[:, 0:8], B.S1[:, 0:8], B.MUSQ[:, 0:1],
                                   None, op0=ALU.is_ge)
        S.sf("vector", i)
        i = S.vector.tensor_tensor(B.S1[:, 0:8], B.S1[:, 0:8], B.S2[:, 0:8],
                                   op=ALU.mult)
        S.sf("vector", i)
        i = S.vector.tensor_tensor(B.VAR[:, 0:1], B.VAR[:, 0:1], B.MUSQ[:, 0:1],
                                   op=ALU.add)
        S.sf("vector", i)
        i = S.vector.reciprocal(B.VAR[:, 0:1], B.VAR[:, 0:1])
        S.sf("vector", i)
        i = S.vector.tensor_scalar(B.WTOK[:, ts_(t, 8)], B.S1[:, 0:8],
                                   B.VAR[:, 0:1], None, op0=ALU.mult)
        S.sf("vector", i)
    S.wait("tensor", "dve", S.cnt["dve"])
    for t in range(NT):
        kb, psb = B.psb_pool.next(S, waiter="tensor")
        S.wait("vector", "pe", 0)
        ic = S.vector.tensor_copy(B.WTOKB[:, ts_(t, 8)], B.WTOK[:, ts_(t, 8)])
        mc = S.sf("vector", ic)
        S.wait("tensor", "dve", mc)
        itr = S.tensor.transpose(psb[0:8, 0:128], B.WTOKB[:, ts_(t, 8)],
                                 B.IDENT[:, 0:128])
        mp = S.tinc(itr, "pe", 1)
        S.wait("vector", "pe", mp)
        ie = S.vector.tensor_copy(B.WFM[0:8, ts_(t, 128)], psb[0:8, 0:128])
        me = S.tinc(ie, "dve", 1)
        B.psb_pool.done(kb, "dve", me)
    m_wfm = S.cnt["dve"]

    iz = S.vector.memset(B.ACC[:, 0:8 * SEQ], 0.0)
    m_acc0 = S.tinc(iz, "dve", 1)

    x_m = 0
    for e in range(8):
        S.wait("gpsimd", "dve", m_wfm)
        ig = S.gpsimd.dma_start(B.MB1C[:, 0:32], W.b1_cols[e, :, :])
        S.tinc(ig, "gdma", 16)
        ig = S.gpsimd.dma_start(B.MB2C[:, 0:8], W.b2_cols[e, :, :])
        S.tinc(ig, "gdma", 16)
        m_bias = ("gdma", S.cnt["gdma"])
        S.wait("tensor", *m_bias)
        S.wait("vector", "dve", m_acc0)

        for c in range(4):
            w1_bf = cached[0] or c > 0
            w2_bf = cached[1] or c > 0
            kb, ps = B.psf_pool.next(S, waiter="tensor")
            mm = S.tensor.matmul(ps[:, 0:512], B.SELE[0:8, ts_(e, 128)],
                                 B.WFM[0:8, ts_(c, 512)], start=True, stop=True)
            mpw = S.tinc(mm, "pe", 1)
            S.wait("vector", "pe", mpw)
            ie = S.vector.tensor_copy(B.WBC[:, 0:512], ps[:, 0:512])
            mwbc = S.tinc(ie, "dve", 1)
            B.psf_pool.done(kb, "dve", mwbc)
            for j in range(8):
                if w1_bf:
                    sl, mr = ws.load_bf16(S, W.w1b_s[e, j, :, :], 4096)
                    S.wait("tensor", "wdma", mr)
                else:
                    sl, mr = ws.load(S, W.w1_s[e, j, :, :], 4096)
                    S.wait("tensor", ENG_SEM[ws.ce], mr)
                    S.wait("gpsimd", ENG_SEM[ws.ce], mr)
                    iw = S.gpsimd.dma_start(W.w1b_s[e, j, :, :], ws.wbuf(sl)[:, 0:4096])
                    mwb = S.tinc(iw, "gdma", 16)
                    ws.wbuf_extra[sl] = ("gdma", mwb)
                    ws.wb_last = mwb
                wb = ws.wbuf(sl)
                last_use = 0
                for sub in range(4):
                    fblk = 4 * j + sub
                    kb, ps = B.psf_pool.next(S, waiter="tensor")
                    lastmm = None
                    for kt in range(ND):
                        lhsT = wb[:, kt * 512 + sub * 128: kt * 512 + (sub + 1) * 128]
                        rhs = B.YT[:, ots(kt * SEQ, c, 512)]
                        lastmm = S.tensor.matmul(ps[:, 0:512], lhsT, rhs,
                                                 start=(kt == 0), stop=(kt == ND - 1))
                    mp = S.tinc(lastmm, "pe", 1)
                    last_use = mp
                    S.wait("scalar", "pe", mp)
                    igl = S.scalar.activation(B.H[:, ts_(fblk, 512)], ps[:, 0:512],
                                              AF.Gelu, bias=B.MB1C[:, fblk:fblk + 1])
                    ma = S.tinc(igl, "act", 1)
                    B.psf_pool.done(kb, "act", ma)
                ws.release(S, sl, last_use)
            m_h = S.cnt["act"]
            S.wait("tensor", "act", m_h)

            for j in range(8):
                if w2_bf:
                    sl, mr = ws.load_bf16(S, W.w2b_s[e, j, :, :], 4096)
                    S.wait("tensor", "wdma", mr)
                else:
                    sl, mr = ws.load(S, W.w2_s[e, j, :, :], 4096)
                    S.wait("tensor", ENG_SEM[ws.ce], mr)
                    S.wait("gpsimd", ENG_SEM[ws.ce], mr)
                    iw = S.gpsimd.dma_start(W.w2b_s[e, j, :, :], ws.wbuf(sl)[:, 0:4096])
                    mwb = S.tinc(iw, "gdma", 16)
                    ws.wbuf_extra[sl] = ("gdma", mwb)
                    ws.wb_last = mwb
                wb = ws.wbuf(sl)
                kb, ps = B.psf_pool.next(S, waiter="tensor")
                lastmm = None
                for kt in range(32):
                    lastmm = S.tensor.matmul(ps[:, 0:512], wb[:, ts_(kt, 128)],
                                             B.H[:, ts_(kt, 512)],
                                             start=(kt == 0), stop=(kt == 31))
                mp = S.tinc(lastmm, "pe", 1)
                ws.release(S, sl, mp)
                S.wait("vector", "pe", mp)
                i1 = S.vector.tensor_scalar(B.LNSQ[:, 0:512], ps[:, 0:512],
                                            B.MB2C[:, j:j + 1], None, op0=ALU.add)
                S.sf("vector", i1)
                i2 = S.vector.tensor_tensor(B.LNSQ[:, 0:512], B.LNSQ[:, 0:512],
                                            B.WBC[:, 0:512], op=ALU.mult)
                S.sf("vector", i2)
                accs = B.ACC[:, j * SEQ + c * 512: j * SEQ + (c + 1) * 512]
                i3 = S.vector.tensor_tensor(accs, accs, B.LNSQ[:, 0:512], op=ALU.add)
                me = S.sf("vector", i3)
                B.psf_pool.done(kb, "dve", me)
        if not cached[0]:
            cached[0] = True if e == 7 else cached[0]
        if not cached[1]:
            cached[1] = True if e == 7 else cached[1]
    m_comb = S.cnt["dve"]

    accv = B.ACC[:, 0:8 * SEQ].rearrange("p (d q) -> p d q", q=SEQ)
    S.wait("tensor", "dve", m_comb)
    for c in range(4):
        for tsub in range(4):
            t = 4 * c + tsub
            for pack in range(2):
                kb, psb = B.psb_pool.next(S, waiter="tensor")
                lastmm = None
                for jj in range(4):
                    d = pack * 4 + jj
                    lastmm = S.tensor.transpose(
                        psb[:, ts_(jj, 128)],
                        accv[:, d, ots(c * 512, tsub, 128)],
                        B.IDENT[:, 0:128])
                mp = S.tinc(lastmm, "pe", 1)
                S.wait("vector", "pe", mp)
                dst = B.X[:, t * DM + pack * 512: t * DM + (pack + 1) * 512]
                ev = S.vector.scalar_tensor_tensor(dst, psb[:, 0:512], 0.5, dst,
                                                   op0=ALU.mult, op1=ALU.add)
                me = S.tinc(ev, "dve", 1)
                B.psb_pool.done(kb, "dve", me)
                x_m = me
    return ("dve", x_m)


def build_l2moe_test_program():
    import contextlib
    nc = bass.Bass("TRN2", target_bir_lowering=False)
    x = nc.dram_tensor("x", [SEQ, DM], F32, kind="ExternalInput").ap()
    wg_cols = nc.dram_tensor("wg_cols", [128, 64], F32, kind="ExternalInput").ap()
    bg_row = nc.dram_tensor("bg_row", [1, 8], F32, kind="ExternalInput").ap()
    w1_s = nc.dram_tensor("w1_s", [8, 8, 128, 4096], F32, kind="ExternalInput").ap()
    w2_s = nc.dram_tensor("w2_s", [8, 8, 128, 4096], F32, kind="ExternalInput").ap()
    b1_cols = nc.dram_tensor("b1_cols", [8, 128, 32], F32, kind="ExternalInput").ap()
    b2_cols = nc.dram_tensor("b2_cols", [8, 128, 8], F32, kind="ExternalInput").ap()
    w1b_s = nc.dram_tensor("w1b_scr", [8, 8, 128, 4096], BF16).ap()
    w2b_s = nc.dram_tensor("w2b_scr", [8, 8, 128, 4096], BF16).ap()
    l0w1b = nc.dram_tensor("l0w1b_scr", [8, 128, 4096], BF16).ap()
    l0w2b = nc.dram_tensor("l0w2b_scr", [8, 128, 4096], BF16).ap()
    l1fcb = nc.dram_tensor("l1fcb_scr", [8, 128, 4096], BF16).ap()
    l1prb = nc.dram_tensor("l1prb_scr", [8, 128, 4096], BF16).ap()
    ident = nc.dram_tensor("cc_ident", [128, 128], BF16, kind="ExternalInput").ap()
    ones = nc.dram_tensor("cc_ones", [128, 128], BF16, kind="ExternalInput").ap()
    onesf = nc.dram_tensor("cc_onesf", [128, 128], F32, kind="ExternalInput").ap()
    sel8 = nc.dram_tensor("cc_sel8", [8, 1024], BF16, kind="ExternalInput").ap()
    xo = nc.dram_tensor("x_out", [SEQ, DM], F32, kind="ExternalOutput").ap()

    with contextlib.ExitStack() as stack:
        B = alloc_common(nc, stack, {"ffn", "l2moe"})
        sems = {n: stack.enter_context(nc.semaphore(n)) for n in SEM_NAMES}
        block = stack.enter_context(nc.Block())

        def schedule(S):
            cm = emit_load_consts(S, B, [
                (ident[:, :], B.IDENT[:, :]), (ones[:, :], B.ONES[:, :]),
                (onesf[:, :], B.ONESF[:, :]), (wg_cols[:, :], B.WGC[:, :]),
                (bg_row[:, :], B.BGR[0:1, 0:8]), (sel8[:, :], B.SELE[:, :])])
            xr = emit_load_x(S, B, x)
            S.wait("vector", *cm)
            S.wait("scalar", *cm)
            S.wait("tensor", *cm)
            Wd = NS(w1_s=w1_s, w2_s=w2_s, w1b_s=w1b_s, w2b_s=w2b_s,
                    b1_cols=b1_cols, b2_cols=b2_cols)
            x_m = emit_l2moe(S, B, Wd, xr, cached=[False, False])
            emit_store_x(S, B, xo, x_m)

        _finish_program(nc, B, sems, block, schedule)
    return nc


def host_prep_l2moe(Wg, bg, W1, b1, W2, b2):
    w1_s = np.stack([prep_lhsT_stream(np.ascontiguousarray(W1[e]), 512)
                     for e in range(8)])
    w2_s = np.stack([prep_lhsT_stream(np.ascontiguousarray(W2[e]), 128)
                     for e in range(8)])
    return {
        "wg_cols": np.ascontiguousarray(
            Wg.reshape(8, 128, 8).transpose(1, 0, 2).reshape(128, 64)),
        "bg_row": np.ascontiguousarray(bg.reshape(1, 8)),
        "w1_s": w1_s,
        "w2_s": w2_s,
        "b1_cols": np.ascontiguousarray(
            b1.reshape(8, 32, 128).transpose(0, 2, 1)),
        "b2_cols": np.ascontiguousarray(
            b2.reshape(8, 8, 128).transpose(0, 2, 1)),
    }


def host_prep_l1ffn(W_fc, W_proj):
    wfcT = np.ascontiguousarray(W_fc.T.astype(np.float32))
    wprT = np.ascontiguousarray(W_proj.T.astype(np.float32))
    return {
        "wfc_s": prep_lhsT_stream(wfcT, 512),
        "wpr_s": prep_lhsT_stream(wprT, 128),
    }


def host_consts():
    import ml_dtypes
    p = np.arange(128)[:, None]
    j = np.arange(512)[None, :]
    masks = np.concatenate(
        [(j >= p + 128 * r).astype(np.float32) for r in range(4)], axis=1)
    return {
        "cc_ident": np.eye(128, dtype=np.float32).astype(ml_dtypes.bfloat16),
        "cc_ones": np.ones((128, 128), dtype=np.float32).astype(ml_dtypes.bfloat16),
        "cc_onesf": np.ones((128, 128), dtype=np.float32),
        "cc_sel8": np.concatenate(
            [np.tile((np.arange(8) == e).astype(np.float32)[:, None], (1, 128))
             for e in range(8)], axis=1).astype(ml_dtypes.bfloat16),
        "cc_masks": masks.astype(ml_dtypes.bfloat16),
    }





NBF = 44672
NF = 4096


def _bind_phase(B, kind):
    bf = B.ARNBF
    f3 = B.ARNF
    o = [0]
    p = [0]

    def tb(cols, rows=128, r0=0):
        ap = bf[r0:r0 + rows, o[0]:o[0] + cols]
        o[0] += cols
        return ap

    def tf(cols, rows=128, r0=0):
        ap = f3[r0:r0 + rows, p[0]:p[0] + cols]
        p[0] += cols
        return ap

    if kind.startswith("attn"):
        B["QTP"] = tb(2048)
        B["KTP"] = tb(2048)
        B["VAUG"] = tb(2080)
        ot = tb(16384)
        B["OT"] = ot
        B["YF"] = ot
        B["PT"] = tb(1536)
        B["QSQ"] = tb(2048)
        B["aw_wbuf0"] = tb(3072)
        B["aw_wbuf1"] = tb(3072)
        B["MASKS"] = tb(2048)
        B["RCOS"] = tb(2048)
        B["RSIN"] = tb(2048)
        B["RSH"] = B["QSQ"]
        B["BVRB"] = tb(1024, rows=1)
        B["OUTT"] = tb(4096)
        B["aw_stage0"] = tf(2048)
        B["LNSQ"] = tf(1024)
        B["LNDISC"] = B["LNSQ"]
        izb = tf(512)
        B["IZB"] = izb
        B["IZ"] = B.ARNF[64:65, p[0] - 512:p[0]]
        B["BB"] = tf(2)
        B["B4"] = B.ARNF[0:1, p[0]:p[0] + 16]
        p[0] += 16
    elif kind == "l0ffn":
        B["H"] = tb(16384)
        B["w_wbuf0"] = tb(4096)
        B["w_wbuf1"] = tb(4096)
        B["ERALL"] = tb(8192, rows=1)
        B["TR"] = tb(2048, rows=1)
        B["UR"] = tb(2048, rows=1)
        B["LORA_T"] = tb(1024, rows=16)
        B["OUTT"] = tb(4096)
        B["w_stage0"] = tf(2048)
        B["LNSQ"] = tf(1024)
        B["LNDISC"] = B["LNSQ"]
    elif kind == "l1ffn":
        B["H"] = tb(16384)
        B["w_wbuf0"] = tb(4096)
        B["w_wbuf1"] = tb(4096)
        B["RELU"] = tb(1024)
        B["SPK"] = tb(512)
        B["OUTT"] = tb(4096)
        B["w_stage0"] = tf(2048)
        B["Y0"] = tf(512)
        B["MEM"] = tf(512)
        B["LNSQ"] = tf(1024)
        B["LNDISC"] = B["LNSQ"]
    elif kind == "l2moe":
        B["ACC"] = tb(16384)
        B["H"] = tb(16384)
        B["w_wbuf0"] = tb(4096)
        B["w_wbuf1"] = tb(4096)
        B["WFM"] = tb(2048)
        B["WTOK"] = tf(128)
        B["SELE"] = tb(1024, rows=8)
        B["WTOKB"] = tb(128)
        B["WBC"] = tb(512)
        B["w_stage0"] = tf(2048)
        B["LNSQ"] = tf(1024)
        B["LNDISC"] = B["LNSQ"]
    elif kind == "l3ffn":
        B["H3"] = tb(11264)
        yf = tb(16384)
        B["YF"] = yf
        B["HTK"] = yf
        B["w_wbuf0"] = tb(2816)
        B["w_wbuf1"] = tb(2816)
        B["QS1"] = tb(1408)
        B["QS2"] = tb(1408)
        B["SGN"] = tb(1408)
        B["OUTT"] = tb(4096)
        B["XN"] = tf(1408)
        B["QT"] = tf(1408)
        B["QSC"] = tf(176)
        B["QRS"] = tf(176)
        B["LNDISC"] = B["XN"]
    assert o[0] <= NBF, (kind, o[0])
    assert p[0] <= NF, (kind, p[0])


def emit_join(S):
    i = S.vector.memset(B_JOIN_DUMMY_V(S), 0.0)
    S.tinc(i, "dve", 1)
    i = S.scalar.memzero(B_JOIN_DUMMY_S(S))
    S.tinc(i, "act", 1)
    i = S.gpsimd.memset(B_JOIN_DUMMY_V(S), 0.0)
    S.tinc(i, "gp", 1)
    for w in ("sync", "tensor", "vector", "scalar", "gpsimd"):
        for sem in ("pe", "dve", "act", "gp", "wdma", "gdma"):
            S.wait(w, sem, S.cnt[sem])


def B_JOIN_DUMMY_V(S):
    return S._join_dummy


def B_JOIN_DUMMY_S(S):
    return S._join_dummy


def build_fused_program():
    import contextlib
    nc = bass.Bass("TRN2", target_bir_lowering=False)
    D = {}

    def inp(name, shape, dt=F32):
        D[name] = nc.dram_tensor(name, shape, dt, kind="ExternalInput").ap()
        return D[name]

    x = inp("x", [SEQ, DM])
    inp("cc_ident", [128, 128], BF16)
    inp("cc_ones", [128, 128], BF16)
    inp("cc_onesf", [128, 128], F32)
    inp("cc_masks", [128, 2048], BF16)
    inp("cc_rcos", [128, SEQ], BF16)
    inp("cc_rsin", [128, SEQ], BF16)
    inp("cc_sel8", [8, 1024], BF16)
    inp("l0f_w1s", [8, 128, 8 * 512])
    inp("l0f_w2s", [8, 128, 32 * 128])
    inp("l0f_b1c", [128, 32])
    inp("l0f_b2c", [128, 8])
    inp("l0f_scale", [1, 1])
    inp("a0_qkvs", [8, 128, 3072])
    inp("a0_wos", [8, 128, 1024])
    inp("l1f_wfcs", [8, 128, 8 * 512])
    inp("l1f_wprs", [8, 128, 32 * 128])
    inp("a1_qkvs", [8, 128, 3072])
    inp("a1_wos", [8, 128, 1024])
    inp("a1_bqc", [128, 8])
    inp("a1_bkc", [128, 8])
    inp("a1_bvr", [1, 1024])
    inp("a1_boc", [128, 8])
    inp("l2m_wgc", [128, 64])
    inp("l2m_bgr", [1, 8])
    inp("l2m_w1s", [8, 8, 128, 4096])
    inp("l2m_w2s", [8, 8, 128, 4096])
    inp("l2m_b1c", [8, 128, 32])
    inp("l2m_b2c", [8, 128, 8])
    inp("a2_qkvs", [8, 128, 3072])
    inp("a2_wos", [8, 128, 1024])
    inp("a3_qkvs", [8, 128, 3072])
    inp("a3_wos", [8, 128, 1024])
    inp("a3_bqkvc", [128, 24])
    inp("a3_boc", [128, 8])
    inp("a3_lnwc", [128, 8])
    inp("a3_lnbc", [128, 8])
    inp("l3f_wg", [2816, 1024])
    inp("l3f_wu", [2816, 1024])
    inp("l3f_wd", [1024, 2816])
    g_s = nc.dram_tensor("g_scr", [22, 128, 1024], BF16).ap()
    u_s = nc.dram_tensor("u_scr", [22, 128, 1024], BF16).ap()
    d_s = nc.dram_tensor("d_scr", [8, 128, 2816], BF16).ap()
    w1b_s = nc.dram_tensor("w1b_scr", [8, 8, 128, 4096], BF16).ap()
    w2b_s = nc.dram_tensor("w2b_scr", [8, 8, 128, 4096], BF16).ap()
    l0w1b = nc.dram_tensor("l0w1b_scr", [8, 128, 4096], BF16).ap()
    l0w2b = nc.dram_tensor("l0w2b_scr", [8, 128, 4096], BF16).ap()
    l1fcb = nc.dram_tensor("l1fcb_scr", [8, 128, 4096], BF16).ap()
    l1prb = nc.dram_tensor("l1prb_scr", [8, 128, 4096], BF16).ap()
    xo = nc.dram_tensor("x_out", [SEQ, DM], F32, kind="ExternalOutput").ap()

    with contextlib.ExitStack() as stack:
        B = NS()

        def sb(name, shape, dt):
            B[name] = stack.enter_context(nc.sbuf_tensor(name, shape, dt))

        def psum(name, shape, dt):
            B[name] = stack.enter_context(nc.psum_tensor(name, shape, dt))

        sb("X", [128, NT * DM], F32)
        sb("YT", [128, ND * SEQ], BF16)
        sb("YROLL", [128, 2 * DM], BF16)
        for nm in ("S1", "S2", "MU", "MUSQ", "VAR", "SD", "RSTD"):
            sb(nm, [128, NT], F32)
        sb("IDENT", [128, 128], BF16)
        sb("ONES", [128, 128], BF16)
        sb("ONESF", [128, 128], F32)
        sb("B1C", [128, 32], F32)
        sb("B2C", [128, 8], F32)
        sb("SCALE", [1, 1], F32)
        sb("BQC", [128, 8], F32)
        sb("BKC", [128, 8], F32)
        sb("BOC", [128, 8], F32)
        sb("BOC3", [128, 8], F32)
        sb("WGC", [128, 64], F32)
        sb("WGCB", [128, 64], BF16)
        sb("BGR", [1, 8], F32)
        sb("MB1C", [128, 32], F32)
        sb("MB2C", [128, 8], F32)
        sb("BQKVC", [128, 24], F32)
        sb("LNWC", [128, 8], F32)
        sb("LNBC", [128, 8], F32)
        sb("BQE", [128, 1], F32)
        sb("BKE", [128, 1], F32)
        sb("VBC", [128, 1], BF16)
        sb("BVRD", [1, 128], BF16)
        sb("ARNBF", [128, NBF], BF16)
        sb("ARNF", [128, NF], F32)
        sb("JD", [128, 8], F32)
        for i in range(6):
            psum(f"PSF{i}", [128, 512], F32)
        psum("PSB0", [128, 1024], BF16)
        psum("PSB1", [128, 1024], BF16)
        sems = {n: stack.enter_context(nc.semaphore(n)) for n in SEM_NAMES}
        block = stack.enter_context(nc.Block())

        def schedule(S):
            S._join_dummy = B.JD[0:1, 0:1]
            B["psf_pool"] = PsumBank([B[f"PSF{i}"] for i in range(6)])
            B["psb_pool"] = PsumBank([B["PSB0"], B["PSB1"]])
            consts = [
                (D["cc_ident"][:, :], B.IDENT[:, :]),
                (D["cc_ones"][:, :], B.ONES[:, :]),
                (D["cc_onesf"][:, :], B.ONESF[:, :]),
                (D["l0f_b1c"][:, :], B.B1C[:, :]),
                (D["l0f_b2c"][:, :], B.B2C[:, :]),
                (D["l0f_scale"][:, :], B.SCALE[0:1, 0:1]),
                (D["a1_bqc"][:, :], B.BQC[:, :]),
                (D["a1_bkc"][:, :], B.BKC[:, :]),
                (D["a1_boc"][:, :], B.BOC[:, :]),
                (D["l2m_wgc"][:, :], B.WGC[:, :]),
                (D["l2m_bgr"][:, :], B.BGR[0:1, 0:8]),
                (D["a3_bqkvc"][:, :], B.BQKVC[:, :]),
                (D["a3_boc"][:, :], B.BOC3[:, :]),
                (D["a3_lnwc"][:, :], B.LNWC[:, :]),
                (D["a3_lnbc"][:, :], B.LNBC[:, :]),
            ]
            cm = emit_load_consts(S, B, consts)
            xr = emit_load_x(S, B, x)
            for e in ("vector", "scalar", "tensor", "gpsimd"):
                S.wait(e, *cm)
                S.wait(e, *xr)

            def lnw_cast(S2, wbuf, stage, off, w):
                last = None
                for kk in range(w // 128):
                    kt = (off // 128) + kk
                    last = S2.vector.tensor_scalar(
                        wbuf[:, ts_(kt, 128)], stage[:, ts_(kk, 128)],
                        B.LNWC[:, (kt % 8):(kt % 8) + 1], None, op0=ALU.mult)
                return last

            x_m = xr
            moe_cached = [False, False]
            l3_prepped = [False]
            l0_cached = [False]
            l1_cached = [False]
            seq = FUSED_SEQ
            for kind in seq:
                _bind_phase(B, kind)
                pl = []
                if kind.startswith("attn"):
                    pl.append((D["cc_masks"][:, :], B.MASKS[:, :]))
                    if kind == "attn_l2":
                        pl.append((D["cc_rcos"][:, :], B.RCOS[:, :]))
                        pl.append((D["cc_rsin"][:, :], B.RSIN[:, :]))
                    if kind == "attn_l1":
                        pl.append((D["a1_bvr"][0:1, :], B.BVRB[0:1, :]))
                if kind == "l2moe":
                    pl.append((D["cc_sel8"][:, :], B.SELE[0:8, :]))
                if pl:
                    pm = emit_load_consts(S, B, pl)
                    for e in ("vector", "scalar", "tensor", "gpsimd"):
                        S.wait(e, *pm)
                if kind == "l0ffn":
                    Wd = NS(w1_s=D["l0f_w1s"], w2_s=D["l0f_w2s"],
                            b1_col=B.B1C, b2_col=B.B2C,
                            w1b_s=l0w1b, w2b_s=l0w2b)
                    x_m = emit_l0ffn(S, B, Wd, x_m, cached=l0_cached)
                elif kind == "l1ffn":
                    Wd = NS(wfc_s=D["l1f_wfcs"], wpr_s=D["l1f_wprs"],
                            wfcb_s=l1fcb, wprb_s=l1prb)
                    x_m = emit_l1ffn(S, B, Wd, x_m, cached=l1_cached)
                elif kind == "l2moe":
                    Wd = NS(w1_s=D["l2m_w1s"], w2_s=D["l2m_w2s"],
                            w1b_s=w1b_s, w2b_s=w2b_s,
                            b1_cols=D["l2m_b1c"], b2_cols=D["l2m_b2c"])
                    x_m = emit_l2moe(S, B, Wd, x_m, cached=moe_cached)
                elif kind == "l3ffn":
                    Wd = NS(wg_nat=D["l3f_wg"], wu_nat=D["l3f_wu"],
                            wd_nat=D["l3f_wd"], g_s=g_s, u_s=u_s, d_s=d_s)
                    x_m = emit_l3ffn(S, B, Wd, x_m, do_prep=not l3_prepped[0])
                    l3_prepped[0] = True
                elif kind == "attn_l0":
                    cfg = NS(causal=True)
                    Wd = NS(qkv_s=D["a0_qkvs"], wo_s=D["a0_wos"])
                    x_m = emit_attn(S, B, Wd, x_m, cfg)
                elif kind == "attn_l1":
                    cfg = NS(causal=True, qk_bias=True, v_bias_row="BVRB16",
                             o_bias=True)
                    Wd = NS(qkv_s=D["a1_qkvs"], wo_s=D["a1_wos"],
                            bq_col=B.BQC, bk_col=B.BKC, bo_col=B.BOC)
                    x_m = emit_attn(S, B, Wd, x_m, cfg)
                elif kind == "attn_l2":
                    cfg = NS(causal=True, rope=True,
                             rope_mask=[i + 1 if i % 2 == 0 else i - 1
                                        for i in range(32)])
                    Wd = NS(qkv_s=D["a2_qkvs"], wo_s=D["a2_wos"])
                    x_m = emit_attn(S, B, Wd, x_m, cfg)
                elif kind == "attn_l3":
                    cfg = NS(causal=False, inner_residual=True,
                             dev_qkv_bias=True, v_bias_row="BVRD",
                             o_bias=True, cast_fn=lnw_cast)
                    Wd = NS(qkv_s=D["a3_qkvs"], wo_s=D["a3_wos"],
                            bqkv_col=B.BQKVC, bo_col=B.BOC3)
                    x_m = emit_attn(S, B, Wd, x_m, cfg)
                emit_join(S)
            emit_store_x(S, B, xo, x_m)

        _finish_program(nc, B, sems, block, schedule)
    return nc


FUSED = True
FUSED_SEQ = ["l0ffn", "attn_l0", "l0ffn",
             "l1ffn", "attn_l1", "l1ffn",
             "l2moe", "attn_l2", "l2moe",
             "l3ffn", "attn_l3", "l3ffn"]

_PROG_CACHE = {}


def _get_prog(kind):
    if kind in _PROG_CACHE:
        return _PROG_CACHE[kind]
    if kind == "fused":
        nc = build_fused_program()
        _PROG_CACHE[kind] = nc
        return nc
    if kind == "l0ffn":
        nc = build_l0ffn_test_program()
    elif kind == "l1ffn":
        nc = build_l1ffn_test_program()
    elif kind == "l3ffn":
        nc = build_l3ffn_test_program()
    elif kind == "l2moe":
        nc = build_l2moe_test_program()
    elif kind == "attn_l0":
        nc = build_attn_test_program(NS(causal=True))
    elif kind == "attn_l1":
        nc = build_attn_test_program(NS(causal=True, qk_bias=True,
                                        v_bias_row="bv_row", o_bias=True))
    elif kind == "attn_l2":
        nc = build_attn_test_program(NS(
            causal=True, rope=True,
            rope_mask=[i + 1 if i % 2 == 0 else i - 1 for i in range(32)]))
    elif kind == "attn_l3":
        nc = build_l3attn_test_program()
    else:
        raise KeyError(kind)
    _PROG_CACHE[kind] = nc
    return nc


_CONST_KEYS = {
    "l0ffn": ("cc_ident",),
    "l1ffn": ("cc_ident",),
    "l3ffn": ("cc_ident",),
    "l2moe": ("cc_ident", "cc_ones", "cc_onesf", "cc_sel8"),
    "attn_l0": ("cc_ident", "cc_ones", "cc_onesf", "cc_masks"),
    "attn_l1": ("cc_ident", "cc_ones", "cc_onesf", "cc_masks"),
    "attn_l2": ("cc_ident", "cc_ones", "cc_onesf", "cc_masks",
                "cc_rcos", "cc_rsin"),
    "attn_l3": ("cc_ident", "cc_ones", "cc_onesf", "cc_masks"),
}


def kernel(**inp):
    consts = host_consts()
    consts.update(host_rope_tables())

    f32 = np.float32
    T = lambda a: np.ascontiguousarray(np.asarray(a, dtype=f32).T)
    A = lambda a: np.ascontiguousarray(np.asarray(a, dtype=f32))

    wmaps = {}
    wmaps["l0ffn"] = host_prep_l0ffn(A(inp["L0_ffn_W1"]), A(inp["L0_ffn_b1"]),
                                     A(inp["L0_ffn_W2"]), A(inp["L0_ffn_b2"]),
                                     np.asarray(inp["L0_ffn_scale"], dtype=f32))
    wmaps["attn_l0"] = host_prep_attn(T(inp["L0_attn_Wq"]), T(inp["L0_attn_Wk"]),
                                      T(inp["L0_attn_Wv"]), T(inp["L0_attn_Wo"]))
    wmaps["l1ffn"] = host_prep_l1ffn(A(inp["L1_ffn_W_fc"]), A(inp["L1_ffn_W_proj"]))
    wmaps["attn_l1"] = host_prep_attn(
        A(inp["L1_attn_wq"]), A(inp["L1_attn_wk"]), A(inp["L1_attn_wv"]),
        A(inp["L1_attn_wo"]), bq=A(inp["L1_attn_bq"]), bk=A(inp["L1_attn_bk"]),
        bv=A(inp["L1_attn_bv"]), bo=A(inp["L1_attn_bo"]))
    wmaps["l2moe"] = host_prep_l2moe(A(inp["L2_ffn_Wg"]), A(inp["L2_ffn_bg"]),
                                     A(inp["L2_ffn_W1"]), A(inp["L2_ffn_b1"]),
                                     A(inp["L2_ffn_W2"]), A(inp["L2_ffn_b2"]))
    wmaps["attn_l2"] = host_prep_attn(T(inp["L2_attn_wq"]), T(inp["L2_attn_wk"]),
                                      T(inp["L2_attn_wv"]), T(inp["L2_attn_wo"]))
    wmaps["attn_l3"] = host_prep_l3attn(A(inp["L3_attn_W_qkv"]),
                                        A(inp["L3_attn_b_qkv"]),
                                        A(inp["L3_attn_W_proj"]),
                                        A(inp["L3_attn_b_proj"]),
                                        A(inp["L3_attn_ln_w"]),
                                        A(inp["L3_attn_ln_b"]))
    wmaps["l3ffn"] = {"wg_nat": A(inp["L3_ffn_w_gate"]),
                      "wu_nat": A(inp["L3_ffn_w_up"]),
                      "wd_nat": A(inp["L3_ffn_w_down"])}

    x0 = np.asarray(inp["L0_attn_x"], dtype=f32)
    xs = [np.ascontiguousarray(x0[b]) for b in range(NCORES)]

    def run(kind):
        nc = _get_prog(kind)
        cst = {k: consts[k] for k in _CONST_KEYS[kind]}
        in_maps = [dict(x=xs[b], **wmaps[kind], **cst) for b in range(NCORES)]
        res = run_bass_kernel_spmd(nc, in_maps, core_ids=list(range(NCORES)))
        for b in range(NCORES):
            xs[b] = res.results[b]["x_out"]

    if FUSED:
        nc = _get_prog("fused")
        im = {}
        im.update({k: consts[k] for k in ("cc_ident", "cc_ones", "cc_onesf",
                                          "cc_masks", "cc_rcos", "cc_rsin",
                                          "cc_sel8")})
        m = wmaps["l0ffn"]
        im["l0f_w1s"] = m["w1_s"]; im["l0f_w2s"] = m["w2_s"]
        im["l0f_b1c"] = m["b1_col"]; im["l0f_b2c"] = m["b2_col"]
        im["l0f_scale"] = m["scale_in"]
        m = wmaps["attn_l0"]
        im["a0_qkvs"] = m["qkv_s"]; im["a0_wos"] = m["wo_s"]
        m = wmaps["l1ffn"]
        im["l1f_wfcs"] = m["wfc_s"]; im["l1f_wprs"] = m["wpr_s"]
        m = wmaps["attn_l1"]
        im["a1_qkvs"] = m["qkv_s"]; im["a1_wos"] = m["wo_s"]
        im["a1_bqc"] = m["bq_col"]; im["a1_bkc"] = m["bk_col"]
        im["a1_bvr"] = m["bv_row"]; im["a1_boc"] = m["bo_col"]
        m = wmaps["l2moe"]
        im["l2m_wgc"] = m["wg_cols"]; im["l2m_bgr"] = m["bg_row"]
        im["l2m_w1s"] = m["w1_s"]; im["l2m_w2s"] = m["w2_s"]
        im["l2m_b1c"] = m["b1_cols"]; im["l2m_b2c"] = m["b2_cols"]
        m = wmaps["attn_l2"]
        im["a2_qkvs"] = m["qkv_s"]; im["a2_wos"] = m["wo_s"]
        m = wmaps["attn_l3"]
        im["a3_qkvs"] = m["qkv_s"]; im["a3_wos"] = m["wo_s"]
        im["a3_bqkvc"] = m["bqkv_col"]; im["a3_boc"] = m["bo_col"]
        im["a3_lnwc"] = m["lnw_col"]; im["a3_lnbc"] = m["lnb_col"]
        m = wmaps["l3ffn"]
        im["l3f_wg"] = m["wg_nat"]; im["l3f_wu"] = m["wu_nat"]
        im["l3f_wd"] = m["wd_nat"]
        in_maps = [dict(x=xs[b], **im) for b in range(NCORES)]
        res = run_bass_kernel_spmd(nc, in_maps, core_ids=list(range(NCORES)))
        return np.stack([res.results[b]["x_out"] for b in range(NCORES)]
                        ).astype(np.float32)

    seq = ["l0ffn", "attn_l0", "l0ffn",
           "l1ffn", "attn_l1", "l1ffn",
           "l2moe", "attn_l2", "l2moe",
           "l3ffn", "attn_l3", "l3ffn"]
    for kind in seq:
        run(kind)
    return np.stack(xs).astype(np.float32)


# version 05 (final)
# speedup vs baseline: 1.4126x; 1.1222x over previous
import math
import numpy as np
import concourse.bass as bass
import concourse.mybir as mybir
from concourse.bass_utils import run_bass_kernel_spmd

F32 = mybir.dt.float32
BF16 = mybir.dt.bfloat16
AF = mybir.ActivationFunctionType
ALU = mybir.AluOpType
AX = mybir.AxisListType

NCORES = 8
SEQ = 2048
DM = 1024
NT = SEQ // 128
ND = DM // 128
NHEAD = 16
DH = 64
LN_EPS = 1e-5
MAGIC = 12582912.0

ENG_SEM = {"tensor": "pe", "vector": "dve", "scalar": "act", "gpsimd": "gp"}
SEM_NAMES = ("wdma", "gdma", "adma", "pe", "dve", "act", "gp")


def ts_(i, n):
    return slice(i * n, (i + 1) * n)


def ots(base, i, n):
    return slice(base + i * n, base + (i + 1) * n)


class NS(dict):
    def __getattr__(self, k):
        return self[k]

    def __setattr__(self, k, v):
        self[k] = v


class _DummyIns:
    def then_inc(self, *a, **k):
        return self


class _DummyEng:
    def __getattr__(self, name):
        def f(*a, **k):
            return _DummyIns()
        return f


class Sched:

    def __init__(self, nc, active, eng, sems):
        self.nc = nc
        self.active = active
        self._eng = eng
        self.sems = sems
        self.cnt = {k: 0 for k in sems}
        d = _DummyEng()
        self.sync = eng if active == "sync" else d
        self.tensor = eng if active == "tensor" else d
        self.vector = eng if active == "vector" else d
        self.scalar = eng if active == "scalar" else d
        self.gpsimd = eng if active == "gpsimd" else d

    def on(self, engname):
        return self.active == engname

    def eng(self, engname):
        return getattr(self, engname)

    def tinc(self, ins, sem, k=1):
        ins.then_inc(self.sems[sem], k)
        self.cnt[sem] += k
        return self.cnt[sem]

    def wait(self, engname, sem, val=None):
        if val is None:
            val = self.cnt[sem]
        if val <= 0:
            return
        if self.active == engname:
            self._eng.wait_ge(self.sems[sem], val)

    def sf(self, engname, ins):
        sem = ENG_SEM[engname]
        m = self.tinc(ins, sem, 1)
        self.wait(engname, sem, m)
        return m



class WStream:

    def __init__(self, B, tag, cast_eng="vector", single_stage=False):
        self.B = B
        self.tag = tag
        self.ce = cast_eng
        self.i = 0
        self.single_stage = single_stage
        self.stage_free = [0, 0]
        self.stage_guard = None
        self.wbuf_free = [0, 0]
        self.wbuf_extra = [None, None]
        self.ready = [0, 0]
        self.last_dma = 0
        self.wb_last = 0

    def load(self, S, dram_ap, cols, cast_fn=None, stage_hook=None):
        B = self.B
        sl = self.i % 2
        self.i += 1
        S.wait(self.ce, "pe", self.wbuf_free[sl])
        if self.wbuf_extra[sl] is not None:
            S.wait(self.ce, *self.wbuf_extra[sl])
            self.wbuf_extra[sl] = None
        stage = B[f"{self.tag}_stage0"]
        scols = stage.shape[1] if hasattr(stage, "shape") else 2048
        pieces = []
        off = 0
        while off < cols:
            w = min(scols, cols - off)
            pieces.append((off, w))
            off += w
        mr = 0
        for off, w in pieces:
            S.wait("sync", ENG_SEM[self.ce], self.stage_free[0])
            if self.stage_guard is not None:
                S.wait("sync", *self.stage_guard)
                self.stage_guard = None
            ins = S.sync.dma_start(stage[:, 0:w], dram_ap[:, off:off + w])
            mdma = S.tinc(ins, "wdma", 16)
            self.last_dma = mdma
            S.wait(self.ce, "wdma", mdma)
            if stage_hook is not None:
                mh = stage_hook(S, stage, off, w, mdma)
                self.stage_guard = ("pe", mh)
            if cast_fn is not None:
                ins = cast_fn(S, B[f"{self.tag}_wbuf{sl}"], stage, off, w)
            else:
                ins = S.eng(self.ce).tensor_copy(
                    B[f"{self.tag}_wbuf{sl}"][:, off:off + w], stage[:, 0:w])
            mr = S.tinc(ins, ENG_SEM[self.ce], 1)
            self.stage_free[0] = mr
        self.ready[sl] = mr
        return sl, mr

    def load_bf16(self, S, dram_ap, cols):
        sl = self.i % 2
        self.i += 1
        S.wait("sync", "pe", self.wbuf_free[sl])
        S.wait("sync", "gdma", self.wb_last)
        if self.wbuf_extra[sl] is not None:
            S.wait("sync", *self.wbuf_extra[sl])
            self.wbuf_extra[sl] = None
        ins = S.sync.dma_start(self.B[f"{self.tag}_wbuf{sl}"][:, 0:cols], dram_ap)
        m = S.tinc(ins, "wdma", 16)
        self.last_dma = m
        return sl, m

    def wbuf(self, sl):
        return self.B[f"{self.tag}_wbuf{sl}"]

    def release(self, S, sl, pe_milestone):
        self.wbuf_free[sl] = pe_milestone



def ws_load_cached(S, B, ws, f32_ap, bf_ap, cols, use_bf):
    if use_bf:
        sl, m = ws.load_bf16(S, bf_ap, cols)
        S.wait("tensor", "wdma", m)
    else:
        sl, m = ws.load(S, f32_ap, cols)
        S.wait("tensor", ENG_SEM[ws.ce], m)
        S.wait("gpsimd", ENG_SEM[ws.ce], m)
        iw = S.gpsimd.dma_start(bf_ap, ws.wbuf(sl)[:, 0:cols])
        mwb = S.tinc(iw, "gdma", 16)
        ws.wbuf_extra[sl] = ("gdma", mwb)
        ws.wb_last = mwb
    return sl


def emit_load_x(S, B, x_hbm):
    for t in range(NT):
        ins = S.sync.dma_start(B.X[:, ts_(t, DM)], x_hbm[ts_(t, 128), :])
        S.tinc(ins, "wdma", 16)
    return ("wdma", S.cnt["wdma"])


def emit_store_x(S, B, out_hbm, x_ready):
    sem, m = x_ready
    S.wait("gpsimd", sem, m)
    base = S.cnt["gdma"]
    for t in range(NT):
        ins = S.gpsimd.dma_start(out_hbm[ts_(t, 128), :], B.X[:, ts_(t, DM)])
        S.tinc(ins, "gdma", 16)
    S.wait("gpsimd", "gdma", base + 16 * NT)


class PsumBank:

    def __init__(self, handles):
        self.h = handles
        self.free = [("dve", 0)] * len(handles)
        self.i = 0

    def next(self, S, waiter="tensor"):
        k = self.i % len(self.h)
        self.i += 1
        sem, m = self.free[k]
        S.wait(waiter, sem, m)
        return k, self.h[k]

    def done(self, k, sem, m):
        self.free[k] = (sem, m)


def emit_ln(S, B, x_ready, full_y=False, yf=None, src=None, make_yt=True,
            fp32_add_to_x=False):
    if src is None:
        src = B.X
    sem, m0 = x_ready
    S.wait("vector", sem, m0)
    S.wait("scalar", sem, m0)
    last_v = last_s = None
    for t in range(NT):
        last_v = S.vector.reduce_sum(B.S1[:, t:t + 1], src[:, ts_(t, DM)], axis=AX.X)
        last_s = S.scalar.activation(B.LNDISC[:, 0:DM], src[:, ts_(t, DM)], AF.Square,
                                     accum_out=B.S2[:, t:t + 1])
    S.sf("vector", last_v)
    ms = S.tinc(last_s, "act", 1)
    S.wait("vector", "act", ms)
    i1 = S.vector.tensor_scalar(B.MU[:, 0:NT], B.S1[:, 0:NT], 1.0 / DM, None, op0=ALU.mult)
    i2 = S.vector.tensor_scalar(B.VAR[:, 0:NT], B.S2[:, 0:NT], 1.0 / DM, None, op0=ALU.mult)
    S.sf("vector", i2)
    i3 = S.vector.tensor_tensor(B.MUSQ[:, 0:NT], B.MU[:, 0:NT], B.MU[:, 0:NT], op=ALU.mult)
    S.sf("vector", i3)
    i4 = S.vector.scalar_tensor_tensor(B.VAR[:, 0:NT], B.VAR[:, 0:NT], LN_EPS,
                                       B.MUSQ[:, 0:NT], op0=ALU.add, op1=ALU.subtract)
    m4 = S.tinc(i4, "dve", 1)
    S.wait("scalar", "dve", m4)
    i5 = S.scalar.activation(B.SD[:, 0:NT], B.VAR[:, 0:NT], AF.Sqrt)
    m5 = S.tinc(i5, "act", 1)
    S.wait("vector", "act", m5)
    i6 = S.vector.reciprocal(B.RSTD[:, 0:NT], B.SD[:, 0:NT])
    S.sf("vector", i6)

    yroll_pe = [0, 0]
    for t in range(NT):
        if full_y:
            ysrc = yf[:, ts_(t, DM)]
        else:
            S.wait("vector", "pe", yroll_pe[t % 2])
            ysrc = B.YROLL[:, ts_(t % 2, DM)]
        if fp32_add_to_x:
            iz = S.vector.tensor_scalar(B.LNDISC[:, 0:DM], src[:, ts_(t, DM)],
                                        B.MU[:, t:t + 1], B.RSTD[:, t:t + 1],
                                        op0=ALU.subtract, op1=ALU.mult)
            S.sf("vector", iz)
            ic = S.vector.tensor_copy(ysrc, B.LNDISC[:, 0:DM])
            S.tinc(ic, "dve", 1)
            ia = S.vector.tensor_tensor(B.X[:, ts_(t, DM)], B.X[:, ts_(t, DM)],
                                        B.LNDISC[:, 0:DM], op=ALU.add)
            S.sf("vector", ia)
            my = S.cnt["dve"]
            continue
        iy = S.vector.tensor_scalar(ysrc, src[:, ts_(t, DM)],
                                    B.MU[:, t:t + 1], B.RSTD[:, t:t + 1],
                                    op0=ALU.subtract, op1=ALU.mult)
        my = S.tinc(iy, "dve", 1)
        if make_yt:
            S.wait("tensor", "dve", my)
            mlast = emit_transpose_tile_to_feat(S, B, ysrc, B.YT, t)
            if not full_y:
                yroll_pe[t % 2] = mlast
    return NS(yt=("dve", S.cnt["dve"]))


def emit_transpose_tile_to_feat(S, B, src_tile_ap, yt_handle, t):
    yt3 = yt_handle[:, :].rearrange("p (d q) -> p d q", q=SEQ)
    mlast = 0
    for pack in range(2):
        kb, ps = B.psb_pool.next(S, waiter="tensor")
        lastmm = None
        for j in range(4):
            d = pack * 4 + j
            lastmm = S.tensor.transpose(
                ps[:, ts_(j, 128)],
                src_tile_ap[:, ts_(d, 128)],
                B.IDENT[:, 0:128])
        mp = S.tinc(lastmm, "pe", 1)
        mlast = mp
        S.wait("vector", "pe", mp)
        src3 = ps[:, 0:512].rearrange("p (b f) -> p b f", f=128)
        dst3 = yt3[:, pack * 4:(pack + 1) * 4, t * 128:(t + 1) * 128]
        ev = S.vector.tensor_copy(dst3, src3)
        me = S.tinc(ev, "dve", 1)
        B.psb_pool.done(kb, "dve", me)
    return mlast


def emit_out_transpose_add(S, B, outt_handle, c, scale_note=None):
    o3 = outt_handle[:, 0:8 * 512].rearrange("p (d q) -> p d q", q=512)
    mlast = 0
    for tsub in range(4):
        t = 4 * c + tsub
        for pack in range(2):
            kb, ps = B.psb_pool.next(S, waiter="tensor")
            lastmm = None
            for j in range(4):
                d = pack * 4 + j
                lastmm = S.tensor.transpose(
                    ps[:, ts_(j, 128)],
                    o3[:, d, ts_(tsub, 128)],
                    B.IDENT[:, 0:128])
            mp = S.tinc(lastmm, "pe", 1)
            S.wait("vector", "pe", mp)
            dst = B.X[:, t * DM + pack * 512: t * DM + (pack + 1) * 512]
            ev = S.vector.tensor_tensor(dst, dst, ps[:, 0:512], op=ALU.add)
            me = S.tinc(ev, "dve", 1)
            B.psb_pool.done(kb, "dve", me)
            mlast = me
    return ("dve", mlast)



def emit_l1ffn(S, B, W, x_ready, cached=None):
    r = emit_ln(S, B, x_ready)
    yt_sem, yt_m = r.yt
    ws = WStream(B, "w", cast_eng="vector", single_stage=True)
    S.wait("tensor", yt_sem, yt_m)
    has_cache = W.get("wfcb_s") is not None

    x_m = 0
    relu_free = [0, 0]
    y0_free = [0, 0]
    outt_pe = 0
    for c in range(4):
        for j in range(8):
            if has_cache:
                use_bf = cached[0] or c > 0
                sl = ws_load_cached(S, B, ws, W.wfc_s[j, :, :],
                                    W.wfcb_s[j, :, :], 8 * 512, use_bf)
            else:
                sl, mr = ws.load(S, W.wfc_s[j, :, :], 8 * 512)
                S.wait("tensor", ENG_SEM[ws.ce], mr)
            wb = ws.wbuf(sl)
            lastmm_for_slot = None
            for sub in range(4):
                fblk = 4 * j + sub
                kb, ps = B.psf_pool.next(S, waiter="tensor")
                lastmm = None
                for kt in range(ND):
                    lhsT = wb[:, kt * 512 + sub * 128: kt * 512 + (sub + 1) * 128]
                    rhs = B.YT[:, kt * SEQ + c * 512: kt * SEQ + (c + 1) * 512]
                    lastmm = S.tensor.matmul(ps[:, 0:512], lhsT, rhs,
                                             start=(kt == 0), stop=(kt == ND - 1))
                mp = S.tinc(lastmm, "pe", 1)
                lastmm_for_slot = mp
                S.wait("scalar", "pe", mp)
                S.wait("scalar", "dve", relu_free[sub % 2])
                ir = S.scalar.activation(B.RELU[:, ts_(sub % 2, 512)], ps[:, 0:512], AF.Relu)
                ma = S.tinc(ir, "act", 1)
                B.psf_pool.done(kb, "act", ma)
                S.wait("vector", "act", ma)
                iq = S.vector.tensor_mul(B.H[:, ts_(fblk, 512)],
                                         B.RELU[:, ts_(sub % 2, 512)],
                                         B.RELU[:, ts_(sub % 2, 512)])
                relu_free[sub % 2] = S.tinc(iq, "dve", 1)
            ws.release(S, sl, lastmm_for_slot)
        h_done = S.cnt["dve"]

        S.wait("tensor", "dve", h_done)
        for j in range(8):
            if has_cache:
                use_bf = cached[0] or c > 0
                sl = ws_load_cached(S, B, ws, W.wpr_s[j, :, :],
                                    W.wprb_s[j, :, :], 32 * 128, use_bf)
            else:
                sl, mr = ws.load(S, W.wpr_s[j, :, :], 32 * 128)
                S.wait("tensor", ENG_SEM[ws.ce], mr)
            wb = ws.wbuf(sl)
            kb, ps = B.psf_pool.next(S, waiter="tensor")
            lastmm = None
            for kt in range(32):
                lhsT = wb[:, ts_(kt, 128)]
                rhs = B.H[:, ts_(kt, 512)]
                lastmm = S.tensor.matmul(ps[:, 0:512], lhsT, rhs,
                                         start=(kt == 0), stop=(kt == 31))
            mp = S.tinc(lastmm, "pe", 1)
            ws.release(S, sl, mp)
            S.wait("scalar", "pe", mp)
            sslot = 0
            S.wait("scalar", "dve", y0_free[sslot])
            if j == 0:
                S.wait("vector", "pe", outt_pe)
            ie = S.scalar.activation(B.Y0[:, ts_(sslot, 512)], ps[:, 0:512], AF.Copy)
            ma = S.tinc(ie, "act", 1)
            B.psf_pool.done(kb, "act", ma)
            S.wait("vector", "act", ma)
            y0 = B.Y0[:, ts_(sslot, 512)]
            mem = B.MEM[:, ts_(sslot, 512)]
            spk = B.SPK[:, ts_(sslot, 512)]
            acc = B.OUTT[:, ts_(j, 512)]
            i = S.vector.tensor_scalar(acc, y0, 1.0, None, op0=ALU.is_gt)
            S.sf("vector", i)
            prev_spk_ap = acc
            for st in range(2, 6):
                if st == 2:
                    i = S.vector.tensor_scalar(mem, y0, 1.9, None, op0=ALU.mult)
                else:
                    i = S.vector.scalar_tensor_tensor(mem, mem, 0.9, y0,
                                                      op0=ALU.mult, op1=ALU.add)
                S.sf("vector", i)
                i = S.vector.tensor_sub(mem, mem, prev_spk_ap)
                S.sf("vector", i)
                i = S.vector.tensor_scalar(spk, mem, 1.0, None, op0=ALU.is_gt)
                S.sf("vector", i)
                i = S.vector.tensor_add(acc, acc, spk)
                S.sf("vector", i)
                prev_spk_ap = spk
            i = S.vector.tensor_scalar(acc, acc, 0.1, None, op0=ALU.mult)
            S.sf("vector", i)
            y0_free[sslot] = S.cnt["dve"]
        S.wait("tensor", "dve", S.cnt["dve"])
        _, x_m = emit_out_transpose_add(S, B, B.OUTT, c)
    if cached is not None:
        cached[0] = True
    return ("dve", x_m)



def emit_attn(S, B, W, x_ready, cfg):
    full_y = bool(cfg.get("inner_residual"))
    if full_y:
        r1 = emit_ln(S, B, x_ready, full_y=True, yf=B.YF, make_yt=False,
                     fp32_add_to_x=True)
        r = emit_ln(S, B, r1.yt, src=B.YF)
    else:
        r = emit_ln(S, B, x_ready)
    yt_sem, yt_m = r.yt

    ws = WStream(B, "aw", cast_eng="vector", single_stage=True)
    wso = ws
    S.wait("tensor", yt_sem, yt_m)

    v4 = B.VAUG[:, 0:NT * 130].rearrange("p (t hh c) -> p t hh c", hh=2, c=65)
    im = S.vector.memset(v4[:, :, :, 64:65], 1.0)
    S.tinc(im, "dve", 1)

    SCB = ("PSF0", "PSF1", "PSF3", "PSF4")
    AVB = ("PSF2", "PSF5")
    sc_free = [("act", 0)] * 4
    av_free = [("dve", 0)] * 2
    u_free = ("dve", 0)
    pt_free = [0, 0, 0, 0]
    bb_free = [("act", 0), ("act", 0)]

    for p in range(8):
        if cfg.get("dev_qkv_bias"):
            ufree_now = u_free

            def bias_hook(S2, stage, off, w, mdma):
                S2.wait("tensor", "wdma", mdma)
                S2.wait("tensor", *ufree_now)
                lastm = 0
                for kk in range(w // 128):
                    kg = (off // 128) + kk
                    which = kg // 8
                    kt = kg % 8
                    ins = S2.tensor.matmul(
                        B.PSF3[:, which:which + 1], stage[:, ts_(kk, 128)],
                        B.LNBC[:, kt:kt + 1], start=(kg == 0), stop=(kg == 23))
                    lastm = S2.tinc(ins, "pe", 1)
                return lastm

            sl, mr = ws.load(S, W.qkv_s[p, :, :], 3072,
                             cast_fn=cfg.get("cast_fn"), stage_hook=bias_hook)
        else:
            sl, mr = ws.load(S, W.qkv_s[p, :, :], 3072,
                             cast_fn=cfg.get("cast_fn"))
        S.wait("tensor", ENG_SEM[ws.ce], mr)
        wb = ws.wbuf(sl)
        last_w_use = 0

        if cfg.get("dev_qkv_bias"):
            mp_b = ws.stage_guard[1]
            S.wait("vector", "pe", mp_b)
            for which, dstcol in ((0, B.BQE), (1, B.BKE), (2, B.VBC)):
                iv = S.vector.tensor_tensor(
                    dstcol[:, 0:1], B.PSF3[:, which:which + 1],
                    W.bqkv_col[:, 8 * which + p:8 * which + p + 1], op=ALU.add)
                mv = S.sf("vector", iv)
            u_free = ("dve", mv)
            kb, psb = B.psb_pool.next(S, waiter="tensor")
            S.wait("tensor", "dve", u_free[1])
            itr = S.tensor.transpose(psb[0:1, 0:128], B.VBC[:, 0:1], B.IDENT[:, 0:128])
            mp = S.tinc(itr, "pe", 1)
            S.wait("vector", "pe", mp)
            ib = S.vector.tensor_copy(B.BVRD[0:1, 0:128], psb[0:1, 0:128])
            mv = S.sf("vector", ib)
            B.psb_pool.done(kb, "dve", mv)

        if cfg.get("dev_qkv_bias"):
            qk_biases = (B.BQE[:, 0:1], B.BKE[:, 0:1])
        elif cfg.get("qk_bias"):
            qk_biases = (W.bq_col[:, p:p + 1], W.bk_col[:, p:p + 1])
        else:
            qk_biases = (None, None)
        for which, dst, bias_ap in (
            (0, B.QTP, qk_biases[0]),
            (1, B.KTP, qk_biases[1]),
        ):
            scale = 0.125 if which == 0 else None
            for c in range(4):
                k, ps = (c % 2), B[SCB[c % 2]]
                S.wait("tensor", *sc_free[k])
                lastmm = None
                for kt in range(ND):
                    lhsT = wb[:, ots(which * 1024, kt, 128)]
                    rhs = B.YT[:, ots(kt * SEQ, c, 512)]
                    lastmm = S.tensor.matmul(ps[:, 0:512], lhsT, rhs,
                                             start=(kt == 0), stop=(kt == ND - 1))
                mp = S.tinc(lastmm, "pe", 1)
                last_w_use = mp
                S.wait("vector", "pe", mp)
                if bias_ap is not None and scale is not None:
                    ie = S.vector.tensor_scalar(dst[:, ts_(c, 512)], ps[:, 0:512],
                                                bias_ap, 0.125,
                                                op0=ALU.add, op1=ALU.mult)
                elif bias_ap is not None:
                    ie = S.vector.tensor_scalar(dst[:, ts_(c, 512)], ps[:, 0:512],
                                                bias_ap, None, op0=ALU.add)
                elif scale is not None:
                    ie = S.vector.tensor_scalar(dst[:, ts_(c, 512)], ps[:, 0:512],
                                                0.125, None, op0=ALU.mult)
                else:
                    ie = S.vector.tensor_copy(dst[:, ts_(c, 512)], ps[:, 0:512])
                me = S.tinc(ie, "dve", 1)
                sc_free[k] = ("dve", me)
        m_qkt = S.cnt["dve"]

        if cfg.get("rope"):
            lastv = None
            for dst in (B.QTP, B.KTP):
                i1 = S.vector.stream_shuffle(B.RSH[:, 0:SEQ], dst[:, 0:SEQ],
                                             cfg["rope_mask"])
                S.sf("vector", i1)
                i2 = S.vector.tensor_mul(B.RSH[:, 0:SEQ], B.RSH[:, 0:SEQ], B.RSIN[:, 0:SEQ])
                S.sf("vector", i2)
                i3 = S.vector.tensor_mul(dst[:, 0:SEQ], dst[:, 0:SEQ], B.RCOS[:, 0:SEQ])
                S.sf("vector", i3)
                lastv = S.vector.tensor_add(dst[:, 0:SEQ], dst[:, 0:SEQ], B.RSH[:, 0:SEQ])
                S.sf("vector", lastv)
            m_qkt = S.cnt["dve"]

        S.wait("tensor", *sc_free[2])
        vbr = cfg.get("v_bias_row")
        for t in range(NT):
            S.wait("tensor", *u_free)
            lastmm = None
            for kt in range(ND):
                lhsT = B.YT[:, ots(kt * SEQ, t, 128)]
                rhs = wb[:, ots(2 * 1024, kt, 128)]
                lastmm = S.tensor.matmul(B.PSF3[:, 0:128], lhsT, rhs,
                                         start=(kt == 0),
                                         stop=(kt == ND - 1 and vbr is None))
            if vbr == "BVRD":
                lastmm = S.tensor.matmul(B.PSF3[:, 0:128], B.ONES[0:1, 0:128],
                                         B.BVRD[0:1, 0:128], start=False, stop=True)
            elif vbr == "BVRB16":
                lastmm = S.tensor.matmul(B.PSF3[:, 0:128], B.ONES[0:1, 0:128],
                                         B.BVRB[0:1, ts_(p, 128)], start=False, stop=True)
            elif vbr is not None:
                rowap = W[vbr][0:1, ts_(p, 128)]
                lastmm = S.tensor.matmul(B.PSF3[:, 0:128], B.ONESF[0:1, 0:128],
                                         rowap, start=False, stop=True)
            mp = S.tinc(lastmm, "pe", 1)
            last_w_use = mp
            S.wait("vector", "pe", mp)
            e1 = S.vector.tensor_copy(v4[:, t, 0, 0:64], B.PSF3[:, 0:64])
            e2 = S.vector.tensor_copy(v4[:, t, 1, 0:64], B.PSF3[:, 64:128])
            me = S.tinc(e2, "dve", 1)
            u_free = ("dve", me)
        ws.release(S, sl, last_w_use)
        m_v = S.cnt["dve"]

        sq_free_pe = 0
        for which, src in ((0, B.QTP), (1, B.KTP)):
            S.wait("scalar", "dve", m_qkt)
            S.wait("scalar", "pe", sq_free_pe)
            iq = S.scalar.activation(B.QSQ[:, 0:SEQ], src[:, 0:SEQ], AF.Square)
            msrc = S.tinc(iq, "act", 1)
            for c4 in range(4):
                S.wait("tensor", "act", msrc)
                S.wait("tensor", *u_free)
                mm = S.tensor.matmul(B.PSF3[0:1, 0:512], B.ONES[:, 0:1],
                                     B.QSQ[:, ts_(c4, 512)], start=True, stop=True)
                mp = S.tinc(mm, "pe", 1)
                sq_free_pe = mp
                S.wait("vector", "pe", mp)
                ir = S.vector.reduce_max(B.B4[0:1, which * 4 + c4:which * 4 + c4 + 1],
                                         B.PSF3[0:1, 0:512], axis=AX.X)
                mu = S.tinc(ir, "dve", 1)
                u_free = ("dve", mu)
        S.wait("vector", "dve", S.cnt["dve"])
        ilast = S.vector.reduce_max(B.B4[0:1, 8:9], B.B4[0:1, 0:4], axis=AX.X)
        S.sf("vector", ilast)
        ilast = S.vector.reduce_max(B.B4[0:1, 9:10], B.B4[0:1, 4:8], axis=AX.X)
        S.sf("vector", ilast)
        i = S.vector.tensor_tensor(B.B4[0:1, 10:11], B.B4[0:1, 8:9],
                                   B.B4[0:1, 9:10], op=ALU.mult)
        m = S.tinc(i, "dve", 1)
        S.wait("scalar", "dve", m)
        i = S.scalar.activation(B.B4[0:1, 11:12], B.B4[0:1, 10:11], AF.Sqrt)
        m = S.tinc(i, "act", 1)
        S.wait("vector", "act", m)
        i = S.vector.tensor_scalar(B.B4[0:1, 12:13], B.B4[0:1, 11:12],
                                   -1.04, -0.5, op0=ALU.mult, op1=ALU.add)
        mneg = S.tinc(i, "dve", 1)
        S.wait("tensor", "dve", mneg)
        S.wait("tensor", *u_free)
        mm = S.tensor.matmul(B.PSF3[:, 0:1], B.ONESF[0:1, :], B.B4[0:1, 12:13],
                             start=True, stop=True)
        mp = S.tinc(mm, "pe", 1)
        S.wait("vector", "pe", mp)
        S.wait("vector", *bb_free[p % 2])
        ib = S.vector.tensor_copy(B.BB[:, p % 2:p % 2 + 1], B.PSF3[:, 0:1])
        mbb = S.tinc(ib, "dve", 1)
        u_free = ("dve", mbb)
        bias_b = B.BB[:, p % 2:p % 2 + 1]

        S.wait("tensor", "dve", max(m_qkt, mbb, m_v))
        S.wait("tensor", *u_free)
        sc_free[2] = ("act", 0)
        ma_last = 0
        sci = 0
        for hh in range(2):
            base = hh * 64
            for c in range(4):
                tmax = (4 * c + 4) if cfg.get("causal") else NT
                avk = (hh * 4 + c) % 2
                avps = B[AVB[avk]]
                S.wait("tensor", *av_free[avk])
                LEAD = 3
                info = {}
                for ti in range(tmax + LEAD):
                    if ti < tmax:
                        t = ti
                        k = sci % 4
                        sci += 1
                        ps = B[SCB[k]]
                        S.wait("tensor", *sc_free[k])
                        mm = S.tensor.matmul(
                            ps[:, 0:512],
                            B.KTP[base:base + 64, ts_(t, 128)],
                            B.QTP[base:base + 64, ts_(c, 512)],
                            start=True, stop=True)
                        mp = S.tinc(mm, "pe", 1)
                        sl4 = ti % 4
                        S.wait("scalar", "pe", max(mp, pt_free[sl4]))
                        ie = S.scalar.activation(B.PT[:, ts_(sl4, 512)],
                                                 ps[:, 0:512], AF.Exp, bias=bias_b)
                        ma = S.tinc(ie, "act", 1)
                        ma_last = ma
                        sc_free[k] = ("act", ma)
                        dep_sem, dep_m = "act", ma
                        if cfg.get("causal") and t >= 4 * c:
                            rr = t - 4 * c
                            S.wait("vector", "act", ma)
                            imk = S.vector.tensor_mul(B.PT[:, ts_(sl4, 512)],
                                                      B.PT[:, ts_(sl4, 512)],
                                                      B.MASKS[:, ts_(rr, 512)])
                            md = S.tinc(imk, "dve", 1)
                            dep_sem, dep_m = "dve", md
                        info[ti] = (sl4, dep_sem, dep_m)
                    tj = ti - LEAD
                    if tj >= 0:
                        sl4, dep_sem, dep_m = info.pop(tj)
                        S.wait("tensor", dep_sem, dep_m)
                        mmv = S.tensor.matmul(
                            avps[0:65, 0:512],
                            v4[:, tj, hh, 0:65],
                            B.PT[:, ts_(sl4, 512)],
                            start=(tj == 0), stop=(tj == tmax - 1))
                        pt_free[sl4] = S.tinc(mmv, "pe", 1)
                mz = pt_free[(tmax - 1) % 4]
                S.wait("vector", "pe", mz)
                i1 = S.vector.tensor_scalar(B.IZ[0:1, 0:512], avps[64:65, 0:512],
                                            1e-30, None, op0=ALU.add)
                S.sf("vector", i1)
                i2 = S.vector.reciprocal(B.IZ[0:1, 0:512], B.IZ[0:1, 0:512])
                miz = S.tinc(i2, "dve", 1)
                S.wait("tensor", "dve", miz)
                kk = sci % 4
                sci += 1
                bps = B[SCB[kk]]
                S.wait("tensor", *sc_free[kk])
                _izname = getattr(getattr(B.IZ, "tensor", B.IZ), "name", "")
                onesf_row = (B.ONESF[64:65, 0:64]
                             if _izname == "ARNF" else B.ONESF[0:1, 0:64])
                mm = S.tensor.matmul(bps[0:64, 0:512], onesf_row,
                                     B.IZ[0:1, 0:512], start=True, stop=True)
                mp = S.tinc(mm, "pe", 1)
                S.wait("vector", "pe", mp)
                ic = S.vector.tensor_copy(B.IZB[0:64, 0:512], bps[0:64, 0:512])
                mic = S.sf("vector", ic)
                sc_free[kk] = ("dve", mic)
                it = S.vector.tensor_tensor(
                    B.OT[base:base + 64, ots(p * SEQ, c, 512)],
                    avps[0:64, 0:512], B.IZB[0:64, 0:512], op=ALU.mult)
                mo = S.tinc(it, "dve", 1)
                av_free[avk] = ("dve", mo)
                u_free = ("dve", mo)
        bb_free[p % 2] = ("act", ma_last)
        u_free = sc_free[2]
    m_ot = S.cnt["dve"]

    S.wait("tensor", "dve", m_ot)
    x_m = ("dve", 0)
    ta_pe = 0
    for c in range(4):
        for j in range(8):
            slo, mro = wso.load(S, W.wo_s[j, :, :], 8 * 128)
            S.wait("tensor", ENG_SEM[wso.ce], mro)
            wbo = wso.wbuf(slo)
            k, ps = (j % 4), B[SCB[j % 4]]
            S.wait("tensor", *sc_free[k])
            lastmm = None
            for kt in range(ND):
                lhsT = wbo[:, ts_(kt, 128)]
                rhs = B.OT[:, ots(kt * SEQ, c, 512)]
                lastmm = S.tensor.matmul(ps[:, 0:512], lhsT, rhs,
                                         start=(kt == 0), stop=(kt == ND - 1))
            mp = S.tinc(lastmm, "pe", 1)
            wso.release(S, slo, mp)
            S.wait("vector", "pe", mp)
            S.wait("vector", "pe", ta_pe)
            if cfg.get("o_bias"):
                ie = S.vector.tensor_scalar(B.OUTT[:, ts_(j, 512)], ps[:, 0:512],
                                            W.bo_col[:, j:j + 1], None, op0=ALU.add)
            else:
                ie = S.vector.tensor_copy(B.OUTT[:, ts_(j, 512)], ps[:, 0:512])
            ma = S.tinc(ie, "dve", 1)
            sc_free[k] = ("dve", ma)
        S.wait("tensor", "dve", S.cnt["dve"])
        _, xm = emit_out_transpose_add(S, B, B.OUTT, c)
        ta_pe = S.cnt["pe"]
        x_m = ("dve", xm)
    return x_m



def alloc_common(nc, stack, need):
    B = NS()

    def sb(name, shape, dt):
        B[name] = stack.enter_context(nc.sbuf_tensor(name, shape, dt))

    def psum(name, shape, dt):
        B[name] = stack.enter_context(nc.psum_tensor(name, shape, dt))

    sb("X", [128, NT * DM], F32)
    sb("YT", [128, ND * SEQ], BF16)
    sb("YROLL", [128, 2 * DM], BF16)
    sb("S1", [128, NT], F32)
    sb("S2", [128, NT], F32)
    sb("MU", [128, NT], F32)
    sb("MUSQ", [128, NT], F32)
    sb("VAR", [128, NT], F32)
    sb("SD", [128, NT], F32)
    sb("RSTD", [128, NT], F32)
    sb("LNSQ", [128, DM], F32)
    B["LNDISC"] = B["LNSQ"]
    sb("IDENT", [128, 128], BF16)

    if need & {"attn", "l1ffn", "l0ffn", "l3ffn"}:
        sb("OUTT", [128, 8 * 512], BF16)
    sb("ONES", [128, 128], BF16)
    sb("ONESF", [128, 128], F32)

    if "ffn" in need:
        sb("H", [128, 32 * 512], BF16)
        sb("w_stage0", [128, 2048], F32)
        sb("w_wbuf0", [128, 4096], BF16)
        sb("w_wbuf1", [128, 4096], BF16)

    if "l1ffn" in need:
        sb("RELU", [128, 2 * 512], BF16)
        sb("Y0", [128, 2 * 512], F32)
        sb("MEM", [128, 2 * 512], F32)
        sb("SPK", [128, 2 * 512], BF16)

    if "attn" in need:
        sb("QTP", [128, SEQ], BF16)
        sb("KTP", [128, SEQ], BF16)
        sb("VAUG", [128, NT * 130], BF16)
        if "l3attn" not in need:
            sb("OT", [128, ND * SEQ], BF16)
        sb("PT", [128, 4 * 512], BF16)
        sb("QSQ", [128, SEQ], BF16)
        sb("B4", [1, 16], F32)
        sb("IZ", [1, 512], F32)
        sb("IZB", [128, 512], F32)
        sb("BB", [128, 2], F32)
        sb("MASKS", [128, 4 * 512], BF16)
        sb("aw_stage0", [128, 3072], F32)
        sb("aw_wbuf0", [128, 3072], BF16)
        sb("aw_wbuf1", [128, 3072], BF16)

    if "l0ffn" in need:
        sb("LORA_T", [16, 1024], BF16)
        sb("ERALL", [1, 8 * 1024], BF16)
        sb("TR", [1, 4 * 512], BF16)
        sb("UR", [1, 4 * 512], BF16)
        sb("B1C", [128, 32], F32)
        sb("B2C", [128, 8], F32)
        sb("SCALE", [1, 1], F32)

    if "l3ffn" in need:
        sb("H3", [128, 22 * 512], BF16)
        sb("YF", [128, NT * DM], BF16) if ("l3attn" not in need and "yfull" not in need) else None
        B["HTK"] = B["YF"]
        sb("XN", [128, 1408], F32)
        sb("QT", [128, 1408], F32)
        sb("QS1", [128, 1408], BF16)
        sb("QS2", [128, 1408], BF16)
        sb("SGN", [128, 1408], BF16)
        sb("QSC", [128, 176], F32)
        sb("QRS", [128, 176], F32)
        sb("w_wbuf0", [128, 2816], BF16) if "ffn" not in need else None
        sb("w_wbuf1", [128, 2816], BF16) if "ffn" not in need else None

    if "l2moe" in need:
        sb("ACC", [128, 8 * SEQ], BF16)
        sb("WTOK", [128, NT * 8], F32)
        sb("WTOKB", [128, NT * 8], BF16)
        sb("WFM", [128, SEQ], BF16)
        sb("WBC", [128, 512], BF16)
        sb("SELE", [8, 1024], BF16)
        sb("WGC", [128, 64], F32)
        sb("WGCB", [128, 64], BF16)
        sb("BGR", [1, 8], F32)
        sb("MB1C", [128, 32], F32)
        sb("MB2C", [128, 8], F32)

    if "rope" in need:
        sb("RSH", [128, SEQ], BF16)
        sb("RCOS", [128, SEQ], BF16)
        sb("RSIN", [128, SEQ], BF16)

    if "l3attn" in need or "yfull" in need:
        sb("YF", [128, NT * DM], BF16)
    if "l3attn" in need:
        B["OT"] = B["YF"]
        sb("BQE", [128, 1], F32)
        sb("BKE", [128, 1], F32)
        sb("VBC", [128, 1], BF16)
        sb("BVRD", [1, 128], BF16)

    for i in range(6):
        psum(f"PSF{i}", [128, 512], F32)
    psum("PSB0", [128, 1024], BF16)
    psum("PSB1", [128, 1024], BF16)
    B["psf_pool"] = PsumBank([B[f"PSF{i}"] for i in range(6)])
    B["psb_pool"] = PsumBank([B["PSB0"], B["PSB1"]])
    return B



def prep_lhsT_stream(wT, chunk_cols):
    K, M = wT.shape
    KT = K // 128
    C = chunk_cols
    nch = M // C
    a = wT.reshape(KT, 128, M)
    out = np.empty((nch, 128, KT * C), dtype=np.float32)
    for j in range(nch):
        blk = a[:, :, j * C:(j + 1) * C]
        out[j] = np.ascontiguousarray(blk.transpose(1, 0, 2)).reshape(128, KT * C)
    return np.ascontiguousarray(out)



def build_l1ffn_test_program():
    import contextlib
    nc = bass.Bass("TRN2", target_bir_lowering=False)
    x = nc.dram_tensor("x", [SEQ, DM], F32, kind="ExternalInput").ap()
    wfc_s = nc.dram_tensor("wfc_s", [8, 128, 8 * 512], F32, kind="ExternalInput").ap()
    wpr_s = nc.dram_tensor("wpr_s", [8, 128, 32 * 128], F32, kind="ExternalInput").ap()
    ident = nc.dram_tensor("cc_ident", [128, 128], BF16, kind="ExternalInput").ap()
    xo = nc.dram_tensor("x_out", [SEQ, DM], F32, kind="ExternalOutput").ap()

    W = NS(wfc_s=wfc_s, wpr_s=wpr_s)

    with contextlib.ExitStack() as stack:
        B = alloc_common(nc, stack, {"ffn", "l1ffn"})
        sems = {n: stack.enter_context(nc.semaphore(n)) for n in SEM_NAMES}
        block = stack.enter_context(nc.Block())

        def schedule(S):
            iid = S.sync.dma_start(B.IDENT[:, :], ident[:, :])
            S.tinc(iid, "wdma", 16)
            xr = emit_load_x(S, B, x)
            x_m = emit_l1ffn(S, B, W, xr)
            emit_store_x(S, B, xo, x_m)

        def mk(engname):
            def f(eng):
                B["psf_pool"] = PsumBank([B[f"PSF{i}"] for i in range(4)])
                B["psb_pool"] = PsumBank([B["PSB0"], B["PSB1"]])
                schedule(Sched(nc, engname, eng, sems))
            return f

        block.sync(mk("sync"))
        block.tensor(mk("tensor"))
        block.vector(mk("vector"))
        block.scalar(mk("scalar"))
        block.gpsimd(mk("gpsimd"))
    return nc


def _finish_program(nc, B, sems, block, schedule):
    def mk(engname):
        def f(eng):
            B["psf_pool"] = PsumBank([B[f"PSF{i}"] for i in range(6)])
            B["psb_pool"] = PsumBank([B["PSB0"], B["PSB1"]])
            schedule(Sched(nc, engname, eng, sems))
        return f
    block.sync(mk("sync"))
    block.tensor(mk("tensor"))
    block.vector(mk("vector"))
    block.scalar(mk("scalar"))
    block.gpsimd(mk("gpsimd"))


def emit_load_consts(S, B, consts):
    for dr, sb in consts:
        cast = getattr(dr.tensor, "dtype", None) != getattr(sb.tensor, "dtype", None)
        if cast:
            ins = S.gpsimd.dma_start(sb, dr)
            S.tinc(ins, "gdma", 16)
        else:
            ins = S.sync.dma_start(sb, dr)
            S.tinc(ins, "wdma", 16)
    return ("wdma", S.cnt["wdma"]) if not any(
        getattr(d.tensor, "dtype", None) != getattr(b.tensor, "dtype", None)
        for d, b in consts) else ("gdma", S.cnt["gdma"])


def host_rope_tables():
    import ml_dtypes
    pos = np.arange(SEQ, dtype=np.float64)[None, :]
    p = np.arange(128)
    kk = (p % 64) // 2
    ang = pos / (10000.0 ** (2.0 * kk[:, None] / 64.0))
    cos = np.cos(ang)
    sin = np.sin(ang) * np.where(p % 2 == 0, -1.0, 1.0)[:, None]
    return {
        "cc_rcos": cos.astype(np.float32).astype(ml_dtypes.bfloat16),
        "cc_rsin": sin.astype(np.float32).astype(ml_dtypes.bfloat16),
    }


def build_attn_test_program(cfg):
    import contextlib
    nc = bass.Bass("TRN2", target_bir_lowering=False)
    x = nc.dram_tensor("x", [SEQ, DM], F32, kind="ExternalInput").ap()
    qkv_s = nc.dram_tensor("qkv_s", [8, 128, 3072], F32, kind="ExternalInput").ap()
    wo_s = nc.dram_tensor("wo_s", [8, 128, 1024], F32, kind="ExternalInput").ap()
    ident = nc.dram_tensor("cc_ident", [128, 128], BF16, kind="ExternalInput").ap()
    ones = nc.dram_tensor("cc_ones", [128, 128], BF16, kind="ExternalInput").ap()
    onesf = nc.dram_tensor("cc_onesf", [128, 128], F32, kind="ExternalInput").ap()
    masks = nc.dram_tensor("cc_masks", [128, 2048], BF16, kind="ExternalInput").ap()
    W = NS(qkv_s=qkv_s, wo_s=wo_s)
    need = {"attn"}
    if cfg.get("rope"):
        need.add("rope")
        W["rcos"] = nc.dram_tensor("cc_rcos", [128, SEQ], BF16, kind="ExternalInput").ap()
        W["rsin"] = nc.dram_tensor("cc_rsin", [128, SEQ], BF16, kind="ExternalInput").ap()
    if cfg.get("qk_bias"):
        W["bq_col"] = nc.dram_tensor("bq_col", [128, 8], F32, kind="ExternalInput").ap()
        W["bk_col"] = nc.dram_tensor("bk_col", [128, 8], F32, kind="ExternalInput").ap()
    if cfg.get("v_bias_row"):
        W["bv_row_t"] = nc.dram_tensor("bv_row", [1, 1024], F32, kind="ExternalInput").ap()
    if cfg.get("o_bias"):
        W["bo_col"] = nc.dram_tensor("bo_col", [128, 8], F32, kind="ExternalInput").ap()
    xo = nc.dram_tensor("x_out", [SEQ, DM], F32, kind="ExternalOutput").ap()

    with contextlib.ExitStack() as stack:
        B = alloc_common(nc, stack, need)
        if cfg.get("qk_bias"):
            B["BQC"] = stack.enter_context(nc.sbuf_tensor("BQC", [128, 8], F32))
            B["BKC"] = stack.enter_context(nc.sbuf_tensor("BKC", [128, 8], F32))
        if cfg.get("v_bias_row"):
            B["BVR"] = stack.enter_context(nc.sbuf_tensor("BVR", [1, 1024], F32))
        if cfg.get("o_bias"):
            B["BOC"] = stack.enter_context(nc.sbuf_tensor("BOC", [128, 8], F32))
        sems = {n: stack.enter_context(nc.semaphore(n)) for n in SEM_NAMES}
        block = stack.enter_context(nc.Block())

        def schedule(S):
            consts = [(ident[:, :], B.IDENT[:, :]), (ones[:, :], B.ONES[:, :]),
                      (onesf[:, :], B.ONESF[:, :]), (masks[:, :], B.MASKS[:, :])]
            Wd = NS(qkv_s=qkv_s, wo_s=wo_s)
            if cfg.get("qk_bias"):
                consts += [(W["bq_col"], B.BQC[:, :]), (W["bk_col"], B.BKC[:, :])]
                Wd["bq_col"] = B.BQC
                Wd["bk_col"] = B.BKC
            if cfg.get("v_bias_row"):
                consts += [(W["bv_row_t"], B.BVR[:, :])]
                Wd["bv_row"] = B.BVR
            if cfg.get("o_bias"):
                consts += [(W["bo_col"], B.BOC[:, :])]
                Wd["bo_col"] = B.BOC
            if cfg.get("rope"):
                consts += [(W["rcos"], B.RCOS[:, :]), (W["rsin"], B.RSIN[:, :])]
            cm = emit_load_consts(S, B, consts)
            xr = emit_load_x(S, B, x)
            S.wait("vector", *cm)
            S.wait("scalar", *cm)
            S.wait("tensor", *cm)
            x_m = emit_attn(S, B, Wd, xr, cfg)
            emit_store_x(S, B, xo, x_m)

        _finish_program(nc, B, sems, block, schedule)
    return nc


def host_prep_attn(wq_io, wk_io, wv_io, woT, bq=None, bk=None, bv=None, bo=None):
    qs = prep_lhsT_stream(np.ascontiguousarray(wq_io), 128)
    ks = prep_lhsT_stream(np.ascontiguousarray(wk_io), 128)
    vs = prep_lhsT_stream(np.ascontiguousarray(wv_io), 128)
    out = {
        "qkv_s": np.ascontiguousarray(np.concatenate([qs, ks, vs], axis=2)),
        "wo_s": prep_lhsT_stream(np.ascontiguousarray(woT), 128),
    }
    if bq is not None:
        out["bq_col"] = np.ascontiguousarray(bq.reshape(8, 128).T)
        out["bk_col"] = np.ascontiguousarray(bk.reshape(8, 128).T)
    if bv is not None:
        out["bv_row"] = np.ascontiguousarray(bv.reshape(1, 1024))
    if bo is not None:
        out["bo_col"] = np.ascontiguousarray(bo.reshape(8, 128).T)
    return out




def build_l3attn_test_program():
    import contextlib
    nc = bass.Bass("TRN2", target_bir_lowering=False)
    x = nc.dram_tensor("x", [SEQ, DM], F32, kind="ExternalInput").ap()
    qkv_s = nc.dram_tensor("qkv_s", [8, 128, 3072], F32, kind="ExternalInput").ap()
    wo_s = nc.dram_tensor("wo_s", [8, 128, 1024], F32, kind="ExternalInput").ap()
    bqkv_col = nc.dram_tensor("bqkv_col", [128, 24], F32, kind="ExternalInput").ap()
    bo_col = nc.dram_tensor("bo_col", [128, 8], F32, kind="ExternalInput").ap()
    lnw_col = nc.dram_tensor("lnw_col", [128, 8], F32, kind="ExternalInput").ap()
    lnb_col = nc.dram_tensor("lnb_col", [128, 8], F32, kind="ExternalInput").ap()
    ident = nc.dram_tensor("cc_ident", [128, 128], BF16, kind="ExternalInput").ap()
    ones = nc.dram_tensor("cc_ones", [128, 128], BF16, kind="ExternalInput").ap()
    onesf = nc.dram_tensor("cc_onesf", [128, 128], F32, kind="ExternalInput").ap()
    masks = nc.dram_tensor("cc_masks", [128, 2048], BF16, kind="ExternalInput").ap()
    xo = nc.dram_tensor("x_out", [SEQ, DM], F32, kind="ExternalOutput").ap()

    with contextlib.ExitStack() as stack:
        B = alloc_common(nc, stack, {"attn", "l3attn"})
        for nm, shape, dt in (("BQKVC", [128, 24], F32), ("BOC", [128, 8], F32),
                              ("LNWC", [128, 8], F32), ("LNBC", [128, 8], F32)):
            B[nm] = stack.enter_context(nc.sbuf_tensor(nm, shape, dt))
        sems = {n: stack.enter_context(nc.semaphore(n)) for n in SEM_NAMES}
        block = stack.enter_context(nc.Block())

        def schedule(S):
            consts = [(ident[:, :], B.IDENT[:, :]), (ones[:, :], B.ONES[:, :]),
                      (onesf[:, :], B.ONESF[:, :]), (masks[:, :], B.MASKS[:, :]),
                      (bqkv_col[:, :], B.BQKVC[:, :]), (bo_col[:, :], B.BOC[:, :]),
                      (lnw_col[:, :], B.LNWC[:, :]), (lnb_col[:, :], B.LNBC[:, :])]
            cm = emit_load_consts(S, B, consts)
            xr = emit_load_x(S, B, x)
            S.wait("vector", *cm)
            S.wait("scalar", *cm)
            S.wait("tensor", *cm)

            def lnw_cast(S2, wbuf, stage, off, w):
                last = None
                for kk in range(w // 128):
                    kt = (off // 128) + kk
                    last = S2.vector.tensor_scalar(
                        wbuf[:, ts_(kt, 128)], stage[:, ts_(kk, 128)],
                        B.LNWC[:, (kt % 8):(kt % 8) + 1], None, op0=ALU.mult)
                return last

            cfg = NS(causal=False, inner_residual=True, dev_qkv_bias=True,
                     v_bias_row="BVRD", o_bias=True, cast_fn=lnw_cast)
            Wd = NS(qkv_s=qkv_s, wo_s=wo_s, bqkv_col=B.BQKVC, bo_col=B.BOC)
            x_m = emit_attn(S, B, Wd, xr, cfg)
            emit_store_x(S, B, xo, x_m)

        _finish_program(nc, B, sems, block, schedule)
    return nc


def host_prep_l3attn(W_qkv, b_qkv, W_proj, b_proj, ln_w, ln_b):
    import ml_dtypes
    qs = prep_lhsT_stream(np.ascontiguousarray(W_qkv[:, 0:1024]), 128)
    ks = prep_lhsT_stream(np.ascontiguousarray(W_qkv[:, 1024:2048]), 128)
    vs = prep_lhsT_stream(np.ascontiguousarray(W_qkv[:, 2048:3072]), 128)
    bq = np.stack([b_qkv[0:1024].reshape(8, 128).T,
                   b_qkv[1024:2048].reshape(8, 128).T,
                   b_qkv[2048:3072].reshape(8, 128).T], axis=1)
    bqkv_col = np.ascontiguousarray(bq.transpose(0, 1, 2).reshape(128, 24))
    return {
        "qkv_s": np.ascontiguousarray(np.concatenate([qs, ks, vs], axis=2)),
        "wo_s": prep_lhsT_stream(np.ascontiguousarray(W_proj), 128),
        "bqkv_col": bqkv_col,
        "bo_col": np.ascontiguousarray(b_proj.reshape(8, 128).T),
        "lnw_col": np.ascontiguousarray(ln_w.reshape(8, 128).T),
        "lnb_col": np.ascontiguousarray(ln_b.reshape(8, 128).T),
    }





def emit_l0ffn(S, B, W, x_ready, cached=None):
    r = emit_ln(S, B, x_ready)
    yt_sem, yt_m = r.yt
    ws = WStream(B, "w", cast_eng="vector", single_stage=True)
    S.wait("tensor", yt_sem, yt_m)
    has_cache = W.get("w1b_s") is not None

    for kt in range(ND):
        kb, psb = B.psb_pool.next(S, waiter="tensor")
        itr = S.tensor.transpose(psb[0:16, 0:128],
                                 B.YT[:, kt * SEQ + 1: kt * SEQ + 17],
                                 B.IDENT[:, 0:128])
        mp = S.tinc(itr, "pe", 1)
        S.wait("vector", "pe", mp)
        ie = S.vector.tensor_copy(B.LORA_T[0:16, ts_(kt, 128)], psb[0:16, 0:128])
        me = S.tinc(ie, "dve", 1)
        B.psb_pool.done(kb, "dve", me)
    m_lt = S.cnt["dve"]
    S.wait("gpsimd", "dve", m_lt)
    base_g = S.cnt["gdma"]
    for sl in range(8):
        row = 2 * sl if sl < 4 else 8 + 2 * (sl - 4)
        ig = S.gpsimd.dma_start(B.ERALL[0:1, ts_(sl, 1024)],
                                B.LORA_T[row:row + 1, 0:1024])
        S.tinc(ig, "gdma", 16)
    m_er = ("gdma", S.cnt["gdma"])

    x_m = 0
    gelu_free = [0, 0]
    out_free = [0, 0]
    for c in range(4):
        uold = S.cnt["dve"]
        for i in range(4):
            kb, ps = B.psf_pool.next(S, waiter="tensor")
            lastmm = None
            for kt in range(ND):
                lastmm = S.tensor.matmul(
                    ps[0:1, 0:512],
                    B.YT[:, kt * SEQ + 2 + 2 * i: kt * SEQ + 3 + 2 * i],
                    B.YT[:, ots(kt * SEQ, c, 512)],
                    start=(kt == 0), stop=(kt == ND - 1))
            mp = S.tinc(lastmm, "pe", 1)
            S.wait("vector", "pe", mp)
            ie = S.vector.tensor_scalar(B.TR[0:1, ts_(i, 512)], ps[0:1, 0:512],
                                        B.SCALE[0:1, 0:1], None, op0=ALU.mult)
            me = S.tinc(ie, "dve", 1)
            B.psf_pool.done(kb, "dve", me)
        m_tr = S.cnt["dve"]
        S.wait("tensor", "dve", m_tr)
        S.wait("tensor", *m_er)

        for j in range(8):
            if has_cache:
                use_bf = cached[0] or c > 0
                sl = ws_load_cached(S, B, ws, W.w1_s[j, :, :],
                                    W.w1b_s[j, :, :], 8 * 512, use_bf)
            else:
                sl, mr = ws.load(S, W.w1_s[j, :, :], 8 * 512)
                S.wait("tensor", ENG_SEM[ws.ce], mr)
            wb = ws.wbuf(sl)
            lastmm_for_slot = None
            for sub in range(4):
                fblk = 4 * j + sub
                i = fblk // 8
                jj = fblk % 8
                kb, ps = B.psf_pool.next(S, waiter="tensor")
                lastmm = None
                for kt in range(ND):
                    lhsT = wb[:, kt * 512 + sub * 128: kt * 512 + (sub + 1) * 128]
                    rhs = B.YT[:, ots(kt * SEQ, c, 512)]
                    lastmm = S.tensor.matmul(ps[:, 0:512], lhsT, rhs,
                                             start=(kt == 0), stop=False)
                lastmm = S.tensor.matmul(
                    ps[:, 0:512],
                    B.ERALL[0:1, i * 1024 + jj * 128: i * 1024 + (jj + 1) * 128],
                    B.TR[0:1, ts_(i, 512)], start=False, stop=True)
                mp = S.tinc(lastmm, "pe", 1)
                lastmm_for_slot = mp
                S.wait("scalar", "pe", mp)
                S.wait("scalar", "dve", gelu_free[sub % 2])
                ig = S.scalar.activation(B.H[:, ts_(fblk, 512)], ps[:, 0:512],
                                         AF.Gelu, bias=W.b1_col[:, fblk:fblk + 1])
                ma = S.tinc(ig, "act", 1)
                B.psf_pool.done(kb, "act", ma)
            ws.release(S, sl, lastmm_for_slot)
        m_h = S.cnt["act"]

        S.wait("tensor", "act", m_h)
        for j in range(8):
            if j % 2 == 0:
                i = j // 2
                for rr in range(4):
                    kb, ps = B.psf_pool.next(S, waiter="tensor")
                    lastmm = None
                    for kt in range(ND):
                        lastmm = S.tensor.matmul(
                            ps[0:1, 0:512],
                            B.YT[:, kt * SEQ + 10 + 2 * i: kt * SEQ + 11 + 2 * i],
                            B.H[:, ts_(8 * rr + kt, 512)],
                            start=(kt == 0), stop=(kt == ND - 1))
                    mp = S.tinc(lastmm, "pe", 1)
                    S.wait("vector", "pe", mp)
                    ie = S.vector.tensor_scalar(B.UR[0:1, ts_(rr, 512)],
                                                ps[0:1, 0:512],
                                                B.SCALE[0:1, 0:1], None, op0=ALU.mult)
                    me = S.tinc(ie, "dve", 1)
                    B.psf_pool.done(kb, "dve", me)
                S.wait("tensor", "dve", S.cnt["dve"])
            if has_cache:
                use_bf = cached[0] or c > 0
                sl = ws_load_cached(S, B, ws, W.w2_s[j, :, :],
                                    W.w2b_s[j, :, :], 32 * 128, use_bf)
            else:
                sl, mr = ws.load(S, W.w2_s[j, :, :], 32 * 128)
                S.wait("tensor", ENG_SEM[ws.ce], mr)
            wb = ws.wbuf(sl)
            i = j // 2
            kb, ps = B.psf_pool.next(S, waiter="tensor")
            lastmm = None
            for kt in range(32):
                lastmm = S.tensor.matmul(ps[:, 0:512], wb[:, ts_(kt, 128)],
                                         B.H[:, ts_(kt, 512)],
                                         start=(kt == 0), stop=False)
            for rr in range(4):
                st = (4 + i) * 1024 + (j % 2) * 512 + rr
                lhsT = B.ERALL[0:1, st: st + 509: 4]
                lastmm = S.tensor.matmul(ps[:, 0:512], lhsT,
                                         B.UR[0:1, ts_(rr, 512)],
                                         start=False, stop=(rr == 3))
            mp = S.tinc(lastmm, "pe", 1)
            ws.release(S, sl, mp)
            S.wait("vector", "pe", mp)
            S.wait("vector", "dve", out_free[j % 2])
            ie = S.vector.tensor_scalar(B.OUTT[:, ts_(j, 512)], ps[:, 0:512],
                                        W.b2_col[:, j:j + 1], 0.5,
                                        op0=ALU.add, op1=ALU.mult)
            me = S.tinc(ie, "dve", 1)
            B.psf_pool.done(kb, "dve", me)
        S.wait("tensor", "dve", S.cnt["dve"])
        _, x_m = emit_out_transpose_add(S, B, B.OUTT, c)
    if cached is not None:
        cached[0] = True
    return ("dve", x_m)


def build_l0ffn_test_program():
    import contextlib
    nc = bass.Bass("TRN2", target_bir_lowering=False)
    x = nc.dram_tensor("x", [SEQ, DM], F32, kind="ExternalInput").ap()
    w1_s = nc.dram_tensor("w1_s", [8, 128, 8 * 512], F32, kind="ExternalInput").ap()
    w2_s = nc.dram_tensor("w2_s", [8, 128, 32 * 128], F32, kind="ExternalInput").ap()
    b1_col = nc.dram_tensor("b1_col", [128, 32], F32, kind="ExternalInput").ap()
    b2_col = nc.dram_tensor("b2_col", [128, 8], F32, kind="ExternalInput").ap()
    scale_in = nc.dram_tensor("scale_in", [1, 1], F32, kind="ExternalInput").ap()
    ident = nc.dram_tensor("cc_ident", [128, 128], BF16, kind="ExternalInput").ap()
    xo = nc.dram_tensor("x_out", [SEQ, DM], F32, kind="ExternalOutput").ap()

    with contextlib.ExitStack() as stack:
        B = alloc_common(nc, stack, {"ffn", "l0ffn"})
        sems = {n: stack.enter_context(nc.semaphore(n)) for n in SEM_NAMES}
        block = stack.enter_context(nc.Block())

        def schedule(S):
            consts = [(ident[:, :], B.IDENT[:, :]), (b1_col[:, :], B.B1C[:, :]),
                      (b2_col[:, :], B.B2C[:, :]), (scale_in[:, :], B.SCALE[0:1, 0:1])]
            cm = emit_load_consts(S, B, consts)
            xr = emit_load_x(S, B, x)
            S.wait("vector", *cm)
            S.wait("scalar", *cm)
            S.wait("tensor", *cm)
            Wd = NS(w1_s=w1_s, w2_s=w2_s, b1_col=B.B1C, b2_col=B.B2C)
            x_m = emit_l0ffn(S, B, Wd, xr)
            emit_store_x(S, B, xo, x_m)

        _finish_program(nc, B, sems, block, schedule)
    return nc


def host_prep_l0ffn(W1, b1, W2, b2, scale):
    return {
        "w1_s": prep_lhsT_stream(np.ascontiguousarray(W1.T), 512),
        "w2_s": prep_lhsT_stream(np.ascontiguousarray(W2.T), 128),
        "b1_col": np.ascontiguousarray(b1.reshape(32, 128).T),
        "b2_col": np.ascontiguousarray(b2.reshape(8, 128).T),
        "scale_in": np.asarray(scale, dtype=np.float32).reshape(1, 1),
    }





def emit_quant(S, B, src_ap, dst_ap, width):
    m = 0
    off = 0
    while off < width:
        w = min(1408, width - off)
        m = emit_quant_chain(S, B, src_ap[:, off:off + w], dst_ap[:, off:off + w], w)
        off += w
    return m


def emit_quant_chain(S, B, src_ap, dst_ap, width):
    nb = width // 16
    src3 = src_ap.rearrange("p (b g) -> p b g", g=16)
    i = S.vector.tensor_reduce(B.QSC[:, 0:nb], src3, axis=AX.X, op=ALU.max,
                               apply_absolute_value=True)
    S.sf("vector", i)
    i = S.vector.tensor_scalar(B.QSC[:, 0:nb], B.QSC[:, 0:nb], 1.0 / 6.0, 1e-12,
                               op0=ALU.mult, op1=ALU.max)
    S.sf("vector", i)
    i = S.vector.reciprocal(B.QRS[:, 0:nb], B.QSC[:, 0:nb])
    mrs = S.sf("vector", i)
    S.wait("scalar", "dve", mrs)
    isg = S.scalar.activation(B.SGN[:, 0:width], src_ap, AF.Sign)
    msg = S.tinc(isg, "act", 1)
    rsb = B.QRS[:, 0:nb].rearrange("p (b g) -> p b g", g=1).broadcast_to([128, nb, 16])
    i = S.vector.tensor_tensor(B.XN[:, 0:width].rearrange("p (b g) -> p b g", g=16),
                               src3, rsb, op=ALU.mult)
    mxn = S.tinc(i, "dve", 1)
    S.wait("vector", "act", msg)
    S.wait("scalar", "dve", mxn)
    ia = S.scalar.activation(B.XN[:, 0:width], B.XN[:, 0:width], AF.Abs)
    ma = S.tinc(ia, "act", 1)
    S.wait("vector", "act", ma)
    i = S.vector.tensor_scalar(B.QT[:, 0:width], B.XN[:, 0:width], 2.0, 4.0,
                               op0=ALU.max, op1=ALU.min)
    S.sf("vector", i)
    i = S.vector.tensor_scalar(B.QT[:, 0:width], B.QT[:, 0:width], MAGIC, None,
                               op0=ALU.add)
    S.sf("vector", i)
    i = S.vector.tensor_scalar(B.QS2[:, 0:width], B.QT[:, 0:width], MAGIC + 2.0,
                               None, op0=ALU.subtract)
    S.sf("vector", i)
    i = S.vector.tensor_scalar(B.QS1[:, 0:width], B.XN[:, 0:width], 5.0, 2.0,
                               op0=ALU.is_gt, op1=ALU.mult)
    S.sf("vector", i)
    i = S.vector.tensor_tensor(B.QS2[:, 0:width], B.QS2[:, 0:width],
                               B.QS1[:, 0:width], op=ALU.add)
    S.sf("vector", i)
    i = S.vector.tensor_scalar(B.QT[:, 0:width], B.XN[:, 0:width], 2.0, 2.0,
                               op0=ALU.min, op1=ALU.mult)
    S.sf("vector", i)
    i = S.vector.tensor_scalar(B.QT[:, 0:width], B.QT[:, 0:width], MAGIC, None,
                               op0=ALU.add)
    S.sf("vector", i)
    i = S.vector.tensor_scalar(B.QS1[:, 0:width], B.QT[:, 0:width], MAGIC, 0.5,
                               op0=ALU.subtract, op1=ALU.mult)
    S.sf("vector", i)
    i = S.vector.tensor_tensor(B.QS1[:, 0:width], B.QS1[:, 0:width],
                               B.QS2[:, 0:width], op=ALU.add)
    S.sf("vector", i)
    i = S.vector.tensor_tensor(B.QS1[:, 0:width], B.QS1[:, 0:width],
                               B.SGN[:, 0:width], op=ALU.mult)
    S.sf("vector", i)
    scb = B.QSC[:, 0:nb].rearrange("p (b g) -> p b g", g=1).broadcast_to([128, nb, 16])
    i = S.vector.tensor_tensor(dst_ap.rearrange("p (b g) -> p b g", g=16),
                               B.QS1[:, 0:width].rearrange("p (b g) -> p b g", g=16),
                               scb, op=ALU.mult)
    m = S.sf("vector", i)
    return m


def emit_l3_wprep(S, B, W):
    jobs = [(W.wg_nat, 22, 1024, W.g_s), (W.wu_nat, 22, 1024, W.u_s),
            (W.wd_nat, 8, 2816, W.d_s)]
    qt_free = ("dve", 0)
    tch_free = ("gdma", 0)
    for wsrc, ntile, tw, scratch in jobs:
        nkt = tw // 128
        for rt in range(ntile):
            S.wait("vector", *tch_free)
            off = 0
            while off < tw:
                w = min(1408, tw - off)
                S.wait("sync", *qt_free)
                idm = S.sync.dma_start(B.QT[:, 0:w],
                                       wsrc[ts_(rt, 128), off:off + w])
                mdma = S.tinc(idm, "wdma", 16)
                S.wait("vector", "wdma", mdma)
                S.wait("scalar", "wdma", mdma)
                mq = emit_quant_chain(S, B, B.QT[:, 0:w],
                                      B.H3[:, off:off + w], w)
                qt_free = ("dve", mq)
                off += w
            S.wait("tensor", "dve", S.cnt["dve"])
            npacks = (nkt + 3) // 4
            for pk in range(npacks):
                cnt = min(4, nkt - 4 * pk)
                kb, psb = B.psb_pool.next(S, waiter="tensor")
                lastmm = None
                for jj in range(cnt):
                    kt = 4 * pk + jj
                    lastmm = S.tensor.transpose(psb[:, ts_(jj, 128)],
                                                B.H3[:, ts_(kt, 128)],
                                                B.IDENT[:, 0:128])
                mp = S.tinc(lastmm, "pe", 1)
                S.wait("vector", "pe", mp)
                ie = S.vector.tensor_copy(
                    B.H3[:, 5632 + pk * 512: 5632 + pk * 512 + cnt * 128],
                    psb[:, 0:cnt * 128])
                me = S.tinc(ie, "dve", 1)
                B.psb_pool.done(kb, "dve", me)
            S.wait("gpsimd", "dve", S.cnt["dve"])
            ig = S.gpsimd.dma_start(scratch[rt, :, :], B.H3[:, 5632:5632 + tw])
            mg = S.tinc(ig, "gdma", 16)
            tch_free = ("gdma", mg)
    return ("gdma", S.cnt["gdma"])


def emit_l3ffn(S, B, W, x_ready, do_prep):
    prep_m = emit_l3_wprep(S, B, W) if do_prep else ("gdma", 0)
    if do_prep:
        S.wait("vector", "dve", S.cnt["dve"])
        S.wait("scalar", "dve", S.cnt["dve"])
    r = emit_ln(S, B, x_ready, full_y=True, yf=B.YF, make_yt=False)
    S.wait("vector", *r.yt)
    for t in range(NT):
        iy = S.vector.tensor_scalar(B.QT[:, 0:DM], B.X[:, ts_(t, DM)],
                                    B.MU[:, t:t + 1], B.RSTD[:, t:t + 1],
                                    op0=ALU.subtract, op1=ALU.mult)
        S.sf("vector", iy)
        emit_quant(S, B, B.QT[:, 0:DM], B.YF[:, ts_(t, DM)], DM)
        S.wait("tensor", "dve", S.cnt["dve"])
        emit_transpose_tile_to_feat(S, B, B.YF[:, ts_(t, DM)], B.YT, t)
    m_yt = S.cnt["dve"]
    ws = WStream(B, "w", cast_eng="vector")
    S.wait("sync", *prep_m)
    S.wait("tensor", "dve", m_yt)

    x_m = 0
    sg_free = [("dve", 0), ("dve", 0)]
    for c in range(4):
        for m in range(22):
            slg, mg = ws.load_bf16(S, W.g_s[m, :, :], 1024)
            S.wait("tensor", "wdma", mg)
            wbg = ws.wbuf(slg)
            ka, psA = B.psf_pool.next(S, waiter="tensor")
            lastmm = None
            for kt in range(ND):
                lastmm = S.tensor.matmul(psA[:, 0:512], wbg[:, ts_(kt, 128)],
                                         B.YT[:, ots(kt * SEQ, c, 512)],
                                         start=(kt == 0), stop=(kt == ND - 1))
            mpa = S.tinc(lastmm, "pe", 1)
            ws.release(S, slg, mpa)
            slu, mu_ = ws.load_bf16(S, W.u_s[m, :, :], 1024)
            S.wait("tensor", "wdma", mu_)
            wbu = ws.wbuf(slu)
            kbb, psB = B.psf_pool.next(S, waiter="tensor")
            lastmm = None
            for kt in range(ND):
                lastmm = S.tensor.matmul(psB[:, 0:512], wbu[:, ts_(kt, 128)],
                                         B.YT[:, ots(kt * SEQ, c, 512)],
                                         start=(kt == 0), stop=(kt == ND - 1))
            mpb = S.tinc(lastmm, "pe", 1)
            ws.release(S, slu, mpb)
            S.wait("scalar", "pe", mpa)
            S.wait("scalar", *sg_free[m % 2])
            isl = S.scalar.activation(B.QS1[:, ts_(m % 2, 512)], psA[:, 0:512], AF.Silu)
            msl = S.tinc(isl, "act", 1)
            B.psf_pool.done(ka, "act", msl)
            S.wait("vector", "act", msl)
            S.wait("vector", "pe", mpb)
            ih = S.vector.tensor_tensor(B.H3[:, ts_(m, 512)], B.QS1[:, ts_(m % 2, 512)],
                                        psB[:, 0:512], op=ALU.mult)
            mh = S.tinc(ih, "dve", 1)
            sg_free[m % 2] = ("dve", mh)
            B.psf_pool.done(kbb, "dve", mh)
        m_h3 = S.cnt["dve"]

        S.wait("tensor", "dve", m_h3)
        h3v = B.H3[:, 0:22 * 512].rearrange("p (m q) -> p m q", q=512)
        for tsub in range(4):
            for pk in range(6):
                cnt = min(4, 22 - 4 * pk)
                kb, psb = B.psb_pool.next(S, waiter="tensor")
                lastmm = None
                for jj in range(cnt):
                    m = 4 * pk + jj
                    lastmm = S.tensor.transpose(psb[:, ts_(jj, 128)],
                                                h3v[:, m, ts_(tsub, 128)],
                                                B.IDENT[:, 0:128])
                mp = S.tinc(lastmm, "pe", 1)
                S.wait("vector", "pe", mp)
                ie = S.vector.tensor_copy(
                    B.HTK[:, tsub * 2816 + pk * 512: tsub * 2816 + pk * 512 + cnt * 128],
                    psb[:, 0:cnt * 128])
                me = S.tinc(ie, "dve", 1)
                B.psb_pool.done(kb, "dve", me)
        for tsub in range(4):
            emit_quant(S, B, B.HTK[:, ts_(tsub, 2816)],
                       B.HTK[:, ts_(tsub, 2816)], 2816)
        m_hq = S.cnt["dve"]
        S.wait("tensor", "dve", m_hq)
        hkv = B.HTK[:, 0:4 * 2816].rearrange("p (t q) -> p t q", q=2816)
        for m in range(22):
            kb, psb = B.psb_pool.next(S, waiter="tensor")
            lastmm = None
            for tsub in range(4):
                lastmm = S.tensor.transpose(psb[:, ts_(tsub, 128)],
                                            hkv[:, tsub, ts_(m, 128)],
                                            B.IDENT[:, 0:128])
            mp = S.tinc(lastmm, "pe", 1)
            S.wait("vector", "pe", mp)
            ie = S.vector.tensor_copy(B.H3[:, ts_(m, 512)], psb[:, 0:512])
            me = S.tinc(ie, "dve", 1)
            B.psb_pool.done(kb, "dve", me)
        m_hqt = S.cnt["dve"]
        S.wait("tensor", "dve", m_hqt)

        for j in range(8):
            sld, md = ws.load_bf16(S, W.d_s[j, :, :], 2816)
            S.wait("tensor", "wdma", md)
            wbd = ws.wbuf(sld)
            kb, ps = B.psf_pool.next(S, waiter="tensor")
            lastmm = None
            for kt in range(22):
                lastmm = S.tensor.matmul(ps[:, 0:512], wbd[:, ts_(kt, 128)],
                                         B.H3[:, ts_(kt, 512)],
                                         start=(kt == 0), stop=(kt == 21))
            mp = S.tinc(lastmm, "pe", 1)
            ws.release(S, sld, mp)
            S.wait("vector", "pe", mp)
            ie = S.vector.tensor_scalar(B.OUTT[:, ts_(j, 512)], ps[:, 0:512],
                                        0.5, None, op0=ALU.mult)
            me = S.tinc(ie, "dve", 1)
            B.psf_pool.done(kb, "dve", me)
        S.wait("tensor", "dve", S.cnt["dve"])
        _, x_m = emit_out_transpose_add(S, B, B.OUTT, c)
    return ("dve", x_m)


def build_l3ffn_test_program():
    import contextlib
    nc = bass.Bass("TRN2", target_bir_lowering=False)
    x = nc.dram_tensor("x", [SEQ, DM], F32, kind="ExternalInput").ap()
    wg = nc.dram_tensor("wg_nat", [2816, 1024], F32, kind="ExternalInput").ap()
    wu = nc.dram_tensor("wu_nat", [2816, 1024], F32, kind="ExternalInput").ap()
    wd = nc.dram_tensor("wd_nat", [1024, 2816], F32, kind="ExternalInput").ap()
    g_s = nc.dram_tensor("g_scr", [22, 128, 1024], BF16).ap()
    u_s = nc.dram_tensor("u_scr", [22, 128, 1024], BF16).ap()
    d_s = nc.dram_tensor("d_scr", [8, 128, 2816], BF16).ap()
    ident = nc.dram_tensor("cc_ident", [128, 128], BF16, kind="ExternalInput").ap()
    xo = nc.dram_tensor("x_out", [SEQ, DM], F32, kind="ExternalOutput").ap()

    with contextlib.ExitStack() as stack:
        B = alloc_common(nc, stack, {"l3ffn"})
        sems = {n: stack.enter_context(nc.semaphore(n)) for n in SEM_NAMES}
        block = stack.enter_context(nc.Block())

        def schedule(S):
            cm = emit_load_consts(S, B, [(ident[:, :], B.IDENT[:, :])])
            xr = emit_load_x(S, B, x)
            S.wait("vector", *cm)
            S.wait("scalar", *cm)
            S.wait("tensor", *cm)
            Wd = NS(wg_nat=wg, wu_nat=wu, wd_nat=wd, g_s=g_s, u_s=u_s, d_s=d_s)
            x_m = emit_l3ffn(S, B, Wd, xr, do_prep=True)
            emit_store_x(S, B, xo, x_m)

        _finish_program(nc, B, sems, block, schedule)
    return nc





def emit_l2moe(S, B, W, x_ready, cached):
    r = emit_ln(S, B, x_ready)
    yt_sem, yt_m = r.yt
    ws = WStream(B, "w", cast_eng="vector", single_stage=True)
    S.wait("tensor", yt_sem, yt_m)

    icst = S.vector.tensor_copy(B.WGCB[:, 0:64], B.WGC[:, 0:64])
    mcst = S.tinc(icst, "dve", 1)
    S.wait("tensor", "dve", mcst)
    for t in range(NT):
        kb, ps = B.psf_pool.next(S, waiter="tensor")
        lastmm = None
        for kt in range(ND):
            lastmm = S.tensor.matmul(ps[:, 0:8],
                                     B.YT[:, ots(kt * SEQ, t, 128)],
                                     B.WGCB[:, ts_(kt, 8)],
                                     start=(kt == 0), stop=False)
        lastmm = S.tensor.matmul(ps[:, 0:8], B.ONESF[0:1, 0:128],
                                 B.BGR[0:1, 0:8], start=False, stop=True)
        mp = S.tinc(lastmm, "pe", 1)
        S.wait("vector", "pe", mp)
        i = S.vector.reduce_max(B.SD[:, 0:1], ps[:, 0:8], axis=AX.X, negate=True)
        S.sf("vector", i)
        S.wait("scalar", "dve", S.cnt["dve"])
        S.wait("scalar", "pe", mp)
        ie = S.scalar.activation(B.S1[:, 0:8], ps[:, 0:8], AF.Exp,
                                 bias=B.SD[:, 0:1])
        ma = S.tinc(ie, "act", 1)
        B.psf_pool.done(kb, "act", ma)
        S.wait("vector", "act", ma)
        i = S.vector.reduce_max(B.VAR[:, 0:1], B.S1[:, 0:8], axis=AX.X)
        S.sf("vector", i)
        i = S.vector.tensor_scalar(B.S2[:, 0:8], B.S1[:, 0:8], B.VAR[:, 0:1],
                                   None, op0=ALU.is_ge)
        S.sf("vector", i)
        i = S.vector.scalar_tensor_tensor(B.S2[:, 0:8], B.S2[:, 0:8], -2.0,
                                          B.S1[:, 0:8], op0=ALU.mult, op1=ALU.add)
        S.sf("vector", i)
        i = S.vector.reduce_max(B.MUSQ[:, 0:1], B.S2[:, 0:8], axis=AX.X)
        S.sf("vector", i)
        i = S.vector.tensor_scalar(B.S2[:, 0:8], B.S1[:, 0:8], B.MUSQ[:, 0:1],
                                   None, op0=ALU.is_ge)
        S.sf("vector", i)
        i = S.vector.tensor_tensor(B.S1[:, 0:8], B.S1[:, 0:8], B.S2[:, 0:8],
                                   op=ALU.mult)
        S.sf("vector", i)
        i = S.vector.tensor_tensor(B.VAR[:, 0:1], B.VAR[:, 0:1], B.MUSQ[:, 0:1],
                                   op=ALU.add)
        S.sf("vector", i)
        i = S.vector.reciprocal(B.VAR[:, 0:1], B.VAR[:, 0:1])
        S.sf("vector", i)
        i = S.vector.tensor_scalar(B.WTOK[:, ts_(t, 8)], B.S1[:, 0:8],
                                   B.VAR[:, 0:1], None, op0=ALU.mult)
        S.sf("vector", i)
    S.wait("tensor", "dve", S.cnt["dve"])
    for t in range(NT):
        kb, psb = B.psb_pool.next(S, waiter="tensor")
        S.wait("vector", "pe", 0)
        ic = S.vector.tensor_copy(B.WTOKB[:, ts_(t, 8)], B.WTOK[:, ts_(t, 8)])
        mc = S.sf("vector", ic)
        S.wait("tensor", "dve", mc)
        itr = S.tensor.transpose(psb[0:8, 0:128], B.WTOKB[:, ts_(t, 8)],
                                 B.IDENT[:, 0:128])
        mp = S.tinc(itr, "pe", 1)
        S.wait("vector", "pe", mp)
        ie = S.vector.tensor_copy(B.WFM[0:8, ts_(t, 128)], psb[0:8, 0:128])
        me = S.tinc(ie, "dve", 1)
        B.psb_pool.done(kb, "dve", me)
    m_wfm = S.cnt["dve"]

    iz = S.vector.memset(B.ACC[:, 0:8 * SEQ], 0.0)
    m_acc0 = S.tinc(iz, "dve", 1)

    x_m = 0
    for e in range(8):
        S.wait("gpsimd", "dve", m_wfm)
        ig = S.gpsimd.dma_start(B.MB1C[:, 0:32], W.b1_cols[e, :, :])
        S.tinc(ig, "gdma", 16)
        ig = S.gpsimd.dma_start(B.MB2C[:, 0:8], W.b2_cols[e, :, :])
        S.tinc(ig, "gdma", 16)
        m_bias = ("gdma", S.cnt["gdma"])
        S.wait("tensor", *m_bias)
        S.wait("vector", "dve", m_acc0)

        for c in range(4):
            w1_bf = cached[0] or c > 0
            w2_bf = cached[1] or c > 0
            kb, ps = B.psf_pool.next(S, waiter="tensor")
            mm = S.tensor.matmul(ps[:, 0:512], B.SELE[0:8, ts_(e, 128)],
                                 B.WFM[0:8, ts_(c, 512)], start=True, stop=True)
            mpw = S.tinc(mm, "pe", 1)
            S.wait("vector", "pe", mpw)
            ie = S.vector.tensor_copy(B.WBC[:, 0:512], ps[:, 0:512])
            mwbc = S.tinc(ie, "dve", 1)
            B.psf_pool.done(kb, "dve", mwbc)
            for j in range(8):
                if w1_bf:
                    sl, mr = ws.load_bf16(S, W.w1b_s[e, j, :, :], 4096)
                    S.wait("tensor", "wdma", mr)
                else:
                    sl, mr = ws.load(S, W.w1_s[e, j, :, :], 4096)
                    S.wait("tensor", ENG_SEM[ws.ce], mr)
                    S.wait("gpsimd", ENG_SEM[ws.ce], mr)
                    iw = S.gpsimd.dma_start(W.w1b_s[e, j, :, :], ws.wbuf(sl)[:, 0:4096])
                    mwb = S.tinc(iw, "gdma", 16)
                    ws.wbuf_extra[sl] = ("gdma", mwb)
                    ws.wb_last = mwb
                wb = ws.wbuf(sl)
                last_use = 0
                for sub in range(4):
                    fblk = 4 * j + sub
                    kb, ps = B.psf_pool.next(S, waiter="tensor")
                    lastmm = None
                    for kt in range(ND):
                        lhsT = wb[:, kt * 512 + sub * 128: kt * 512 + (sub + 1) * 128]
                        rhs = B.YT[:, ots(kt * SEQ, c, 512)]
                        lastmm = S.tensor.matmul(ps[:, 0:512], lhsT, rhs,
                                                 start=(kt == 0), stop=(kt == ND - 1))
                    mp = S.tinc(lastmm, "pe", 1)
                    last_use = mp
                    S.wait("scalar", "pe", mp)
                    igl = S.scalar.activation(B.H[:, ts_(fblk, 512)], ps[:, 0:512],
                                              AF.Gelu, bias=B.MB1C[:, fblk:fblk + 1])
                    ma = S.tinc(igl, "act", 1)
                    B.psf_pool.done(kb, "act", ma)
                ws.release(S, sl, last_use)
            m_h = S.cnt["act"]
            S.wait("tensor", "act", m_h)

            for j in range(8):
                if w2_bf:
                    sl, mr = ws.load_bf16(S, W.w2b_s[e, j, :, :], 4096)
                    S.wait("tensor", "wdma", mr)
                else:
                    sl, mr = ws.load(S, W.w2_s[e, j, :, :], 4096)
                    S.wait("tensor", ENG_SEM[ws.ce], mr)
                    S.wait("gpsimd", ENG_SEM[ws.ce], mr)
                    iw = S.gpsimd.dma_start(W.w2b_s[e, j, :, :], ws.wbuf(sl)[:, 0:4096])
                    mwb = S.tinc(iw, "gdma", 16)
                    ws.wbuf_extra[sl] = ("gdma", mwb)
                    ws.wb_last = mwb
                wb = ws.wbuf(sl)
                kb, ps = B.psf_pool.next(S, waiter="tensor")
                lastmm = None
                for kt in range(32):
                    lastmm = S.tensor.matmul(ps[:, 0:512], wb[:, ts_(kt, 128)],
                                             B.H[:, ts_(kt, 512)],
                                             start=(kt == 0), stop=(kt == 31))
                mp = S.tinc(lastmm, "pe", 1)
                ws.release(S, sl, mp)
                S.wait("vector", "pe", mp)
                i1 = S.vector.tensor_scalar(B.LNSQ[:, 0:512], ps[:, 0:512],
                                            B.MB2C[:, j:j + 1], None, op0=ALU.add)
                S.sf("vector", i1)
                i2 = S.vector.tensor_tensor(B.LNSQ[:, 0:512], B.LNSQ[:, 0:512],
                                            B.WBC[:, 0:512], op=ALU.mult)
                S.sf("vector", i2)
                accs = B.ACC[:, j * SEQ + c * 512: j * SEQ + (c + 1) * 512]
                i3 = S.vector.tensor_tensor(accs, accs, B.LNSQ[:, 0:512], op=ALU.add)
                me = S.sf("vector", i3)
                B.psf_pool.done(kb, "dve", me)
        if not cached[0]:
            cached[0] = True if e == 7 else cached[0]
        if not cached[1]:
            cached[1] = True if e == 7 else cached[1]
    m_comb = S.cnt["dve"]

    accv = B.ACC[:, 0:8 * SEQ].rearrange("p (d q) -> p d q", q=SEQ)
    S.wait("tensor", "dve", m_comb)
    for c in range(4):
        for tsub in range(4):
            t = 4 * c + tsub
            for pack in range(2):
                kb, psb = B.psb_pool.next(S, waiter="tensor")
                lastmm = None
                for jj in range(4):
                    d = pack * 4 + jj
                    lastmm = S.tensor.transpose(
                        psb[:, ts_(jj, 128)],
                        accv[:, d, ots(c * 512, tsub, 128)],
                        B.IDENT[:, 0:128])
                mp = S.tinc(lastmm, "pe", 1)
                S.wait("vector", "pe", mp)
                dst = B.X[:, t * DM + pack * 512: t * DM + (pack + 1) * 512]
                ev = S.vector.scalar_tensor_tensor(dst, psb[:, 0:512], 0.5, dst,
                                                   op0=ALU.mult, op1=ALU.add)
                me = S.tinc(ev, "dve", 1)
                B.psb_pool.done(kb, "dve", me)
                x_m = me
    return ("dve", x_m)


def build_l2moe_test_program():
    import contextlib
    nc = bass.Bass("TRN2", target_bir_lowering=False)
    x = nc.dram_tensor("x", [SEQ, DM], F32, kind="ExternalInput").ap()
    wg_cols = nc.dram_tensor("wg_cols", [128, 64], F32, kind="ExternalInput").ap()
    bg_row = nc.dram_tensor("bg_row", [1, 8], F32, kind="ExternalInput").ap()
    w1_s = nc.dram_tensor("w1_s", [8, 8, 128, 4096], F32, kind="ExternalInput").ap()
    w2_s = nc.dram_tensor("w2_s", [8, 8, 128, 4096], F32, kind="ExternalInput").ap()
    b1_cols = nc.dram_tensor("b1_cols", [8, 128, 32], F32, kind="ExternalInput").ap()
    b2_cols = nc.dram_tensor("b2_cols", [8, 128, 8], F32, kind="ExternalInput").ap()
    w1b_s = nc.dram_tensor("w1b_scr", [8, 8, 128, 4096], BF16).ap()
    w2b_s = nc.dram_tensor("w2b_scr", [8, 8, 128, 4096], BF16).ap()
    l0w1b = nc.dram_tensor("l0w1b_scr", [8, 128, 4096], BF16).ap()
    l0w2b = nc.dram_tensor("l0w2b_scr", [8, 128, 4096], BF16).ap()
    l1fcb = nc.dram_tensor("l1fcb_scr", [8, 128, 4096], BF16).ap()
    l1prb = nc.dram_tensor("l1prb_scr", [8, 128, 4096], BF16).ap()
    ident = nc.dram_tensor("cc_ident", [128, 128], BF16, kind="ExternalInput").ap()
    ones = nc.dram_tensor("cc_ones", [128, 128], BF16, kind="ExternalInput").ap()
    onesf = nc.dram_tensor("cc_onesf", [128, 128], F32, kind="ExternalInput").ap()
    sel8 = nc.dram_tensor("cc_sel8", [8, 1024], BF16, kind="ExternalInput").ap()
    xo = nc.dram_tensor("x_out", [SEQ, DM], F32, kind="ExternalOutput").ap()

    with contextlib.ExitStack() as stack:
        B = alloc_common(nc, stack, {"ffn", "l2moe"})
        sems = {n: stack.enter_context(nc.semaphore(n)) for n in SEM_NAMES}
        block = stack.enter_context(nc.Block())

        def schedule(S):
            cm = emit_load_consts(S, B, [
                (ident[:, :], B.IDENT[:, :]), (ones[:, :], B.ONES[:, :]),
                (onesf[:, :], B.ONESF[:, :]), (wg_cols[:, :], B.WGC[:, :]),
                (bg_row[:, :], B.BGR[0:1, 0:8]), (sel8[:, :], B.SELE[:, :])])
            xr = emit_load_x(S, B, x)
            S.wait("vector", *cm)
            S.wait("scalar", *cm)
            S.wait("tensor", *cm)
            Wd = NS(w1_s=w1_s, w2_s=w2_s, w1b_s=w1b_s, w2b_s=w2b_s,
                    b1_cols=b1_cols, b2_cols=b2_cols)
            x_m = emit_l2moe(S, B, Wd, xr, cached=[False, False])
            emit_store_x(S, B, xo, x_m)

        _finish_program(nc, B, sems, block, schedule)
    return nc


def host_prep_l2moe(Wg, bg, W1, b1, W2, b2):
    w1_s = np.stack([prep_lhsT_stream(np.ascontiguousarray(W1[e]), 512)
                     for e in range(8)])
    w2_s = np.stack([prep_lhsT_stream(np.ascontiguousarray(W2[e]), 128)
                     for e in range(8)])
    return {
        "wg_cols": np.ascontiguousarray(
            Wg.reshape(8, 128, 8).transpose(1, 0, 2).reshape(128, 64)),
        "bg_row": np.ascontiguousarray(bg.reshape(1, 8)),
        "w1_s": w1_s,
        "w2_s": w2_s,
        "b1_cols": np.ascontiguousarray(
            b1.reshape(8, 32, 128).transpose(0, 2, 1)),
        "b2_cols": np.ascontiguousarray(
            b2.reshape(8, 8, 128).transpose(0, 2, 1)),
    }


def host_prep_l1ffn(W_fc, W_proj):
    wfcT = np.ascontiguousarray(W_fc.T.astype(np.float32))
    wprT = np.ascontiguousarray(W_proj.T.astype(np.float32))
    return {
        "wfc_s": prep_lhsT_stream(wfcT, 512),
        "wpr_s": prep_lhsT_stream(wprT, 128),
    }


def host_consts():
    import ml_dtypes
    p = np.arange(128)[:, None]
    j = np.arange(512)[None, :]
    masks = np.concatenate(
        [(j >= p + 128 * r).astype(np.float32) for r in range(4)], axis=1)
    return {
        "cc_ident": np.eye(128, dtype=np.float32).astype(ml_dtypes.bfloat16),
        "cc_ones": np.ones((128, 128), dtype=np.float32).astype(ml_dtypes.bfloat16),
        "cc_onesf": np.ones((128, 128), dtype=np.float32),
        "cc_sel8": np.concatenate(
            [np.tile((np.arange(8) == e).astype(np.float32)[:, None], (1, 128))
             for e in range(8)], axis=1).astype(ml_dtypes.bfloat16),
        "cc_masks": masks.astype(ml_dtypes.bfloat16),
    }





NBF = 44672
NF = 4096


def _bind_phase(B, kind):
    bf = B.ARNBF
    f3 = B.ARNF
    o = [0]
    p = [0]

    def tb(cols, rows=128, r0=0):
        ap = bf[r0:r0 + rows, o[0]:o[0] + cols]
        o[0] += cols
        return ap

    def tf(cols, rows=128, r0=0):
        ap = f3[r0:r0 + rows, p[0]:p[0] + cols]
        p[0] += cols
        return ap

    if kind.startswith("attn"):
        B["QTP"] = tb(2048)
        B["KTP"] = tb(2048)
        B["VAUG"] = tb(2080)
        ot = tb(16384)
        B["OT"] = ot
        B["YF"] = ot
        B["PT"] = tb(2048)
        B["QSQ"] = tb(2048)
        B["aw_wbuf0"] = tb(3072)
        B["aw_wbuf1"] = tb(3072)
        B["MASKS"] = tb(2048)
        B["RCOS"] = tb(2048)
        B["RSIN"] = tb(2048)
        B["RSH"] = B["QSQ"]
        B["BVRB"] = tb(1024, rows=1)
        B["OUTT"] = tb(4096)
        B["aw_stage0"] = tf(2048)
        B["LNSQ"] = tf(1024)
        B["LNDISC"] = B["LNSQ"]
        izb = tf(512)
        B["IZB"] = izb
        B["IZ"] = B.ARNF[64:65, p[0] - 512:p[0]]
        B["BB"] = tf(2)
        B["B4"] = B.ARNF[0:1, p[0]:p[0] + 16]
        p[0] += 16
    elif kind == "l0ffn":
        B["H"] = tb(16384)
        B["w_wbuf0"] = tb(4096)
        B["w_wbuf1"] = tb(4096)
        B["ERALL"] = tb(8192, rows=1)
        B["TR"] = tb(2048, rows=1)
        B["UR"] = tb(2048, rows=1)
        B["LORA_T"] = tb(1024, rows=16)
        B["OUTT"] = tb(4096)
        B["w_stage0"] = tf(2048)
        B["LNSQ"] = tf(1024)
        B["LNDISC"] = B["LNSQ"]
    elif kind == "l1ffn":
        B["H"] = tb(16384)
        B["w_wbuf0"] = tb(4096)
        B["w_wbuf1"] = tb(4096)
        B["RELU"] = tb(1024)
        B["SPK"] = tb(512)
        B["OUTT"] = tb(4096)
        B["w_stage0"] = tf(2048)
        B["Y0"] = tf(512)
        B["MEM"] = tf(512)
        B["LNSQ"] = tf(1024)
        B["LNDISC"] = B["LNSQ"]
    elif kind == "l2moe":
        B["ACC"] = tb(16384)
        B["H"] = tb(16384)
        B["w_wbuf0"] = tb(4096)
        B["w_wbuf1"] = tb(4096)
        B["WFM"] = tb(2048)
        B["WTOK"] = tf(128)
        B["SELE"] = tb(1024, rows=8)
        B["WTOKB"] = tb(128)
        B["WBC"] = tb(512)
        B["w_stage0"] = tf(2048)
        B["LNSQ"] = tf(1024)
        B["LNDISC"] = B["LNSQ"]
    elif kind == "l3ffn":
        B["H3"] = tb(11264)
        yf = tb(16384)
        B["YF"] = yf
        B["HTK"] = yf
        B["w_wbuf0"] = tb(2816)
        B["w_wbuf1"] = tb(2816)
        B["QS1"] = tb(1408)
        B["QS2"] = tb(1408)
        B["SGN"] = tb(1408)
        B["OUTT"] = tb(4096)
        B["XN"] = tf(1408)
        B["QT"] = tf(1408)
        B["QSC"] = tf(176)
        B["QRS"] = tf(176)
        B["LNDISC"] = B["XN"]
    assert o[0] <= NBF, (kind, o[0])
    assert p[0] <= NF, (kind, p[0])


def emit_join(S):
    i = S.vector.memset(B_JOIN_DUMMY_V(S), 0.0)
    S.tinc(i, "dve", 1)
    i = S.scalar.memzero(B_JOIN_DUMMY_S(S))
    S.tinc(i, "act", 1)
    i = S.gpsimd.memset(B_JOIN_DUMMY_V(S), 0.0)
    S.tinc(i, "gp", 1)
    for w in ("sync", "tensor", "vector", "scalar", "gpsimd"):
        for sem in ("pe", "dve", "act", "gp", "wdma", "gdma"):
            S.wait(w, sem, S.cnt[sem])


def B_JOIN_DUMMY_V(S):
    return S._join_dummy


def B_JOIN_DUMMY_S(S):
    return S._join_dummy


def build_fused_program():
    import contextlib
    nc = bass.Bass("TRN2", target_bir_lowering=False)
    D = {}

    def inp(name, shape, dt=F32):
        D[name] = nc.dram_tensor(name, shape, dt, kind="ExternalInput").ap()
        return D[name]

    x = inp("x", [SEQ, DM])
    inp("cc_ident", [128, 128], BF16)
    inp("cc_ones", [128, 128], BF16)
    inp("cc_onesf", [128, 128], F32)
    inp("cc_masks", [128, 2048], BF16)
    inp("cc_rcos", [128, SEQ], BF16)
    inp("cc_rsin", [128, SEQ], BF16)
    inp("cc_sel8", [8, 1024], BF16)
    inp("l0f_w1s", [8, 128, 8 * 512])
    inp("l0f_w2s", [8, 128, 32 * 128])
    inp("l0f_b1c", [128, 32])
    inp("l0f_b2c", [128, 8])
    inp("l0f_scale", [1, 1])
    inp("a0_qkvs", [8, 128, 3072])
    inp("a0_wos", [8, 128, 1024])
    inp("l1f_wfcs", [8, 128, 8 * 512])
    inp("l1f_wprs", [8, 128, 32 * 128])
    inp("a1_qkvs", [8, 128, 3072])
    inp("a1_wos", [8, 128, 1024])
    inp("a1_bqc", [128, 8])
    inp("a1_bkc", [128, 8])
    inp("a1_bvr", [1, 1024])
    inp("a1_boc", [128, 8])
    inp("l2m_wgc", [128, 64])
    inp("l2m_bgr", [1, 8])
    inp("l2m_w1s", [8, 8, 128, 4096])
    inp("l2m_w2s", [8, 8, 128, 4096])
    inp("l2m_b1c", [8, 128, 32])
    inp("l2m_b2c", [8, 128, 8])
    inp("a2_qkvs", [8, 128, 3072])
    inp("a2_wos", [8, 128, 1024])
    inp("a3_qkvs", [8, 128, 3072])
    inp("a3_wos", [8, 128, 1024])
    inp("a3_bqkvc", [128, 24])
    inp("a3_boc", [128, 8])
    inp("a3_lnwc", [128, 8])
    inp("a3_lnbc", [128, 8])
    inp("l3f_wg", [2816, 1024])
    inp("l3f_wu", [2816, 1024])
    inp("l3f_wd", [1024, 2816])
    g_s = nc.dram_tensor("g_scr", [22, 128, 1024], BF16).ap()
    u_s = nc.dram_tensor("u_scr", [22, 128, 1024], BF16).ap()
    d_s = nc.dram_tensor("d_scr", [8, 128, 2816], BF16).ap()
    w1b_s = nc.dram_tensor("w1b_scr", [8, 8, 128, 4096], BF16).ap()
    w2b_s = nc.dram_tensor("w2b_scr", [8, 8, 128, 4096], BF16).ap()
    l0w1b = nc.dram_tensor("l0w1b_scr", [8, 128, 4096], BF16).ap()
    l0w2b = nc.dram_tensor("l0w2b_scr", [8, 128, 4096], BF16).ap()
    l1fcb = nc.dram_tensor("l1fcb_scr", [8, 128, 4096], BF16).ap()
    l1prb = nc.dram_tensor("l1prb_scr", [8, 128, 4096], BF16).ap()
    xo = nc.dram_tensor("x_out", [SEQ, DM], F32, kind="ExternalOutput").ap()

    with contextlib.ExitStack() as stack:
        B = NS()

        def sb(name, shape, dt):
            B[name] = stack.enter_context(nc.sbuf_tensor(name, shape, dt))

        def psum(name, shape, dt):
            B[name] = stack.enter_context(nc.psum_tensor(name, shape, dt))

        sb("X", [128, NT * DM], F32)
        sb("YT", [128, ND * SEQ], BF16)
        sb("YROLL", [128, 2 * DM], BF16)
        for nm in ("S1", "S2", "MU", "MUSQ", "VAR", "SD", "RSTD"):
            sb(nm, [128, NT], F32)
        sb("IDENT", [128, 128], BF16)
        sb("ONES", [128, 128], BF16)
        sb("ONESF", [128, 128], F32)
        sb("B1C", [128, 32], F32)
        sb("B2C", [128, 8], F32)
        sb("SCALE", [1, 1], F32)
        sb("BQC", [128, 8], F32)
        sb("BKC", [128, 8], F32)
        sb("BOC", [128, 8], F32)
        sb("BOC3", [128, 8], F32)
        sb("WGC", [128, 64], F32)
        sb("WGCB", [128, 64], BF16)
        sb("BGR", [1, 8], F32)
        sb("MB1C", [128, 32], F32)
        sb("MB2C", [128, 8], F32)
        sb("BQKVC", [128, 24], F32)
        sb("LNWC", [128, 8], F32)
        sb("LNBC", [128, 8], F32)
        sb("BQE", [128, 1], F32)
        sb("BKE", [128, 1], F32)
        sb("VBC", [128, 1], BF16)
        sb("BVRD", [1, 128], BF16)
        sb("ARNBF", [128, NBF], BF16)
        sb("ARNF", [128, NF], F32)
        sb("JD", [128, 8], F32)
        for i in range(6):
            psum(f"PSF{i}", [128, 512], F32)
        psum("PSB0", [128, 1024], BF16)
        psum("PSB1", [128, 1024], BF16)
        sems = {n: stack.enter_context(nc.semaphore(n)) for n in SEM_NAMES}
        block = stack.enter_context(nc.Block())

        def schedule(S):
            S._join_dummy = B.JD[0:1, 0:1]
            B["psf_pool"] = PsumBank([B[f"PSF{i}"] for i in range(6)])
            B["psb_pool"] = PsumBank([B["PSB0"], B["PSB1"]])
            consts = [
                (D["cc_ident"][:, :], B.IDENT[:, :]),
                (D["cc_ones"][:, :], B.ONES[:, :]),
                (D["cc_onesf"][:, :], B.ONESF[:, :]),
                (D["l0f_b1c"][:, :], B.B1C[:, :]),
                (D["l0f_b2c"][:, :], B.B2C[:, :]),
                (D["l0f_scale"][:, :], B.SCALE[0:1, 0:1]),
                (D["a1_bqc"][:, :], B.BQC[:, :]),
                (D["a1_bkc"][:, :], B.BKC[:, :]),
                (D["a1_boc"][:, :], B.BOC[:, :]),
                (D["l2m_wgc"][:, :], B.WGC[:, :]),
                (D["l2m_bgr"][:, :], B.BGR[0:1, 0:8]),
                (D["a3_bqkvc"][:, :], B.BQKVC[:, :]),
                (D["a3_boc"][:, :], B.BOC3[:, :]),
                (D["a3_lnwc"][:, :], B.LNWC[:, :]),
                (D["a3_lnbc"][:, :], B.LNBC[:, :]),
            ]
            cm = emit_load_consts(S, B, consts)
            xr = emit_load_x(S, B, x)
            for e in ("vector", "scalar", "tensor", "gpsimd"):
                S.wait(e, *cm)
                S.wait(e, *xr)

            def lnw_cast(S2, wbuf, stage, off, w):
                last = None
                for kk in range(w // 128):
                    kt = (off // 128) + kk
                    last = S2.vector.tensor_scalar(
                        wbuf[:, ts_(kt, 128)], stage[:, ts_(kk, 128)],
                        B.LNWC[:, (kt % 8):(kt % 8) + 1], None, op0=ALU.mult)
                return last

            x_m = xr
            moe_cached = [False, False]
            l3_prepped = [False]
            l0_cached = [False]
            l1_cached = [False]
            seq = FUSED_SEQ
            for kind in seq:
                _bind_phase(B, kind)
                pl = []
                if kind.startswith("attn"):
                    pl.append((D["cc_masks"][:, :], B.MASKS[:, :]))
                    if kind == "attn_l2":
                        pl.append((D["cc_rcos"][:, :], B.RCOS[:, :]))
                        pl.append((D["cc_rsin"][:, :], B.RSIN[:, :]))
                    if kind == "attn_l1":
                        pl.append((D["a1_bvr"][0:1, :], B.BVRB[0:1, :]))
                if kind == "l2moe":
                    pl.append((D["cc_sel8"][:, :], B.SELE[0:8, :]))
                if pl:
                    pm = emit_load_consts(S, B, pl)
                    for e in ("vector", "scalar", "tensor", "gpsimd"):
                        S.wait(e, *pm)
                if kind == "l0ffn":
                    Wd = NS(w1_s=D["l0f_w1s"], w2_s=D["l0f_w2s"],
                            b1_col=B.B1C, b2_col=B.B2C,
                            w1b_s=l0w1b, w2b_s=l0w2b)
                    x_m = emit_l0ffn(S, B, Wd, x_m, cached=l0_cached)
                elif kind == "l1ffn":
                    Wd = NS(wfc_s=D["l1f_wfcs"], wpr_s=D["l1f_wprs"],
                            wfcb_s=l1fcb, wprb_s=l1prb)
                    x_m = emit_l1ffn(S, B, Wd, x_m, cached=l1_cached)
                elif kind == "l2moe":
                    Wd = NS(w1_s=D["l2m_w1s"], w2_s=D["l2m_w2s"],
                            w1b_s=w1b_s, w2b_s=w2b_s,
                            b1_cols=D["l2m_b1c"], b2_cols=D["l2m_b2c"])
                    x_m = emit_l2moe(S, B, Wd, x_m, cached=moe_cached)
                elif kind == "l3ffn":
                    Wd = NS(wg_nat=D["l3f_wg"], wu_nat=D["l3f_wu"],
                            wd_nat=D["l3f_wd"], g_s=g_s, u_s=u_s, d_s=d_s)
                    x_m = emit_l3ffn(S, B, Wd, x_m, do_prep=not l3_prepped[0])
                    l3_prepped[0] = True
                elif kind == "attn_l0":
                    cfg = NS(causal=True)
                    Wd = NS(qkv_s=D["a0_qkvs"], wo_s=D["a0_wos"])
                    x_m = emit_attn(S, B, Wd, x_m, cfg)
                elif kind == "attn_l1":
                    cfg = NS(causal=True, qk_bias=True, v_bias_row="BVRB16",
                             o_bias=True)
                    Wd = NS(qkv_s=D["a1_qkvs"], wo_s=D["a1_wos"],
                            bq_col=B.BQC, bk_col=B.BKC, bo_col=B.BOC)
                    x_m = emit_attn(S, B, Wd, x_m, cfg)
                elif kind == "attn_l2":
                    cfg = NS(causal=True, rope=True,
                             rope_mask=[i + 1 if i % 2 == 0 else i - 1
                                        for i in range(32)])
                    Wd = NS(qkv_s=D["a2_qkvs"], wo_s=D["a2_wos"])
                    x_m = emit_attn(S, B, Wd, x_m, cfg)
                elif kind == "attn_l3":
                    cfg = NS(causal=False, inner_residual=True,
                             dev_qkv_bias=True, v_bias_row="BVRD",
                             o_bias=True, cast_fn=lnw_cast)
                    Wd = NS(qkv_s=D["a3_qkvs"], wo_s=D["a3_wos"],
                            bqkv_col=B.BQKVC, bo_col=B.BOC3)
                    x_m = emit_attn(S, B, Wd, x_m, cfg)
                emit_join(S)
            emit_store_x(S, B, xo, x_m)

        _finish_program(nc, B, sems, block, schedule)
    return nc


FUSED = True
FUSED_SEQ = ["l0ffn", "attn_l0", "l0ffn",
             "l1ffn", "attn_l1", "l1ffn",
             "l2moe", "attn_l2", "l2moe",
             "l3ffn", "attn_l3", "l3ffn"]

_PROG_CACHE = {}


def _get_prog(kind):
    if kind in _PROG_CACHE:
        return _PROG_CACHE[kind]
    if kind == "fused":
        nc = build_fused_program()
        _PROG_CACHE[kind] = nc
        return nc
    if kind == "l0ffn":
        nc = build_l0ffn_test_program()
    elif kind == "l1ffn":
        nc = build_l1ffn_test_program()
    elif kind == "l3ffn":
        nc = build_l3ffn_test_program()
    elif kind == "l2moe":
        nc = build_l2moe_test_program()
    elif kind == "attn_l0":
        nc = build_attn_test_program(NS(causal=True))
    elif kind == "attn_l1":
        nc = build_attn_test_program(NS(causal=True, qk_bias=True,
                                        v_bias_row="bv_row", o_bias=True))
    elif kind == "attn_l2":
        nc = build_attn_test_program(NS(
            causal=True, rope=True,
            rope_mask=[i + 1 if i % 2 == 0 else i - 1 for i in range(32)]))
    elif kind == "attn_l3":
        nc = build_l3attn_test_program()
    else:
        raise KeyError(kind)
    _PROG_CACHE[kind] = nc
    return nc


_CONST_KEYS = {
    "l0ffn": ("cc_ident",),
    "l1ffn": ("cc_ident",),
    "l3ffn": ("cc_ident",),
    "l2moe": ("cc_ident", "cc_ones", "cc_onesf", "cc_sel8"),
    "attn_l0": ("cc_ident", "cc_ones", "cc_onesf", "cc_masks"),
    "attn_l1": ("cc_ident", "cc_ones", "cc_onesf", "cc_masks"),
    "attn_l2": ("cc_ident", "cc_ones", "cc_onesf", "cc_masks",
                "cc_rcos", "cc_rsin"),
    "attn_l3": ("cc_ident", "cc_ones", "cc_onesf", "cc_masks"),
}


def kernel(**inp):
    consts = host_consts()
    consts.update(host_rope_tables())

    f32 = np.float32
    T = lambda a: np.ascontiguousarray(np.asarray(a, dtype=f32).T)
    A = lambda a: np.ascontiguousarray(np.asarray(a, dtype=f32))

    wmaps = {}
    wmaps["l0ffn"] = host_prep_l0ffn(A(inp["L0_ffn_W1"]), A(inp["L0_ffn_b1"]),
                                     A(inp["L0_ffn_W2"]), A(inp["L0_ffn_b2"]),
                                     np.asarray(inp["L0_ffn_scale"], dtype=f32))
    wmaps["attn_l0"] = host_prep_attn(T(inp["L0_attn_Wq"]), T(inp["L0_attn_Wk"]),
                                      T(inp["L0_attn_Wv"]), T(inp["L0_attn_Wo"]))
    wmaps["l1ffn"] = host_prep_l1ffn(A(inp["L1_ffn_W_fc"]), A(inp["L1_ffn_W_proj"]))
    wmaps["attn_l1"] = host_prep_attn(
        A(inp["L1_attn_wq"]), A(inp["L1_attn_wk"]), A(inp["L1_attn_wv"]),
        A(inp["L1_attn_wo"]), bq=A(inp["L1_attn_bq"]), bk=A(inp["L1_attn_bk"]),
        bv=A(inp["L1_attn_bv"]), bo=A(inp["L1_attn_bo"]))
    wmaps["l2moe"] = host_prep_l2moe(A(inp["L2_ffn_Wg"]), A(inp["L2_ffn_bg"]),
                                     A(inp["L2_ffn_W1"]), A(inp["L2_ffn_b1"]),
                                     A(inp["L2_ffn_W2"]), A(inp["L2_ffn_b2"]))
    wmaps["attn_l2"] = host_prep_attn(T(inp["L2_attn_wq"]), T(inp["L2_attn_wk"]),
                                      T(inp["L2_attn_wv"]), T(inp["L2_attn_wo"]))
    wmaps["attn_l3"] = host_prep_l3attn(A(inp["L3_attn_W_qkv"]),
                                        A(inp["L3_attn_b_qkv"]),
                                        A(inp["L3_attn_W_proj"]),
                                        A(inp["L3_attn_b_proj"]),
                                        A(inp["L3_attn_ln_w"]),
                                        A(inp["L3_attn_ln_b"]))
    wmaps["l3ffn"] = {"wg_nat": A(inp["L3_ffn_w_gate"]),
                      "wu_nat": A(inp["L3_ffn_w_up"]),
                      "wd_nat": A(inp["L3_ffn_w_down"])}

    x0 = np.asarray(inp["L0_attn_x"], dtype=f32)
    xs = [np.ascontiguousarray(x0[b]) for b in range(NCORES)]

    def run(kind):
        nc = _get_prog(kind)
        cst = {k: consts[k] for k in _CONST_KEYS[kind]}
        in_maps = [dict(x=xs[b], **wmaps[kind], **cst) for b in range(NCORES)]
        res = run_bass_kernel_spmd(nc, in_maps, core_ids=list(range(NCORES)))
        for b in range(NCORES):
            xs[b] = res.results[b]["x_out"]

    if FUSED:
        nc = _get_prog("fused")
        im = {}
        im.update({k: consts[k] for k in ("cc_ident", "cc_ones", "cc_onesf",
                                          "cc_masks", "cc_rcos", "cc_rsin",
                                          "cc_sel8")})
        m = wmaps["l0ffn"]
        im["l0f_w1s"] = m["w1_s"]; im["l0f_w2s"] = m["w2_s"]
        im["l0f_b1c"] = m["b1_col"]; im["l0f_b2c"] = m["b2_col"]
        im["l0f_scale"] = m["scale_in"]
        m = wmaps["attn_l0"]
        im["a0_qkvs"] = m["qkv_s"]; im["a0_wos"] = m["wo_s"]
        m = wmaps["l1ffn"]
        im["l1f_wfcs"] = m["wfc_s"]; im["l1f_wprs"] = m["wpr_s"]
        m = wmaps["attn_l1"]
        im["a1_qkvs"] = m["qkv_s"]; im["a1_wos"] = m["wo_s"]
        im["a1_bqc"] = m["bq_col"]; im["a1_bkc"] = m["bk_col"]
        im["a1_bvr"] = m["bv_row"]; im["a1_boc"] = m["bo_col"]
        m = wmaps["l2moe"]
        im["l2m_wgc"] = m["wg_cols"]; im["l2m_bgr"] = m["bg_row"]
        im["l2m_w1s"] = m["w1_s"]; im["l2m_w2s"] = m["w2_s"]
        im["l2m_b1c"] = m["b1_cols"]; im["l2m_b2c"] = m["b2_cols"]
        m = wmaps["attn_l2"]
        im["a2_qkvs"] = m["qkv_s"]; im["a2_wos"] = m["wo_s"]
        m = wmaps["attn_l3"]
        im["a3_qkvs"] = m["qkv_s"]; im["a3_wos"] = m["wo_s"]
        im["a3_bqkvc"] = m["bqkv_col"]; im["a3_boc"] = m["bo_col"]
        im["a3_lnwc"] = m["lnw_col"]; im["a3_lnbc"] = m["lnb_col"]
        m = wmaps["l3ffn"]
        im["l3f_wg"] = m["wg_nat"]; im["l3f_wu"] = m["wu_nat"]
        im["l3f_wd"] = m["wd_nat"]
        in_maps = [dict(x=xs[b], **im) for b in range(NCORES)]
        res = run_bass_kernel_spmd(nc, in_maps, core_ids=list(range(NCORES)))
        return np.stack([res.results[b]["x_out"] for b in range(NCORES)]
                        ).astype(np.float32)

    seq = ["l0ffn", "attn_l0", "l0ffn",
           "l1ffn", "attn_l1", "l1ffn",
           "l2moe", "attn_l2", "l2moe",
           "l3ffn", "attn_l3", "l3ffn"]
    for kind in seq:
        run(kind)
    return np.stack(xs).astype(np.float32)
